# version 02
# baseline (speedup 1.0000x reference)
import numpy as np
import ml_dtypes
from contextlib import ExitStack
import concourse.bass as bass
import concourse.mybir as mybir
from concourse.bass_utils import run_bass_kernel_spmd

F32 = mybir.dt.float32
BF16 = mybir.dt.bfloat16
I32 = mybir.dt.int32
AF = mybir.ActivationFunctionType
OP = mybir.AluOpType
AX = mybir.AxisListType
BF = ml_dtypes.bfloat16

NCORES = 8
T = 2048
D = 2048
DCH = D // 128
PAIRS = [[0, 1], [2, 3], [4, 5], [6, 7]]


def tile_lhsT(wT):
    K, O = wT.shape
    t = wT.reshape(K // 128, 128, O // 128, 128).transpose(2, 1, 0, 3)
    return np.ascontiguousarray(t.reshape(O // 128, 128, (K // 128) * 128))


def vecc(v):
    return np.ascontiguousarray(np.asarray(v, np.float32).reshape(-1, 128))



ENG2BLOCK = {"PE": "tensor", "ACT": "scalar", "DVE": "vector", "SP": "sync", "POOL": "gpsimd"}


class Step:
    __slots__ = ("eng", "fn", "waits", "sem", "inc", "tag")

    def __init__(self, eng, fn, waits, sem, inc, tag):
        self.eng, self.fn, self.waits, self.sem, self.inc, self.tag = eng, fn, waits, sem, inc, tag


class Chain:
    __slots__ = ("sem", "count", "total", "pending")

    def __init__(self, sem):
        self.sem = sem
        self.count = 0
        self.total = 0
        self.pending = {}


class PB:

    SEM_CAP = 30000

    def __init__(self, nc, es):
        self.nc = nc
        self.es = es
        self.steps = []
        self.cur_sem = {}
        self.cnt = {}
        self.chains = {}
        self.last_w = {}
        self.readers = {}
        self.readers_ch = {}
        self.hwm = {}
        self.n_sem = 0
        for e in ENG2BLOCK:
            self._new_sem(e)

    def _new_sem(self, eng):
        h = self.es.enter_context(self.nc.semaphore(f"s_{eng}_{self.n_sem}"))
        self.n_sem += 1
        self.cur_sem[eng] = h
        self.cnt[eng] = 0

    def chain(self, name):
        if name not in self.chains:
            h = self.es.enter_context(self.nc.semaphore(f"c_{name}_{self.n_sem}"))
            self.n_sem += 1
            self.chains[name] = Chain(h)
        return self.chains[name]

    def _resolve(self, tok, w):
        if tok is None:
            return
        if tok[0] == "c":
            _, s, v = tok
            if w.get(s, -1) < v:
                w[s] = v
        else:
            ch = tok[1]
            s, v = ch.sem, ch.total
            if w.get(s, -1) < v:
                w[s] = v

    def op(self, eng, fn, reads=(), writes=(), dma=False, coll=False, chain=None, tag=""):
        w = {}
        for r in reads:
            self._resolve(self.last_w.get(r), w)
        for wr in writes:
            self._resolve(self.last_w.get(wr), w)
            for s, v in self.readers.get(wr, {}).items():
                if w.get(s, -1) < v:
                    w[s] = v
            for rch in self.readers_ch.get(wr, ()):
                v = rch.total
                if w.get(rch.sem, -1) < v:
                    w[rch.sem] = v
        if dma or coll:
            assert chain is not None, f"dma step {tag} needs a chain"
            ch = self.chain(chain)
            for s, v in ch.pending.items():
                if w.get(s, -1) < v:
                    w[s] = v
            ch.pending = {}
            ch.count += 1
            inc = 1 if coll else 16
            ch.total += inc
            sem = ch.sem
            tok = ("d", ch)
            rd_tok = (ch.sem, ch.total)
            dma = True
        else:
            inc = 1
            if self.cnt[eng] + inc > self.SEM_CAP:
                self._new_sem(eng)
            sem = self.cur_sem[eng]
            self.cnt[eng] += inc
            tok = ("c", sem, self.cnt[eng])
            rd_tok = (sem, self.cnt[eng])
        waits = []
        for s, v in w.items():
            k = (eng, id(s))
            if self.hwm.get(k, 0) >= v:
                continue
            self.hwm[k] = v
            waits.append((s, v))
        for r in reads:
            if dma:
                self.readers_ch.setdefault(r, set()).add(ch)
            else:
                d = self.readers.setdefault(r, {})
                if d.get(rd_tok[0], -1) < rd_tok[1]:
                    d[rd_tok[0]] = rd_tok[1]
            lw = self.last_w.get(r)
            if lw is not None and lw[0] == "d":
                p = lw[1].pending
                if p.get(rd_tok[0], -1) < rd_tok[1]:
                    p[rd_tok[0]] = rd_tok[1]
        for wr in writes:
            self.last_w[wr] = tok
            self.readers[wr] = {}
            self.readers_ch[wr] = set()
        self.steps.append(Step(eng, fn, waits, sem, inc, tag))
        return tok

    def full_barrier(self, tag="bar"):
        waits = []
        for e2, s in self.cur_sem.items():
            if self.cnt[e2] > 0:
                waits.append((s, self.cnt[e2]))
        for ch in self.chains.values():
            if ch.total > 0:
                waits.append((ch.sem, ch.total))
        for eng in ENG2BLOCK:
            mywaits = []
            for s, v in waits:
                k = (eng, id(s))
                if self.hwm.get(k, 0) >= v:
                    continue
                self.hwm[k] = v
                mywaits.append((s, v))
            if self.cnt[eng] + 1 > self.SEM_CAP:
                self._new_sem(eng)
            sem = self.cur_sem[eng]
            self.cnt[eng] += 1
            self.steps.append(Step(eng, lambda e: e.nop(), mywaits, sem, 1, f"{tag}_{eng}"))

    def emit(self):
        with self.nc.Block() as block:
            for eng, bname in ENG2BLOCK.items():
                def mk(engname):
                    def body(e):
                        for st in self.steps:
                            if st.eng != engname:
                                continue
                            for s, v in st.waits:
                                e.wait_ge(s, v)
                            ins = st.fn(e)
                            assert ins is not None, f"step {st.tag} returned None"
                            ins.then_inc(st.sem, st.inc)
                    return body
                getattr(block, bname)(mk(eng))


class Ctx:
    def __init__(self, nc, es, pb):
        self.nc = nc
        self.es = es
        self.pb = pb
        self._cc = {}

    def sb(self, nm, sh, dt=F32):
        return self.es.enter_context(self.nc.sbuf_tensor(nm, sh, dt, allow_name_mangling=True))

    def ps(self, nm, sh, dt=F32):
        return self.es.enter_context(self.nc.psum_tensor(nm, sh, dt, allow_name_mangling=True))

    def cc(self, val, parts=128):
        key = ("cc", float(val), parts)
        if key not in self._cc:
            t = self.sb(f"cc_{len(self._cc)}", [parts, 1], F32)
            self.pb.op("DVE", lambda e, t=t, v=float(val): e.memset(t[:, :], v),
                       writes=[key], tag="cc")
            self._cc[key] = t
        return self._cc[key], key



class LNKit:

    def __init__(self, cx, banks4, tw=512):
        self.cx = cx
        self.tw = tw
        self.b = banks4
        self.stg = cx.sb("ln_stg", [128, 2 * tw], BF16)
        self.sqg = cx.sb("ln_sqg", [128, 2 * tw], BF16)
        self.rows = cx.sb("ln_rows", [1, 4 * tw], F32)
        self.onesb = cx.sb("ln_onesb", [128, 1], BF16)
        self.ones1f = cx.sb("ln_ones1f", [1, 128], F32)
        cx.pb.op("DVE", lambda e: e.memset(self.onesb[:, :], 1.0), writes=["ln_onesb"], tag="ln_ones")
        cx.pb.op("DVE", lambda e: e.memset(self.ones1f[:, :], 1.0), writes=["ln_ones1f"], tag="ln_ones1")

    def emit(self, ts, s_tile, s_keys, xres_new, xres_keyf, xbf, xbf_keys, eps=1e-5, tag="ln"):
        cx, pb, tw = self.cx, self.cx.pb, self.tw
        (bs, ks), (bq, kq), (bm, km), (br, kr) = self.b
        for c in range(DCH):
            par = c % 2
            sl = slice(par * tw, (par + 1) * tw)
            pb.op("ACT", lambda e, c=c, sl=sl: e.activation(
                self.stg[:, sl], s_tile[:, c * tw:(c + 1) * tw], AF.Copy),
                reads=[s_keys[c]], writes=[("ln_stg", par)], tag=f"{tag}_cp{c}")
            pb.op("PE", lambda e, c=c, sl=sl: e.matmul(
                bs[0:1, :], self.onesb[:, 0:1], self.stg[:, sl],
                start=(c == 0), stop=(c == DCH - 1)),
                reads=[("ln_stg", par), "ln_onesb"], writes=[ks], tag=f"{tag}_ms{c}")
            pb.op("ACT", lambda e, c=c, sl=sl: e.activation(
                self.sqg[:, sl], s_tile[:, c * tw:(c + 1) * tw], AF.Square),
                reads=[s_keys[c]], writes=[("ln_sqg", par)], tag=f"{tag}_sq{c}")
            pb.op("PE", lambda e, c=c, sl=sl: e.matmul(
                bq[0:1, :], self.onesb[:, 0:1], self.sqg[:, sl],
                start=(c == 0), stop=(c == DCH - 1)),
                reads=[("ln_sqg", par), "ln_onesb"], writes=[kq], tag=f"{tag}_mq{c}")
        r = self.rows
        pb.op("DVE", lambda e: e.tensor_scalar_mul(r[0:1, 0:tw], bs[0:1, :], 1.0 / D),
              reads=[ks], writes=["ln_r0"], tag=f"{tag}_mean")
        pb.op("DVE", lambda e: e.tensor_scalar_mul(r[0:1, tw:2 * tw], bq[0:1, :], 1.0 / D),
              reads=[kq], writes=["ln_r1"], tag=f"{tag}_msq")
        pb.op("DVE", lambda e: e.tensor_mul(r[0:1, 2 * tw:3 * tw], r[0:1, 0:tw], r[0:1, 0:tw]),
              reads=["ln_r0"], writes=["ln_r2"], tag=f"{tag}_m2")
        pb.op("DVE", lambda e: e.tensor_sub(r[0:1, tw:2 * tw], r[0:1, tw:2 * tw], r[0:1, 2 * tw:3 * tw]),
              reads=["ln_r1", "ln_r2"], writes=["ln_r1"], tag=f"{tag}_var")
        epst, epsk = cx.cc(eps, 1)
        pb.op("ACT", lambda e: e.activation(r[0:1, 2 * tw:3 * tw], r[0:1, tw:2 * tw], AF.Sqrt,
                                            bias=epst[0:1, 0:1]),
              reads=["ln_r1", epsk], writes=["ln_r2"], tag=f"{tag}_sd")
        pb.op("DVE", lambda e: e.reciprocal(r[0:1, 3 * tw:4 * tw], r[0:1, 2 * tw:3 * tw]),
              reads=["ln_r2"], writes=["ln_r3"], tag=f"{tag}_rstd")
        pb.op("PE", lambda e: e.matmul(bm[:, :], self.ones1f[0:1, :], r[0:1, 0:tw],
                                       start=True, stop=True),
              reads=["ln_r0", "ln_ones1f"], writes=[km], tag=f"{tag}_bcm")
        pb.op("PE", lambda e: e.matmul(br[:, :], self.ones1f[0:1, :], r[0:1, 3 * tw:4 * tw],
                                       start=True, stop=True),
              reads=["ln_r3", "ln_ones1f"], writes=[kr], tag=f"{tag}_bcr")
        s3 = s_tile[:, :].rearrange("p (c w) -> p c w", w=tw)
        mb = bm[:, :].unsqueeze(1).broadcast_to([128, DCH, tw])
        rb = br[:, :].unsqueeze(1).broadcast_to([128, DCH, tw])
        pb.op("DVE", lambda e: e.tensor_tensor(s3, s3, mb, OP.subtract),
              reads=list(s_keys) + [km], writes=list(s_keys), tag=f"{tag}_sub")
        pb.op("DVE", lambda e: e.tensor_tensor(s3, s3, rb, OP.mult),
              reads=list(s_keys) + [kr], writes=list(s_keys), tag=f"{tag}_mul")
        for c in range(DCH):
            pb.op("SP", lambda e, c=c: e.dma_start(
                out=xres_new[c * 128:(c + 1) * 128, ts * tw:(ts + 1) * tw],
                in_=s_tile[:, c * tw:(c + 1) * tw]),
                reads=list(s_keys), writes=[(xres_keyf, ts, c)], dma=True, chain="lnst", tag=f"{tag}_st{c}")
        if xbf is not None:
            xbf3 = xbf[:, :].rearrange("p (c t) -> p c t", t=T)[:, :, ts * tw:(ts + 1) * tw]
            pb.op("DVE", lambda e: e.tensor_copy(xbf3, s3),
                  reads=list(s_keys), writes=list(xbf_keys), tag=f"{tag}_bf")



F32EPS = 1.1920929e-07


def emit_ffn(cx, nm, xbf, XBF, gu_fn, wd_scr, hscr, cast_w, rms_eps, pre_ts, evac, post_ts,
             dps, lnrows=None):
    pb = cx.pb
    wbufA = [cx.sb(f"{nm}_wA{i}", [128, 16 * 128], BF16) for i in range(2)]
    hstg = cx.sb(f"{nm}_hstg", [128, T], BF16)
    sg = cx.sb(f"{nm}_sg", [128, 512], F32)
    if rms_eps is not None:
        hsq = cx.sb(f"{nm}_hsq", [128, 512], BF16)
        rstdbc = cx.sb(f"{nm}_rstdbc", [128, 4 * 512], F32)
        rows = cx.sb(f"{nm}_rows", [1, 2 * 512], F32)
        onesb = cx.sb(f"{nm}_onesb", [128, 1], BF16)
        ones1f = cx.sb(f"{nm}_ones1f", [1, 128], F32)
        pb.op("DVE", lambda e: e.memset(onesb[:, :], 1.0), writes=[(nm, "onesb")], tag=f"{nm}ones")
        pb.op("DVE", lambda e: e.memset(ones1f[:, :], 1.0), writes=[(nm, "ones1f")], tag=f"{nm}ones1")

    ldq = "POOL" if cast_w else "SP"
    for g in range(64):
        pb.op(ldq, lambda e, g=g: e.dma_start(out=wbufA[0][:, :], in_=gu_fn(g)),
              writes=[(nm, "wA", 0)], dma=True, chain=f"{nm}_wA0", tag=f"{nm}wg{g}")
        pb.op(ldq, lambda e, g=g: e.dma_start(out=wbufA[1][:, :], in_=gu_fn(g + 64)),
              writes=[(nm, "wA", 1)], dma=True, chain=f"{nm}_wA1", tag=f"{nm}wu{g}")
        for ts in range(4):
            pg, pgk = dps[ts % 2], ("psum", ts % 2)
            pu, puk = dps[2 + ts % 2], ("psum", 2 + ts % 2)

            def mmA(e, wb, pp, ts=ts):
                last = None
                for k in range(16):
                    last = e.matmul(pp[:, :], wb[:, k * 128:(k + 1) * 128],
                                    xbf[:, k * T + ts * 512: k * T + (ts + 1) * 512],
                                    start=(k == 0), stop=(k == 15))
                return last
            pb.op("PE", lambda e, pg=pg, ts=ts: mmA(e, wbufA[0], pg, ts),
                  reads=[(nm, "wA", 0)] + XBF, writes=[pgk], tag=f"{nm}gmm{g}_{ts}")
            pb.op("PE", lambda e, pu=pu, ts=ts: mmA(e, wbufA[1], pu, ts),
                  reads=[(nm, "wA", 1)] + XBF, writes=[puk], tag=f"{nm}umm{g}_{ts}")
            pb.op("ACT", lambda e, pg=pg: e.activation(sg[:, :], pg[:, :], AF.Silu),
                  reads=[pgk], writes=[(nm, "sg")], tag=f"{nm}silu{g}_{ts}")
            pb.op("DVE", lambda e, pu=pu, ts=ts: e.tensor_mul(
                hstg[:, ts * 512:(ts + 1) * 512], sg[:, :], pu[:, :]),
                reads=[(nm, "sg"), puk], writes=[(nm, "hstg", ts)], tag=f"{nm}hmul{g}_{ts}")
            if rms_eps is not None:
                pb.op("ACT", lambda e, ts=ts: e.activation(
                    hsq[:, :], hstg[:, ts * 512:(ts + 1) * 512], AF.Square),
                    reads=[(nm, "hstg", ts)], writes=[(nm, "hsq")], tag=f"{nm}hsq{g}_{ts}")
                pb.op("PE", lambda e, ts=ts, g=g: e.matmul(
                    dps[4 + ts][0:1, :], onesb[:, 0:1], hsq[:, :],
                    start=(g == 0), stop=(g == 63)),
                    reads=[(nm, "hsq"), (nm, "onesb")], writes=[("psum", 4 + ts)],
                    tag=f"{nm}hstat{g}_{ts}")
        pb.op("SP", lambda e, g=g: e.dma_start(out=hscr[g], in_=hstg[:, :]),
              reads=[(nm, "hstg", t_) for t_ in range(4)],
              writes=[(nm, "hscr", g)], dma=True, chain=f"{nm}_hst", tag=f"{nm}hst{g}")

    if rms_eps is not None:
        epst, epsk = cx.cc(rms_eps, 1)
        for ts in range(4):
            pb.op("DVE", lambda e, ts=ts: e.tensor_scalar_mul(
                rows[0:1, 0:512], dps[4 + ts][0:1, :], 1.0 / 8192.0),
                reads=[("psum", 4 + ts)], writes=[(nm, "rows0")], tag=f"{nm}rmean{ts}")
            pb.op("ACT", lambda e: e.activation(rows[0:1, 512:1024], rows[0:1, 0:512], AF.Sqrt,
                                                bias=epst[0:1, 0:1]),
                  reads=[(nm, "rows0"), epsk], writes=[(nm, "rows1")], tag=f"{nm}rsd{ts}")
            pb.op("DVE", lambda e: e.reciprocal(rows[0:1, 0:512], rows[0:1, 512:1024]),
                  reads=[(nm, "rows1")], writes=[(nm, "rows0")], tag=f"{nm}rrecip{ts}")
            pb.op("PE", lambda e, ts=ts: e.matmul(
                dps[4][:, :], ones1f[0:1, :], rows[0:1, 0:512],
                start=True, stop=True),
                reads=[(nm, "rows0"), (nm, "ones1f")], writes=[("psum", 4)], tag=f"{nm}rbc{ts}")
            pb.op("DVE", lambda e, ts=ts: e.tensor_copy(
                rstdbc[:, ts * 512:(ts + 1) * 512], dps[4][:, :]),
                reads=[("psum", 4)], writes=[(nm, "rstdbc")], tag=f"{nm}rbce{ts}")

    hbuf = cx.sb(f"{nm}_hbuf", [128, 512], BF16)
    hnbuf = cx.sb(f"{nm}_hnbuf", [128, 512], BF16) if rms_eps is not None else None
    wdb = cx.sb(f"{nm}_wdb", [128, 8 * 128], BF16)
    for ts in range(4):
        pre_ts(ts)
        for half in range(2):
            for i in range(64):
                pb.op("SP", lambda e, i=i, ts=ts: e.dma_start(
                    out=hbuf[:, :], in_=hscr[i, :, ts * 512:(ts + 1) * 512]),
                    reads=[(nm, "hscr", i)], writes=[(nm, "hbuf")], dma=True,
                    chain=f"{nm}_hld", tag=f"{nm}hld{ts}_{half}_{i}")
                rhs = hbuf
                rhs_key = (nm, "hbuf")
                if rms_eps is not None:
                    pb.op("DVE", lambda e, ts=ts: e.tensor_mul(
                        hnbuf[:, :], hbuf[:, :], rstdbc[:, ts * 512:(ts + 1) * 512]),
                        reads=[(nm, "hbuf"), (nm, "rstdbc")], writes=[(nm, "hnbuf")],
                        tag=f"{nm}hn{ts}_{half}_{i}")
                    rhs = hnbuf
                    rhs_key = (nm, "hnbuf")
                pb.op("SP", lambda e, i=i, half=half: e.dma_start(
                    out=wdb[:, :], in_=wd_scr[i, :, half * 8 * 128:(half + 1) * 8 * 128]),
                    writes=[(nm, "wdb")], dma=True, chain=f"{nm}_wdl",
                    tag=f"{nm}wdl{ts}_{half}_{i}")

                def mmdown(e, i=i, rhs=rhs):
                    last = None
                    for o in range(8):
                        last = e.matmul(dps[o][:, :], wdb[:, o * 128:(o + 1) * 128], rhs[:, :],
                                        start=(i == 0), stop=(i == 63))
                    return last
                pb.op("PE", mmdown, reads=[rhs_key, (nm, "wdb")],
                      writes=[("psum", o) for o in range(8)], tag=f"{nm}dmm{ts}_{half}_{i}")
            evac(ts, half)
        post_ts(ts)


def emit_postln_ffn(cx, pref, xres, xout, okey, gu_fn, wd_src_fn, wdbf, hscr,
                    dps6, ln, cast_w, rms_eps):
    pb, nc = cx.pb, cx.nc
    esF = ExitStack()
    with esF:
        saved_es = cx.es
        cx.es = esF
        dpsX = [esF.enter_context(nc.psum_tensor(f"{pref}_px{i}", [128, 512], F32, allow_name_mangling=True))
                for i in range(2)]
        dps = dps6 + dpsX
        xbf = cx.sb(f"{pref}_xbf", [128, DCH * T], BF16)
        s_tile = cx.sb(f"{pref}_s_tile", [128, DCH * 512], F32)
        xold = cx.sb(f"{pref}_xold", [128, DCH * 512], F32)

        XBF = [(pref, "xbf", c) for c in range(DCH)]
        for c in range(DCH):
            pb.op("POOL", lambda e, c=c: e.dma_start(
                out=xbf[:, c * T:(c + 1) * T], in_=xres[c * 128:(c + 1) * 128, :]),
                writes=[(pref, "xbf", c)], dma=True, chain=f"{pref}xld", tag=f"{pref}xld{c}")

        for oc in range(16):
            dst = wdbf.rearrange("i p (o j) -> o i p j", j=128)[oc]
            srcap = wd_src_fn(oc)
            pb.op("POOL", lambda e, dst=dst, srcap=srcap: e.dma_start(out=dst, in_=srcap),
                  writes=[(pref, "wdbf", oc)], dma=True, chain=f"{pref}wdc", tag=f"{pref}wdc{oc}")
        pb.op("SP", lambda e: e.nop(), reads=[(pref, "wdbf", oc) for oc in range(16)],
              tag=f"{pref}wdbf_barrier")

        def pre_ts(ts):
            for c in range(DCH):
                pb.op("SP", lambda e, c=c, ts=ts: e.dma_start(
                    out=xold[:, c * 512:(c + 1) * 512],
                    in_=xres[c * 128:(c + 1) * 128, ts * 512:(ts + 1) * 512]),
                    writes=[(pref, "xold", c)], dma=True, chain=f"{pref}xold",
                    tag=f"{pref}xold{ts}_{c}")

        def evac(ts, half):
            for o in range(8):
                c = half * 8 + o
                pb.op("DVE", lambda e, o=o, c=c: e.tensor_add(
                    s_tile[:, c * 512:(c + 1) * 512], dps[o][:, :], xold[:, c * 512:(c + 1) * 512]),
                    reads=[("psum", o), (pref, "xold", c)], writes=[(pref, "s", c)],
                    tag=f"{pref}sadd{ts}_{c}")

        def post_ts(ts):
            ln.emit(ts, s_tile, [(pref, "s", c) for c in range(DCH)], xout, okey, xbf, XBF,
                    tag=f"{pref}lnB_{ts}")

        emit_ffn(cx, pref, xbf, XBF, gu_fn, wdbf, hscr, cast_w, rms_eps, pre_ts, evac,
                 post_ts, dps)
        cx.es = saved_es


def _ffn_postln_standalone(nc, es, pref, cast_w, rms_eps):
    pb = PB(nc, es)
    cx = Ctx(nc, es, pb)
    wdt = F32 if cast_w else BF16
    xres = nc.dram_tensor(f"{pref}_xres", [D, T], F32, kind="ExternalInput").ap()
    gu_t = nc.dram_tensor(f"{pref}_gu", [128, 128, 16 * 128], wdt, kind="ExternalInput").ap()
    wd_t = nc.dram_tensor(f"{pref}_wd", [16, 128, 64 * 128], wdt, kind="ExternalInput").ap()
    xout = nc.dram_tensor(f"{pref}_xout", [D, T], F32, kind="ExternalOutput").ap()
    hscr = nc.dram_tensor(f"{pref}_h", [64, 128, T], BF16, kind="Internal").ap()
    wdbf = nc.dram_tensor(f"{pref}_wdbf", [64, 128, 16 * 128], BF16, kind="Internal").ap()
    dps6 = [cx.ps(f"dps{i}", [128, 512], F32) for i in range(6)]
    PSK = lambda i: ("psum", i)
    ln = LNKit(cx, [(dps6[2], PSK(2)), (dps6[3], PSK(3)), (dps6[4], PSK(4)), (dps6[5], PSK(5))])
    emit_postln_ffn(cx, pref, xres, xout, f"{pref}xo", lambda g: gu_t[g],
                    lambda oc: wd_t[oc].rearrange("p (i j) -> i p j", j=128),
                    wdbf, hscr, dps6, ln, cast_w, rms_eps)
    pb.op("POOL", lambda e: e.nop(),
          reads=[(f"{pref}xo", ts, c) for ts in range(4) for c in range(DCH)], tag="fin")
    pb.emit()
    return nc


def build_stage4(nc, es, debug=False):
    return _ffn_postln_standalone(nc, es, "s4", cast_w=True, rms_eps=None)


def build_stage2(nc, es):
    return _ffn_postln_standalone(nc, es, "s2", cast_w=False, rms_eps=F32EPS)



def declare_quant(nc, internal_out):
    A = {}
    A["q_gu"] = nc.dram_tensor("q_gu", [2048, 2048], F32, kind="ExternalInput").ap()
    A["q_wp"] = nc.dram_tensor("q_wp", [8192, 256], F32, kind="ExternalInput").ap()
    A["q_E"] = nc.dram_tensor("q_E", [128, 2], BF16, kind="ExternalInput").ap()
    A["q_ET"] = nc.dram_tensor("q_ET", [2, 128], F32, kind="ExternalInput").ap()
    ok = "Internal" if internal_out else "ExternalOutput"
    A["o_gu"] = nc.dram_tensor("q_gu_out", [2048, 2048], BF16, kind=ok).ap()
    A["o_wp"] = nc.dram_tensor("q_wp_out", [8192, 256], BF16, kind=ok).ap()
    if internal_out:
        A["gath_gu"] = nc.dram_tensor("q_gath_gu", [8 * 2048, 2048], BF16, kind="Internal").ap()
        A["gath_wp"] = nc.dram_tensor("q_gath_wp", [8 * 8192, 256], BF16, kind="Internal").ap()
    return A


def emit_quant(cx, A, dps, gather):
    pb, nc = cx.pb, cx.nc
    q_gu, q_wp, q_E, q_ET = A["q_gu"], A["q_wp"], A["q_E"], A["q_ET"]
    o_gu, o_wp = A["o_gu"], A["o_wp"]
    PSK = lambda i: ("psum", i)
    p_sum = dps[2][0:2, :]
    p_rsb = dps[3]
    p_scb = dps[4]

    wt = cx.sb("wt", [128, 512], F32)
    wbf = cx.sb("wbf", [128, 512], BF16)
    wabs = cx.sb("wabs", [128, 512], BF16)
    Eb = cx.sb("Eb", [128, 2], BF16)
    ETf = cx.sb("ETf", [2, 128], F32)
    srow = cx.sb("srow", [2, 512], F32)
    sbfrow = cx.sb("sbfrow", [2, 512], BF16)
    sfrow = cx.sb("sfrow", [2, 512], F32)
    rrow = cx.sb("rrow", [2, 512], F32)
    ratio = cx.sb("ratio", [128, 512], BF16)
    qi = cx.sb("qi", [128, 512], I32)
    qf = cx.sb("qf", [128, 512], F32)
    wq = cx.sb("wq", [128, 512], BF16)

    pb.op("SP", lambda e: e.dma_start(out=Eb[:, :], in_=q_E), writes=["Eb"], dma=True,
          chain="qconst", tag="ldE")
    pb.op("SP", lambda e: e.dma_start(out=ETf[:, :], in_=q_ET), writes=["ETf"], dma=True,
          chain="qconst", tag="ldET")

    def tile(src, dst, r0, No, tg):
        pb.op("SP", lambda e: e.dma_start(out=wt[:, 0:No], in_=src),
              writes=["wt"], dma=True, chain="qw", tag=f"qld{tg}")
        pb.op("DVE", lambda e: e.tensor_copy(wbf[:, 0:No], wt[:, 0:No]),
              reads=["wt"], writes=["wbf"], tag=f"qbf{tg}")
        pb.op("ACT", lambda e: e.activation(wabs[:, 0:No], wbf[:, 0:No], AF.Abs),
              reads=["wbf"], writes=["wabs"], tag=f"qabs{tg}")
        pb.op("PE", lambda e: e.matmul(p_sum[:, 0:No], Eb[:, :], wabs[:, 0:No],
                                       start=True, stop=True),
              reads=["wabs", "Eb"], writes=[("psum", 2)], tag=f"qsum{tg}")
        pb.op("DVE", lambda e: e.tensor_scalar(srow[:, 0:No], p_sum[:, 0:No],
                                               1.0 / 64.0, 1e-8, OP.mult, OP.max),
              reads=[("psum", 2)], writes=["srow"], tag=f"qscale{tg}")
        pb.op("DVE", lambda e: e.tensor_copy(sbfrow[:, 0:No], srow[:, 0:No]),
              reads=["srow"], writes=["sbfrow"], tag=f"qsbf{tg}")
        pb.op("DVE", lambda e: e.tensor_copy(sfrow[:, 0:No], sbfrow[:, 0:No]),
              reads=["sbfrow"], writes=["sfrow"], tag=f"qsf{tg}")
        pb.op("DVE", lambda e: e.reciprocal(rrow[:, 0:No], sfrow[:, 0:No]),
              reads=["sfrow"], writes=["rrow"], tag=f"qrr{tg}")
        pb.op("PE", lambda e: e.matmul(p_rsb[:, 0:No], ETf[:, :], rrow[:, 0:No],
                                       start=True, stop=True),
              reads=["rrow", "ETf"], writes=[("psum", 3)], tag=f"qrsb{tg}")
        pb.op("PE", lambda e: e.matmul(p_scb[:, 0:No], ETf[:, :], sfrow[:, 0:No],
                                       start=True, stop=True),
              reads=["sfrow", "ETf"], writes=[("psum", 4)], tag=f"qscb{tg}")
        pb.op("DVE", lambda e: e.tensor_mul(ratio[:, 0:No], wbf[:, 0:No], p_rsb[:, 0:No]),
              reads=["wbf", ("psum", 3)], writes=["ratio"], tag=f"qratio{tg}")
        pb.op("DVE", lambda e: e.tensor_copy(qi[:, 0:No], ratio[:, 0:No]),
              reads=["ratio"], writes=["qi"], tag=f"qi{tg}")
        pb.op("DVE", lambda e: e.tensor_copy(qf[:, 0:No], qi[:, 0:No]),
              reads=["qi"], writes=["qf"], tag=f"qf{tg}")
        pb.op("DVE", lambda e: e.tensor_scalar(qf[:, 0:No], qf[:, 0:No], -1.0, 1.0,
                                               OP.max, OP.min),
              reads=["qf"], writes=["qf"], tag=f"qclip{tg}")
        pb.op("DVE", lambda e: e.tensor_mul(wq[:, 0:No], qf[:, 0:No], p_scb[:, 0:No]),
              reads=["qf", ("psum", 4)], writes=["wq"], tag=f"qwq{tg}")
        pb.op("SP", lambda e: e.dma_start(out=dst, in_=wq[:, 0:No]),
              reads=["wq"], writes=[("qo", tg)], dma=True, chain="qo", tag=f"qst{tg}")
        return ("qo", tg)

    outs = []
    for dc in range(16):
        for j in range(4):
            outs.append(tile(q_gu[dc * 128:(dc + 1) * 128, j * 512:(j + 1) * 512],
                             o_gu[dc * 128:(dc + 1) * 128, j * 512:(j + 1) * 512],
                             dc, 512, f"gu{dc}_{j}"))
    for hc in range(64):
        outs.append(tile(q_wp[hc * 128:(hc + 1) * 128, :],
                         o_wp[hc * 128:(hc + 1) * 128, :],
                         hc, 256, f"wp{hc}"))
    if gather:
        pb.op("POOL", lambda e: e.collective_compute(
            "AllGather", OP.bypass, replica_groups=[list(range(8))],
            ins=[A["o_gu"].opt()], outs=[A["gath_gu"].opt()]),
            reads=outs, writes=["qgath_gu"], coll=True, chain="qag1", tag="qag1")
        pb.op("POOL", lambda e: e.collective_compute(
            "AllGather", OP.bypass, replica_groups=[list(range(8))],
            ins=[A["o_wp"].opt()], outs=[A["gath_wp"].opt()]),
            reads=outs, writes=["qgath_wp"], coll=True, chain="qag2", tag="qag2")
    return outs


def build_quant(nc, es):
    pb = PB(nc, es)
    cx = Ctx(nc, es, pb)
    A = declare_quant(nc, internal_out=False)
    dps = [cx.ps(f"dps{i}", [128, 512], F32) for i in range(6)]
    outs = emit_quant(cx, A, dps, gather=False)
    pb.op("POOL", lambda e: e.nop(), reads=outs, tag="fin")
    pb.emit()
    return nc



def declare_stage1(nc, xres=None):
    A = {}
    A["xres"] = xres if xres is not None else nc.dram_tensor("s1_xres", [D, T], F32, kind="ExternalInput").ap()
    for nm, sh, dt in [("wq", [16, 128, 2048], F32), ("wk", [16, 128, 2048], F32),
                       ("wv", [16, 128, 2048], F32), ("wg", [16, 128, 2048], F32),
                       ("wo", [16, 128, 2048], F32), ("bcols", [7 * 16, 128], F32),
                       ("gam", [1, 16], F32), ("DL", [128, 128], F32),
                       ("T1", [128, 128], F32), ("C01", [128, 128], F32),
                       ("REV", [128, 1], F32), ("J128", [128, 16], F32),
                       ("IDB", [128, 128], BF16), ("SEL", [128, 1], F32)]:
        A[nm] = nc.dram_tensor(f"s1_{nm}", sh, dt, kind="ExternalInput").ap()
    for nm, sh, dt in [("qs", [16, 128, T], BF16), ("ks", [16, 128, T], BF16),
                       ("gs", [16, 128, T], BF16), ("vts", [16, 128, T], BF16),
                       ("sls", [16, 128, 15 * 128], BF16), ("ccsi", [128, 16 * 128], BF16),
                       ("ccso", [256, 16 * 128], BF16), ("ccti", [1, 32], F32),
                       ("ccto", [1, 32], F32)]:
        A[nm] = nc.dram_tensor(f"s1_{nm}", sh, dt, kind="Internal").ap()
    return A


def emit_stage1(cx, A, dps, ln, xout, okey, pre_ops=None, post_proj=None):
    pb, nc = cx.pb, cx.nc
    xres = A["xres"]
    w_q, w_k, w_v, w_g, w_o = A["wq"], A["wk"], A["wv"], A["wg"], A["wo"]
    bcolsd, gamr = A["bcols"], A["gam"]
    DL, T1, C01, REV, J128, IDB, SEL = A["DL"], A["T1"], A["C01"], A["REV"], A["J128"], A["IDB"], A["SEL"]
    q_scr, k_scr, g_scr, vt_scr, sl_scr = A["qs"], A["ks"], A["gs"], A["vts"], A["sls"]
    cc_s_in, cc_s_out, cc_t_in, cc_t_out = A["ccsi"], A["ccso"], A["ccti"], A["ccto"]
    PSK = lambda i: ("psum", i)

    esS = ExitStack()
    esS.__enter__()
    ptbs = [esS.enter_context(nc.psum_tensor(f"s1ptb{i}", [128, 512], BF16, allow_name_mangling=True)) for i in range(2)]
    bcols = esS.enter_context(nc.sbuf_tensor("bcols", [128, 7 * 16], F32, allow_name_mangling=True))
    if pre_ops is not None:
        pre_ops()
    for v in range(7 * 16):
        pb.op("SP", lambda e, v=v: e.dma_start(out=bcols[:, v:v + 1], in_=bcolsd[v, :]),
              writes=[("bcols", v)], dma=True, chain="cols", tag=f"col{v}")
    BCOLS = [("bcols", v) for v in range(7 * 16)]

    def bcol(vec, oc):
        return bcols[:, vec * 16 + oc: vec * 16 + oc + 1]

    cx.cc(1e-5, 1)

    esP = ExitStack()
    with esP:
        def sbP(nm, sh, dt=F32):
            return esP.enter_context(nc.sbuf_tensor(nm, sh, dt, allow_name_mangling=True))
        xbf = sbP("xbf", [128, DCH * T], BF16)
        wpb = [sbP(f"wpb{i}", [128, 16 * 128], BF16) for i in range(2)]
        stgo = sbP("stgo", [128, T], BF16)
        vtstg = sbP("vtstg", [128, T], BF16)
        identP = sbP("identP", [128, 128], BF16)

        pb.op("SP", lambda e: e.dma_start(out=identP[:, :], in_=IDB),
              writes=["identP"], dma=True, chain="cst", tag="idb")
        XBF = [("xbf", c) for c in range(DCH)]
        for c in range(DCH):
            pb.op("POOL", lambda e, c=c: e.dma_start(
                out=xbf[:, c * T:(c + 1) * T], in_=xres[c * 128:(c + 1) * 128, :]),
                writes=[("xbf", c)], dma=True, chain="xld", tag=f"xld{c}")

        def proj(nm, w_ap, bvec, func, scr, do_vt):
            for oc in range(16):
                wb = wpb[oc % 2]
                pb.op("POOL", lambda e, oc=oc, wb=wb: e.dma_start(out=wb[:, :], in_=w_ap[oc]),
                      writes=[("wpb", oc % 2)], dma=True, chain=f"{nm}w", tag=f"{nm}w{oc}")
                for ts in range(4):
                    bank, bk = dps[ts % 2], PSK(ts % 2)

                    def mm(e, wb=wb, bank=bank, ts=ts):
                        last = None
                        for k in range(16):
                            last = e.matmul(bank[:, :], wb[:, k * 128:(k + 1) * 128],
                                            xbf[:, k * T + ts * 512: k * T + (ts + 1) * 512],
                                            start=(k == 0), stop=(k == 15))
                        return last
                    pb.op("PE", mm, reads=[("wpb", oc % 2)] + XBF, writes=[bk], tag=f"{nm}mm{oc}_{ts}")
                    pb.op("ACT", lambda e, bank=bank, ts=ts, oc=oc: e.activation(
                        stgo[:, ts * 512:(ts + 1) * 512], bank[:, :], func,
                        bias=bcol(bvec, oc)),
                        reads=[bk] + BCOLS, writes=[("stgo", ts)], tag=f"{nm}ev{oc}_{ts}")
                if not do_vt:
                    pb.op("SP", lambda e, oc=oc, scr=scr: e.dma_start(out=scr[oc], in_=stgo[:, :]),
                          reads=[("stgo", t_) for t_ in range(4)],
                          writes=[(f"{nm}s", oc)], dma=True, chain=f"{nm}st", tag=f"{nm}st{oc}")
                else:
                    for tc in range(16):
                        pt = ptbs[tc % 2][:, 0:128]
                        pb.op("PE", lambda e, tc=tc, pt=pt: e.transpose(
                            pt, stgo[:, tc * 128:(tc + 1) * 128], identP[:, :]),
                            reads=[("stgo", tc // 4), "identP"], writes=[("ptb", tc % 2)],
                            tag=f"{nm}tr{oc}_{tc}")
                        pb.op("DVE", lambda e, tc=tc, pt=pt: e.tensor_copy(
                            vtstg[:, tc * 128:(tc + 1) * 128], pt),
                            reads=[("ptb", tc % 2)], writes=[("vtstg", tc)], tag=f"{nm}te{oc}_{tc}")
                    pb.op("SP", lambda e, oc=oc, scr=scr: e.dma_start(out=scr[oc], in_=vtstg[:, :]),
                          reads=[("vtstg", t_) for t_ in range(16)],
                          writes=[(f"{nm}s", oc)], dma=True, chain=f"{nm}st", tag=f"{nm}st{oc}")

        proj("q", w_q, 0, AF.Identity, q_scr, False)
        proj("k", w_k, 1, AF.Identity, k_scr, False)
        proj("g", w_g, 2, AF.Silu, g_scr, False)
        proj("v", w_v, 3, AF.Identity, vt_scr, True)
        pb.full_barrier("endP")
    if post_proj is not None:
        post_proj()

    esA = ExitStack()
    with esA:
        def sbA(nm, sh, dt=F32):
            return esA.enter_context(nc.sbuf_tensor(nm, sh, dt, allow_name_mangling=True))
        ret = sbA("ret", [128, 16 * T], BF16)
        khb = sbA("khb", [128, T], BF16)
        qhb = sbA("qhb", [128, T], BF16)
        vthb = sbA("vthb", [128, T], BF16)
        slocb = sbA("slocb", [128, 15 * 128], BF16)
        ghb = sbA("ghb", [128, T], BF16)
        sinb = sbA("sinb", [128, 16 * 128], BF16)
        Dmask = sbA("Dmask", [128, 16 * 128], BF16)
        Gq = sbA("Gq", [128, 16 * 128], BF16)
        gpow = sbA("gpow", [128, 16 * 16], F32)
        wdecs = sbA("wdecs", [128, 16], F32)
        lnbc = sbA("lnbc", [128, 16], F32)
        lnrow = sbA("lnrow", [1, 16], F32)
        gamrs = sbA("gamrs", [1, 16], F32)
        S = sbA("S", [128, 16 * 128], F32)
        sfin = sbA("sfin", [128, 16 * 128], BF16)
        sstg = sbA("sstg", [128, 15 * 128], BF16)
        ktd = sbA("ktd", [128, 128], BF16)
        msk = sbA("msk", [128, 128], BF16)
        qg = sbA("qg", [128, 128], BF16)
        qg2 = sbA("qg2", [128, 128], BF16)
        statc = sbA("statc", [128, 32], F32)
        dead = sbA("dead", [128, T], BF16)
        onesf = sbA("onesf", [128, 1], F32)
        tmpc = sbA("tmpc", [128, 128], F32)
        tmpm = sbA("tmpm", [128, 128], F32)
        rws = sbA("rws", [1, 64], F32)
        acol = sbA("acol", [128, 16], F32)
        bcolt = sbA("bcolt", [128, 16], F32)
        identA = sbA("identA", [128, 128], BF16)
        cD = sbA("cD", [128, 128], F32)
        cT1 = sbA("cT1", [128, 128], F32)
        cC01 = sbA("cC01", [128, 128], F32)
        cREV = sbA("cREV", [128, 1], F32)
        cJ = sbA("cJ", [128, 16], F32)
        cSEL = sbA("cSEL", [128, 1], F32)
        s_tile = sbA("s_tile", [128, DCH * 512], F32)
        xold = sbA("xold", [128, DCH * 512], F32)
        wob = [sbA(f"wob{i}", [128, 16 * 128], BF16) for i in range(2)]

        for t_, ap_ in [(identA, IDB), (cD, DL), (cT1, T1), (cC01, C01), (cREV, REV),
                        (cJ, J128), (cSEL, SEL), (gamrs, gamr)]:
            pb.op("SP", lambda e, t_=t_, ap_=ap_: e.dma_start(out=t_[:, :], in_=ap_),
                  writes=[("cst2", id(t_))], dma=True, chain="cst2", tag="cst2")
        CST = [("cst2", id(t_)) for t_ in (identA, cD, cT1, cC01, cREV, cJ, cSEL, gamrs)]
        pb.op("DVE", lambda e: e.memset(onesf[:, :], 1.0), writes=["onesf"], tag="onesf")
        pb.op("DVE", lambda e: e.memset(S[:, :], 0.0), writes=["S"], tag="S0")

        pb.op("ACT", lambda e: e.activation(lnrow[0:1, :], gamrs[0:1, :], AF.Ln),
              reads=CST, writes=["lnrow"], tag="lngam")
        pb.op("PE", lambda e: e.matmul(dps[4][:, 0:16], ln.ones1f[0:1, :], lnrow[0:1, :],
                                       start=True, stop=True),
              reads=["lnrow", "ln_ones1f"], writes=[PSK(4)], tag="lnbcm")
        pb.op("DVE", lambda e: e.tensor_copy(lnbc[:, :], dps[4][:, 0:16]),
              reads=[PSK(4)], writes=["lnbc"], tag="lnbce")
        for h in range(16):
            lc = lnbc[:, h:h + 1]
            pb.op("DVE", lambda e, h=h, lc=lc: e.tensor_scalar_mul(tmpm[:, :], cD[:, :], lc),
                  reads=["lnbc"] + CST, writes=["tmpm"], tag=f"dm1_{h}")
            pb.op("ACT", lambda e: e.activation(tmpc[:, :], tmpm[:, :], AF.Exp),
                  reads=["tmpm"], writes=["tmpc"], tag=f"dm2_{h}")
            pb.op("DVE", lambda e, h=h: e.tensor_mul(Dmask[:, h * 128:(h + 1) * 128],
                                                     tmpc[:, :], cC01[:, :]),
                  reads=["tmpc"] + CST, writes=[("Dmask", h)], tag=f"dm3_{h}")
            pb.op("DVE", lambda e, h=h, lc=lc: e.tensor_scalar_mul(tmpm[:, :], cT1[:, :], lc),
                  reads=["lnbc"] + CST, writes=["tmpm"], tag=f"gq1_{h}")
            pb.op("ACT", lambda e, h=h: e.activation(Gq[:, h * 128:(h + 1) * 128],
                                                     tmpm[:, :], AF.Exp),
                  reads=["tmpm"], writes=[("Gq", h)], tag=f"gq2_{h}")
            pb.op("DVE", lambda e, h=h, lc=lc: e.tensor_scalar_mul(tmpm[:, 0:16], cJ[:, :], lc),
                  reads=["lnbc"] + CST, writes=["tmpm"], tag=f"gp1_{h}")
            pb.op("ACT", lambda e, h=h: e.activation(gpow[:, h * 16:(h + 1) * 16],
                                                     tmpm[:, 0:16], AF.Exp),
                  reads=["tmpm"], writes=[("gpow", h)], tag=f"gp2_{h}")
            pb.op("DVE", lambda e, h=h, lc=lc: e.tensor_scalar_mul(tmpm[:, 0:1], cREV[:, :], lc),
                  reads=["lnbc"] + CST, writes=["tmpm"], tag=f"wd1_{h}")
            pb.op("ACT", lambda e, h=h: e.activation(wdecs[:, h:h + 1], tmpm[:, 0:1], AF.Exp),
                  reads=["tmpm"], writes=[("wdecs", h)], tag=f"wd2_{h}")

        for h in range(16):
            pb.op("SP", lambda e, h=h: e.dma_start(out=khb[:, :], in_=k_scr[h]),
                  reads=[("ks", h)], writes=["khb"], dma=True, chain="khl", tag=f"kh1_{h}")
            pb.op("SP", lambda e, h=h: e.dma_start(out=vthb[:, :], in_=vt_scr[h]),
                  reads=[("vs", h)], writes=["vthb"], dma=True, chain="vhl", tag=f"vh1_{h}")
            Sh = S[:, h * 128:(h + 1) * 128]
            for j in range(16):
                if j >= 1:
                    pb.op("DVE", lambda e, h=h, j=j, Sh=Sh: e.tensor_copy(
                        sstg[:, (j - 1) * 128:j * 128], Sh),
                        reads=[("S", h)], writes=[("sstg", j - 1)], tag=f"sst{h}_{j}")
                pt = ptbs[0][:, 0:128]
                pb.op("PE", lambda e, j=j, pt=pt: e.transpose(
                    pt, khb[:, j * 128:(j + 1) * 128], identA[:, :]),
                    reads=["khb", "identA"] + CST, writes=[("ptb", 0)], tag=f"ktr{h}_{j}")
                pb.op("DVE", lambda e, h=h, pt=pt: e.tensor_scalar_mul(
                    ktd[:, :], pt, wdecs[:, h:h + 1]),
                    reads=[("ptb", 0), ("wdecs", h)], writes=["ktd"], tag=f"ktd{h}_{j}")
                pb.op("PE", lambda e, j=j: e.matmul(dps[2][:, 0:128], ktd[:, :],
                                                    vthb[:, j * 128:(j + 1) * 128],
                                                    start=True, stop=True),
                      reads=["ktd", "vthb"], writes=[PSK(2)], tag=f"ds{h}_{j}")
                pb.op("DVE", lambda e, h=h, Sh=Sh: e.scalar_tensor_tensor(
                    Sh, Sh, gpow[:, h * 16 + 1:h * 16 + 2], dps[2][:, 0:128],
                    OP.mult, OP.add),
                    reads=[("S", h), ("gpow", h), PSK(2)], writes=[("S", h)], tag=f"su{h}_{j}")
            pb.op("SP", lambda e, h=h: e.dma_start(out=sl_scr[h], in_=sstg[:, :]),
                  reads=[("sstg", j_) for j_ in range(15)],
                  writes=[("sls", h)], dma=True, chain="slst", tag=f"slst{h}")
            pb.op("DVE", lambda e, h=h, Sh=Sh: e.tensor_copy(sfin[:, h * 128:(h + 1) * 128], Sh),
                  reads=[("S", h)], writes=[("sfin", h)], tag=f"sfin{h}")

        pb.op("POOL", lambda e: e.dma_start(out=cc_s_in, in_=sfin[:, :]),
              reads=[("sfin", h) for h in range(16)], writes=["ccsi"], dma=True,
              chain="ccst", tag="ccst")
        pb.op("POOL", lambda e: e.collective_compute(
            "AllGather", OP.bypass, replica_groups=PAIRS,
            ins=[cc_s_in.opt()], outs=[cc_s_out.opt()]),
            reads=["ccsi"], writes=["ccso"], coll=True, chain="ccag", tag="ccag")
        pb.op("POOL", lambda e: e.dma_start(out=sinb[:, :], in_=cc_s_out[0:128, :]),
              reads=["ccso"], writes=["sinraw"], dma=True, chain="ccld", tag="ccld")
        pb.op("DVE", lambda e: e.tensor_scalar_mul(sinb[:, :], sinb[:, :], cSEL[:, :]),
              reads=["sinraw"] + CST, writes=["sinb"], tag="sinsel")

        for h in range(16):
            pb.op("SP", lambda e, h=h: e.dma_start(out=khb[:, :], in_=k_scr[h]),
                  reads=[("ks", h)], writes=["khb"], dma=True, chain="khl", tag=f"kh2_{h}")
            pb.op("SP", lambda e, h=h: e.dma_start(out=qhb[:, :], in_=q_scr[h]),
                  reads=[("qs", h)], writes=["qhb"], dma=True, chain="qhl", tag=f"qh2_{h}")
            pb.op("SP", lambda e, h=h: e.dma_start(out=vthb[:, :], in_=vt_scr[h]),
                  reads=[("vs", h)], writes=["vthb"], dma=True, chain="vhl", tag=f"vh2_{h}")
            pb.op("SP", lambda e, h=h: e.dma_start(out=slocb[:, :], in_=sl_scr[h]),
                  reads=[("sls", h)], writes=["slocb"], dma=True, chain="sll", tag=f"sll{h}")
            for j in range(16):
                pb.op("PE", lambda e, j=j: e.matmul(
                    dps[2][:, 0:128], khb[:, j * 128:(j + 1) * 128],
                    qhb[:, j * 128:(j + 1) * 128], start=True, stop=True),
                    reads=["khb", "qhb"], writes=[PSK(2)], tag=f"sc{h}_{j}")
                pb.op("DVE", lambda e, h=h: e.tensor_mul(
                    msk[:, :], dps[2][:, 0:128], Dmask[:, h * 128:(h + 1) * 128]),
                    reads=[PSK(2), ("Dmask", h)], writes=["msk"], tag=f"mk{h}_{j}")
                pb.op("DVE", lambda e, h=h, j=j: e.tensor_mul(
                    qg[:, :], qhb[:, j * 128:(j + 1) * 128], Gq[:, h * 128:(h + 1) * 128]),
                    reads=["qhb", ("Gq", h)], writes=["qg"], tag=f"qg{h}_{j}")
                pb.op("DVE", lambda e, h=h, j=j: e.tensor_scalar_mul(
                    qg2[:, :], qg[:, :], gpow[:, h * 16 + j:h * 16 + j + 1]),
                    reads=["qg", ("gpow", h)], writes=["qg2"], tag=f"qg2{h}_{j}")

                def ymm(e, h=h, j=j):
                    e.matmul(dps[3][:, 0:128], vthb[:, j * 128:(j + 1) * 128], msk[:, :],
                             start=True, stop=False)
                    if j >= 1:
                        e.matmul(dps[3][:, 0:128], slocb[:, (j - 1) * 128:j * 128], qg[:, :],
                                 start=False, stop=False)
                    return e.matmul(dps[3][:, 0:128], sinb[:, h * 128:(h + 1) * 128], qg2[:, :],
                                    start=False, stop=True)
                pb.op("PE", ymm, reads=["vthb", "msk", "slocb", "qg", "qg2", "sinb"],
                      writes=[PSK(3)], tag=f"y{h}_{j}")
                pb.op("DVE", lambda e, h=h, j=j: e.tensor_copy(
                    ret[:, h * T + j * 128: h * T + (j + 1) * 128], dps[3][:, 0:128]),
                    reads=[PSK(3)], writes=[("ret", h)], tag=f"ye{h}_{j}")

        for h in range(16):
            pb.op("ACT", lambda e, h=h: e.activation(
                dead[:, :], ret[:, h * T:(h + 1) * T], AF.Copy,
                accum_out=statc[:, h:h + 1]),
                reads=[("ret", h)], writes=["dead", ("statc", h)], tag=f"sts{h}")
            pb.op("ACT", lambda e, h=h: e.activation(
                dead[:, :], ret[:, h * T:(h + 1) * T], AF.Square,
                accum_out=statc[:, 16 + h:17 + h]),
                reads=[("ret", h)], writes=["dead", ("statc", 16 + h)], tag=f"stq{h}")
        pb.op("PE", lambda e: e.matmul(dps[5][0:1, 0:32], onesf[:, :], statc[:, :],
                                       start=True, stop=True),
              reads=["onesf"] + [("statc", i) for i in range(32)], writes=[PSK(5)], tag="stmm")
        pb.op("DVE", lambda e: e.tensor_copy(rws[0:1, 0:32], dps[5][0:1, 0:32]),
              reads=[PSK(5)], writes=["rws"], tag="stev")
        pb.op("POOL", lambda e: e.dma_start(out=cc_t_in, in_=rws[0:1, 0:32]),
              reads=["rws"], writes=["ccti"], dma=True, chain="ccst2", tag="ccst2")
        pb.op("POOL", lambda e: e.collective_compute(
            "AllReduce", OP.add, replica_groups=PAIRS,
            ins=[cc_t_in.opt()], outs=[cc_t_out.opt()]),
            reads=["ccti"], writes=["ccto"], coll=True, chain="ccar", tag="ccar")
        pb.op("POOL", lambda e: e.dma_start(out=rws[0:1, 32:64], in_=cc_t_out),
              reads=["ccto"], writes=["rws2"], dma=True, chain="ccld2", tag="ccld2")
        CNT = 1.0 / (4096.0 * 128.0)
        pb.op("DVE", lambda e: e.tensor_scalar_mul(rws[0:1, 0:16], rws[0:1, 32:48], CNT),
              reads=["rws2"], writes=["rwm"], tag="gn_mean")
        pb.op("DVE", lambda e: e.tensor_scalar_mul(rws[0:1, 16:32], rws[0:1, 48:64], CNT),
              reads=["rws2"], writes=["rwq"], tag="gn_msq")
        pb.op("DVE", lambda e: e.tensor_mul(rws[0:1, 32:48], rws[0:1, 0:16], rws[0:1, 0:16]),
              reads=["rwm"], writes=["rwm2"], tag="gn_m2")
        pb.op("DVE", lambda e: e.tensor_sub(rws[0:1, 16:32], rws[0:1, 16:32], rws[0:1, 32:48]),
              reads=["rwq", "rwm2"], writes=["rwv"], tag="gn_var")
        epst, epsk = cx.cc(1e-5, 1)
        pb.op("ACT", lambda e: e.activation(rws[0:1, 32:48], rws[0:1, 16:32], AF.Sqrt,
                                            bias=epst[0:1, 0:1]),
              reads=["rwv", epsk], writes=["rwsd"], tag="gn_sd")
        pb.op("DVE", lambda e: e.reciprocal(rws[0:1, 16:32], rws[0:1, 32:48]),
              reads=["rwsd"], writes=["rwr"], tag="gn_rstd")
        pb.op("PE", lambda e: e.matmul(dps[4][:, 0:32], ln.ones1f[0:1, :], rws[0:1, 0:32],
                                       start=True, stop=True),
              reads=["rwm", "rwr", "ln_ones1f"], writes=[PSK(4)], tag="gn_bc")
        pb.op("DVE", lambda e: e.tensor_mul(acol[:, :], bcols[:, 5 * 16:6 * 16], dps[4][:, 16:32]),
              reads=[PSK(4)] + BCOLS, writes=["acol"], tag="gn_a")
        pb.op("DVE", lambda e: e.tensor_mul(bcolt[:, :], acol[:, :], dps[4][:, 0:16]),
              reads=["acol", PSK(4)], writes=["bcolt"], tag="gn_b1")
        pb.op("DVE", lambda e: e.tensor_sub(bcolt[:, :], bcols[:, 6 * 16:7 * 16], bcolt[:, :]),
              reads=["bcolt"] + BCOLS, writes=["bcolt"], tag="gn_b2")
        for h in range(16):
            pb.op("ACT", lambda e, h=h: e.activation(
                ret[:, h * T:(h + 1) * T], ret[:, h * T:(h + 1) * T], AF.Identity,
                scale=acol[:, h:h + 1], bias=bcolt[:, h:h + 1]),
                reads=[("ret", h), "acol", "bcolt"], writes=[("ret", h)], tag=f"gnap{h}")
            pb.op("SP", lambda e, h=h: e.dma_start(out=ghb[:, :], in_=g_scr[h]),
                  reads=[("gs", h)], writes=["ghb"], dma=True, chain="ghl", tag=f"ghl{h}")
            pb.op("DVE", lambda e, h=h: e.tensor_mul(
                ret[:, h * T:(h + 1) * T], ret[:, h * T:(h + 1) * T], ghb[:, :]),
                reads=[("ret", h), "ghb"], writes=[("ret", h)], tag=f"gmul{h}")

        RETALL = [("ret", h) for h in range(16)]
        for ts in range(4):
            for c in range(DCH):
                pb.op("SP", lambda e, c=c, ts=ts: e.dma_start(
                    out=xold[:, c * 512:(c + 1) * 512],
                    in_=xres[c * 128:(c + 1) * 128, ts * 512:(ts + 1) * 512]),
                    writes=[("xold", c)], dma=True, chain="xold", tag=f"xold{ts}_{c}")
            for oc in range(16):
                wb = wob[oc % 2]
                pb.op("POOL", lambda e, oc=oc, wb=wb, ts=ts: e.dma_start(out=wb[:, :], in_=w_o[oc]),
                      writes=[("wob", oc % 2)], dma=True, chain="wol", tag=f"wol{ts}_{oc}")
                bank, bk = dps[oc % 2], PSK(oc % 2)

                def mmo(e, wb=wb, bank=bank, ts=ts):
                    last = None
                    for k in range(16):
                        last = e.matmul(bank[:, :], wb[:, k * 128:(k + 1) * 128],
                                        ret[:, k * T + ts * 512: k * T + (ts + 1) * 512],
                                        start=(k == 0), stop=(k == 15))
                    return last
                pb.op("PE", mmo, reads=[("wob", oc % 2)] + RETALL, writes=[bk], tag=f"omm{ts}_{oc}")
                pb.op("DVE", lambda e, bank=bank, oc=oc: e.scalar_tensor_tensor(
                    s_tile[:, oc * 512:(oc + 1) * 512], bank[:, :], bcol(4, oc),
                    xold[:, oc * 512:(oc + 1) * 512], OP.add, OP.add),
                    reads=[bk, ("xold", oc)] + BCOLS, writes=[("s", oc)], tag=f"sadd{ts}_{oc}")
            ln.emit(ts, s_tile, [("s", c) for c in range(DCH)], xout, okey, None, [],
                    tag=f"ln1_{ts}")
    esS.__exit__(None, None, None)


def build_stage1(nc, es):
    pb = PB(nc, es)
    cx = Ctx(nc, es, pb)
    A = declare_stage1(nc)
    xout = nc.dram_tensor("s1_xout", [D, T], F32, kind="ExternalOutput").ap()
    dps = [cx.ps(f"dps{i}", [128, 512], F32) for i in range(6)]
    PSK = lambda i: ("psum", i)
    ln = LNKit(cx, [(dps[2], PSK(2)), (dps[3], PSK(3)), (dps[4], PSK(4)), (dps[5], PSK(5))])
    emit_stage1(cx, A, dps, ln, xout, "s1xo")
    pb.op("POOL", lambda e: e.nop(),
          reads=[("s1xo", ts, c) for ts in range(4) for c in range(DCH)], tag="fin")
    pb.emit()
    return nc


def _s1_host_base(wq, bq, wk, bk, wv, bv, wg, bg, wo, bo, gn_w, gn_b, gamma):
    idx = np.arange(128, dtype=np.float32)
    DLh = idx[None, :] - idx[:, None]
    T1h = np.broadcast_to(idx[None, :] + 1.0, (128, 128)).copy()
    C01h = (idx[None, :] >= idx[:, None]).astype(np.float32)
    REVh = (127.0 - idx)[:, None].copy()
    J128h = np.broadcast_to(128.0 * np.arange(16, dtype=np.float32)[None, :], (128, 16)).copy()
    IDBh = np.eye(128).astype(BF)
    bcolsh = np.concatenate([vecc(v) for v in (bq, bk, bg, bv, bo, gn_w, gn_b)], axis=0)
    return {"s1_wq": tile_lhsT(wq), "s1_wk": tile_lhsT(wk), "s1_wv": tile_lhsT(wv),
            "s1_wg": tile_lhsT(wg), "s1_wo": tile_lhsT(wo),
            "s1_bcols": bcolsh, "s1_gam": np.asarray(gamma).reshape(1, 16).astype(np.float32),
            "s1_DL": DLh.astype(np.float32), "s1_T1": T1h, "s1_C01": C01h,
            "s1_REV": REVh, "s1_J128": J128h, "s1_IDB": IDBh}


def run_stage1(x, wq, bq, wk, bk, wv, bv, wg, bg, wo, bo, gn_w, gn_b, gamma):
    nc = bass.Bass("TRN2", target_bir_lowering=False)
    es = ExitStack()
    with es:
        build_stage1(nc, es)
    base = _s1_host_base(wq, bq, wk, bk, wv, bv, wg, bg, wo, bo, gn_w, gn_b, gamma)
    xs = _shard_x_fm(x)
    ims = []
    for c in range(NCORES):
        im = dict(base)
        im["s1_xres"] = xs[c]
        im["s1_SEL"] = np.full((128, 1), float(c % 2), np.float32)
        ims.append(im)
    res = run_bass_kernel_spmd(nc, ims, core_ids=list(range(NCORES)))
    return _gather_fm(res, "s1_xout", x)



def declare_stage3(nc, xres=None):
    A = {}
    NH, DS, NCH = 32, 64, 16
    A["xres"] = xres if xres is not None else nc.dram_tensor("s3_xres", [D, T], F32, kind="ExternalInput").ap()
    for nm, sh, dt in [("win", [34, 128, 2048], F32), ("wop", [16, 128, 2048], F32),
                       ("mgu", [128, 128, 2048], F32), ("mwd", [16, 128, 64 * 128], F32),
                       ("ccols", [128, 128], F32), ("convw", [68, 128], F32),
                       ("arow", [1, 32], F32), ("drow", [1, 32], F32),
                       ("TRIL", [128, 128], F32), ("TRIU", [128, 128], F32),
                       ("C01", [128, 128], F32), ("IDF", [128, 128], F32),
                       ("IDB", [128, 128], BF16), ("SEL", [128, 1], F32)]:
        A[nm] = nc.dram_tensor(f"s3_{nm}", sh, dt, kind="ExternalInput").ap()
    for nm, sh, dt in [("zs", [16, 128, T], BF16), ("xct", [NCH, 128, 2048], BF16),
                       ("bct", [NCH, 128, 128], BF16), ("stl", [NCH, 64, NH * DS], BF16),
                       ("yn", [NCH, 128, 2048], BF16), ("xmid", [D, T], F32),
                       ("h", [64, 128, T], BF16), ("mwdbf", [64, 128, 16 * 128], BF16),
                       ("ccci", [128, 51], BF16), ("ccco", [256, 51], BF16),
                       ("ccsi", [64, NH * DS], BF16), ("ccso", [128, NH * DS], BF16)]:
        A[nm] = nc.dram_tensor(f"s3_{nm}", sh, dt, kind="Internal").ap()
    return A


def emit_stage3(cx, A, dps, ln, xout, okey):
    pb, nc = cx.pb, cx.nc
    NH, DH, DS, Q = 32, 64, 64, 128
    NCH = 16
    xres = A["xres"]
    w_in, w_op, mlp_gu, mlp_wd = A["win"], A["wop"], A["mgu"], A["mwd"]
    ccols, convw, arow, drow = A["ccols"], A["convw"], A["arow"], A["drow"]
    TRIL, TRIU, C01, IDF, IDB, SEL = A["TRIL"], A["TRIU"], A["C01"], A["IDF"], A["IDB"], A["SEL"]
    z_scr, xct_scr, bct_scr, stl_scr, yn_scr = A["zs"], A["xct"], A["bct"], A["stl"], A["yn"]
    xmid_scr, h_scr, mwd_scr = A["xmid"], A["h"], A["mwdbf"]
    cc_c_in, cc_c_out, cc_s_in, cc_s_out = A["ccci"], A["ccco"], A["ccsi"], A["ccso"]
    PSK = lambda i: ("psum", i)
    cx.cc(1e-5, 1)

    esS = ExitStack()
    esS.__enter__()

    def sbS(nm, sh, dt=F32):
        return esS.enter_context(nc.sbuf_tensor(nm, sh, dt, allow_name_mangling=True))
    cco = sbS("cco", [128, 128], F32)
    cvw = sbS("cvw", [128, 68], F32)
    cTRIL = sbS("cTRIL", [128, 128], F32)
    cTRIU = sbS("cTRIU", [128, 128], F32)
    cC01 = sbS("cC01", [128, 128], F32)
    cIDF = sbS("cIDF", [128, 128], F32)
    cIDB = sbS("cIDB", [128, 128], BF16)
    cSEL = sbS("cSEL", [128, 1], F32)
    car = sbS("car", [1, 32], F32)
    cdr = sbS("cdr", [1, 32], F32)
    acol = sbS("acol", [32, 1], F32)

    pb.op("SP", lambda e: e.dma_start(out=cco[:, :], in_=ccols), writes=["cco"], dma=True,
          chain="cst", tag="ccols")
    for t_, ap_, k_ in [(cvw, convw.rearrange("a b -> b a"), "cvw0")]:
        pass
    for v in range(68):
        pb.op("SP", lambda e, v=v: e.dma_start(out=cvw[:, v:v + 1], in_=convw[v, :]),
              writes=[("cvw", v)], dma=True, chain="cst", tag=f"cvw{v}")
    CVW = [("cvw", v) for v in range(68)]
    for t_, ap_ in [(cTRIL, TRIL), (cTRIU, TRIU), (cC01, C01), (cIDF, IDF), (cIDB, IDB),
                    (cSEL, SEL), (car, arow), (cdr, drow)]:
        pb.op("SP", lambda e, t_=t_, ap_=ap_: e.dma_start(out=t_[:, :], in_=ap_),
              writes=[("cst", id(t_))], dma=True, chain="cst", tag="cst")
    CST = ["cco"] + CVW + [("cst", id(t_)) for t_ in (cTRIL, cTRIU, cC01, cIDF, cIDB, cSEL, car, cdr)]

    def ccol(i):
        return cco[:, i:i + 1]

    pb.op("ACT", lambda e: e.activation(acol[:, :], cco[0:32, 65:66], AF.Exp),
          reads=CST, writes=["acol"], tag="aexp")
    pb.op("DVE", lambda e: e.tensor_scalar_mul(acol[:, :], acol[:, :], -1.0),
          reads=["acol"], writes=["acol"], tag="aneg")

    XBCK = [("xbc", i) for i in range(17)]

    esB = ExitStack()
    esB.__enter__()
    xBC = esB.enter_context(nc.sbuf_tensor("xBC", [128, 17 * T], BF16, allow_name_mangling=True))
    stk = esB.enter_context(nc.sbuf_tensor("stk", [64, T], F32, allow_name_mangling=True))

    es1 = ExitStack()
    with es1:
        def sb1(nm, sh, dt=F32):
            return es1.enter_context(nc.sbuf_tensor(nm, sh, dt, allow_name_mangling=True))
        xnsl = sb1("xnsl", [128, DCH * 512], BF16)
        xsl = sb1("xsl", [128, DCH * 512], F32)
        sq1 = sb1("sq1", [128, 2 * 512], BF16)
        rws1 = sb1("rws1", [1, 2 * 512], F32)
        wpb = [sb1(f"wpb{i}", [128, 16 * 128], BF16) for i in range(2)]
        zstg = sb1("zstg", [128, 512], BF16)
        dstg = sb1("dstg", [32, T], F32)
        onesb1 = sb1("onesb1", [128, 1], BF16)
        pb.op("DVE", lambda e: e.memset(onesb1[:, :], 1.0), writes=["onesb1"], tag="ones1b")
        epst, epsk = cx.cc(1e-5, 1)

        XN = [("xnsl", c) for c in range(DCH)]
        for ts in range(4):
            for c in range(DCH):
                pb.op("SP", lambda e, c=c, ts=ts: e.dma_start(
                    out=xsl[:, c * 512:(c + 1) * 512],
                    in_=xres[c * 128:(c + 1) * 128, ts * 512:(ts + 1) * 512]),
                    writes=[("xsl", c)], dma=True, chain="xsl", tag=f"x1_{ts}_{c}")
            for c in range(DCH):
                par = c % 2
                pb.op("ACT", lambda e, c=c, par=par: e.activation(
                    sq1[:, par * 512:(par + 1) * 512], xsl[:, c * 512:(c + 1) * 512], AF.Square),
                    reads=[("xsl", c)], writes=[("sq1", par)], tag=f"q1_{ts}_{c}")
                pb.op("PE", lambda e, c=c, par=par: e.matmul(
                    dps[2][0:1, :], onesb1[:, 0:1], sq1[:, par * 512:(par + 1) * 512],
                    start=(c == 0), stop=(c == DCH - 1)),
                    reads=[("sq1", par), "onesb1"], writes=[PSK(2)], tag=f"qm1_{ts}_{c}")
            pb.op("DVE", lambda e: e.tensor_scalar_mul(rws1[0:1, 0:512], dps[2][0:1, :], 1.0 / D),
                  reads=[PSK(2)], writes=["r1m"], tag=f"r1m{ts}")
            pb.op("ACT", lambda e: e.activation(rws1[0:1, 512:1024], rws1[0:1, 0:512], AF.Sqrt,
                                                bias=epst[0:1, 0:1]),
                  reads=["r1m", epsk], writes=["r1s"], tag=f"r1s{ts}")
            pb.op("DVE", lambda e: e.reciprocal(rws1[0:1, 0:512], rws1[0:1, 512:1024]),
                  reads=["r1s"], writes=["r1m"], tag=f"r1r{ts}")
            pb.op("PE", lambda e: e.matmul(dps[3][:, :], ln.ones1f[0:1, :], rws1[0:1, 0:512],
                                           start=True, stop=True),
                  reads=["r1m", "ln_ones1f"], writes=[PSK(3)], tag=f"r1b{ts}")
            for c in range(DCH):
                pb.op("DVE", lambda e, c=c, ts=ts: e.scalar_tensor_tensor(
                    xnsl[:, c * 512:(c + 1) * 512],
                    xsl[:, c * 512:(c + 1) * 512], ccol(c), dps[3][:, :], OP.mult, OP.mult),
                    reads=[("xsl", c), PSK(3)] + CST, writes=[("xnsl", c)], tag=f"xn{ts}_{c}")

            for oc in range(34):
                wb = wpb[oc % 2]
                pb.op("POOL", lambda e, oc=oc, wb=wb, ts=ts: e.dma_start(out=wb[:, :], in_=w_in[oc]),
                      writes=[("wpb", oc % 2)], dma=True, chain="winl", tag=f"winl{ts}_{oc}")
                bank, bk = dps[oc % 2], PSK(oc % 2)

                def mm(e, wb=wb, bank=bank):
                    last = None
                    for k in range(16):
                        last = e.matmul(bank[:, :], wb[:, k * 128:(k + 1) * 128],
                                        xnsl[:, k * 512:(k + 1) * 512],
                                        start=(k == 0), stop=(k == 15))
                    return last
                pb.op("PE", mm, reads=[("wpb", oc % 2)] + XN, writes=[bk], tag=f"ipm{oc}_{ts}")
                if oc < 16:
                    pb.op("ACT", lambda e, bank=bank: e.activation(
                        zstg[:, :], bank[:, :], AF.Copy),
                        reads=[bk], writes=["zstg"], tag=f"zev{oc}_{ts}")
                    pb.op("SP", lambda e, oc=oc, ts=ts: e.dma_start(
                        out=z_scr[oc, :, ts * 512:(ts + 1) * 512], in_=zstg[:, :]),
                        reads=["zstg"], writes=[("zs", oc)], dma=True, chain="zst",
                        tag=f"zst{ts}_{oc}")
                elif oc < 33:
                    pb.op("ACT", lambda e, bank=bank, ts=ts, oc=oc: e.activation(
                        xBC[:, (oc - 16) * T + ts * 512:(oc - 16) * T + (ts + 1) * 512],
                        bank[:, :], AF.Copy),
                        reads=[bk], writes=[("xbc", oc - 16)], tag=f"bev{oc}_{ts}")
                else:
                    pb.op("DVE", lambda e, bank=bank, ts=ts: e.tensor_copy(
                        dstg[:, ts * 512:(ts + 1) * 512], bank[0:32, :]),
                        reads=[bk], writes=[("dstg", ts)], tag=f"dev{ts}")

        dtb = dstg
        pb.op("DVE", lambda e: e.tensor_scalar_add(dstg[:, :], dstg[:, :], cco[0:32, 66:67]),
              reads=[("dstg", t_) for t_ in range(4)] + CST, writes=["dtb"], tag="dtb")
        ab1 = sb1("ab1", [32, T], F32)
        ab2 = sb1("ab2", [32, T], F32)
        pb.op("DVE", lambda e: e.scalar_tensor_tensor(ab1[:, :], dstg[:, :], -1.0, dstg[:, :],
                                                      OP.mult, OP.max),
              reads=["dtb"], writes=["ab1"], tag="dtabs")
        pb.op("ACT", lambda e: e.activation(ab2[:, :], ab1[:, :], AF.Exp, scale=-1.0),
              reads=["ab1"], writes=["ab2"], tag="dtexp")
        pb.op("ACT", lambda e: e.activation(ab1[:, :], ab2[:, :], AF.Ln, bias=1.0),
              reads=["ab2"], writes=["ab1"], tag="dtln")
        pb.op("DVE", lambda e: e.tensor_scalar_max(ab2[:, :], dstg[:, :], 0.0),
              reads=["dtb"], writes=["ab2"], tag="dtrelu")
        pb.op("DVE", lambda e: e.tensor_add(ab1[:, :], ab1[:, :], ab2[:, :]),
              reads=["ab1", "ab2"], writes=["dt"], tag="dtsum")
        pb.op("DVE", lambda e: e.tensor_scalar_mul(stk[0:32, :], ab1[:, :], acol[:, :]),
              reads=["dt", "acol"], writes=["stk0"], tag="adt")
        pb.op("ACT", lambda e: e.activation(stk[32:64, :], ab1[:, :], AF.Ln),
              reads=["dt"], writes=["stk1"], tag="lndt")

        slv = sb1("slv", [128, 51], BF16)
        extb = sb1("extb", [128, 51], BF16)
        for cc_ in range(17):
            pb.op("DVE", lambda e, cc_=cc_: e.tensor_copy(
                slv[:, cc_ * 3:(cc_ + 1) * 3], xBC[:, cc_ * T + T - 3: cc_ * T + T]),
                reads=[("xbc", cc_)], writes=[("slv", cc_)], tag=f"slv{cc_}")
        pb.op("POOL", lambda e: e.dma_start(out=cc_c_in, in_=slv[:, :]),
              reads=[("slv", i) for i in range(17)], writes=["ccci"], dma=True,
              chain="ccc", tag="cccst")
        pb.op("POOL", lambda e: e.collective_compute(
            "AllGather", OP.bypass, replica_groups=PAIRS,
            ins=[cc_c_in.opt()], outs=[cc_c_out.opt()]),
            reads=["ccci"], writes=["ccco"], coll=True, chain="cccg", tag="cccg")
        pb.op("POOL", lambda e: e.dma_start(out=extb[:, :], in_=cc_c_out[0:128, :]),
              reads=["ccco"], writes=["extraw"], dma=True, chain="cccl", tag="cccl")
        pb.op("DVE", lambda e: e.tensor_scalar_mul(extb[:, :], extb[:, :], cSEL[:, :]),
              reads=["extraw"] + CST, writes=["extb"], tag="extsel")

        ext = sb1("ext", [128, 3 + T], BF16)
        cacc = sb1("cacc", [128, T], F32)
        for cc_ in range(17):
            pb.op("DVE", lambda e, cc_=cc_: e.tensor_copy(ext[:, 0:3], extb[:, cc_ * 3:(cc_ + 1) * 3]),
                  reads=["extb"], writes=["ext3"], tag=f"cx3_{cc_}")
            pb.op("DVE", lambda e, cc_=cc_: e.tensor_copy(ext[:, 3:3 + T], xBC[:, cc_ * T:(cc_ + 1) * T]),
                  reads=[("xbc", cc_)], writes=["extT"], tag=f"cxT_{cc_}")
            pb.op("DVE", lambda e, cc_=cc_: e.tensor_scalar_mul(
                cacc[:, :], ext[:, 0:T], cvw[:, cc_ * 4:cc_ * 4 + 1]),
                reads=["ext3", "extT"] + CST, writes=["cacc"], tag=f"cv0_{cc_}")
            for kk in range(1, 4):
                pb.op("DVE", lambda e, cc_=cc_, kk=kk: e.scalar_tensor_tensor(
                    cacc[:, :], ext[:, kk:kk + T], cvw[:, cc_ * 4 + kk:cc_ * 4 + kk + 1],
                    cacc[:, :], OP.mult, OP.add),
                    reads=["ext3", "extT", "cacc"] + CST, writes=["cacc"], tag=f"cv{kk}_{cc_}")
            pb.op("ACT", lambda e, cc_=cc_: e.activation(
                xBC[:, cc_ * T:(cc_ + 1) * T], cacc[:, :], AF.Silu, bias=ccol(48 + cc_)),
                reads=["cacc"] + CST, writes=[("xbc", cc_)], tag=f"cvs_{cc_}")
        pb.full_barrier("end1")

    es2 = ExitStack()
    with es2:
        def sb2(nm, sh, dt=F32):
            return es2.enter_context(nc.sbuf_tensor(nm, sh, dt, allow_name_mangling=True))
        ptbs = [es2.enter_context(nc.psum_tensor(f"ptb{i}", [128, 512], BF16, allow_name_mangling=True)) for i in range(2)]
        XT = sb2("XT", [128, 2048], BF16)
        BCT = sb2("BCT", [128, 128], BF16)
        ST = sb2("ST", [64, NH * DS], F32)
        STst = sb2("STst", [64, NH * DS], BF16)
        STin = sb2("STin", [64, NH * DS], BF16)
        STlo = sb2("STlo", [64, NH * DS], BF16)
        ybuf = sb2("ybuf", [128, 2048], BF16)
        yz = sb2("yz", [128, 2048], F32)
        zbuf = sb2("zbuf", [128, 2048], BF16)
        szb = sb2("szb", [128, 2048], F32)
        sqz = sb2("sqz", [128, 2 * 128], BF16)
        Cb0 = sb2("Cb0", [64, 128], BF16)
        adlT = sb2("adlT", [128, 64], F32)
        cumAT = sb2("cumAT", [128, 32], F32)
        vwt = sb2("vwt", [128, 32], F32)
        bcolt = sb2("bcolt3", [128, 32], F32)
        cumaF = sb2("cumaF", [32, 128], F32)
        crowsb = sb2("crowsb", [1, 128], F32)
        msk3 = sb2("msk3", [128, 128], F32)
        ecb = sb2("ecb", [64, 128], F32)
        Wt = sb2("Wt", [128, 128], BF16)
        Cw = sb2("Cw", [64, 128], BF16)
        Bw = sb2("Bw", [128, 64], BF16)
        CBc = sb2("CBc", [128, 128], F32)
        dabc = sb2("dabc", [64, 32], F32)
        pbc = sb2("pbc", [64, 32], F32)
        dcol = sb2("dcol", [128, 32], F32)
        exr = sb2("exr", [1, 32], F32)
        logp = sb2("logp", [1, 32], F32)
        rws3 = sb2("rws3", [1, 2 * 128], F32)
        rbz = sb2("rbz", [128, 128], F32)
        onesb3 = sb2("onesb3", [128, 1], BF16)
        onesf3 = sb2("onesf3", [128, 1], F32)
        pb.op("DVE", lambda e: e.memset(onesb3[:, :], 1.0), writes=["onesb3"], tag="o3b")
        pb.op("DVE", lambda e: e.memset(onesf3[:, :], 1.0), writes=["onesf3"], tag="o3f")
        pb.op("DVE", lambda e: e.memset(logp[:, :], 0.0), writes=["logp"], tag="lp0")
        pb.op("DVE", lambda e: e.memset(ST[:, :], 0.0), writes=["ST"], tag="st0")

        pb.op("PE", lambda e: e.matmul(dps[4][:, 0:32], ln.ones1f[0:1, :], cdr[0:1, :],
                                       start=True, stop=True),
              reads=CST + ["ln_ones1f"], writes=[PSK(4)], tag="dbc")
        pb.op("DVE", lambda e: e.tensor_copy(dcol[:, :], dps[4][:, 0:32]),
              reads=[PSK(4)], writes=["dcol"], tag="dbce")

        def sschunk(c, pass2):
            sfx = f"p{2 if pass2 else 1}_{c}"
            pb.op("PE", lambda e, c=c: e.transpose(
                dps[0][:, 0:64], stk[:, c * Q:(c + 1) * Q], cIDF[0:64, 0:64]),
                reads=["stk0", "stk1"] + CST, writes=[PSK(0)], tag=f"adl{sfx}")
            pb.op("DVE", lambda e: e.tensor_copy(adlT[:, :], dps[0][:, 0:64]),
                  reads=[PSK(0)], writes=["adlT"], tag=f"adle{sfx}")
            if not pass2:
                pb.op("PE", lambda e: e.matmul(dps[0][:, 0:32], cTRIU[:, :], adlT[:, 0:32],
                                               start=True, stop=True),
                      reads=["adlT"] + CST, writes=[PSK(0)], tag=f"rcu{sfx}")
                pb.op("DVE", lambda e: e.tensor_add(vwt[:, :], dps[0][:, 0:32], adlT[:, 32:64]),
                      reads=[PSK(0), "adlT"], writes=["vwt"], tag=f"vw1{sfx}")
                pb.op("ACT", lambda e: e.activation(vwt[:, :], vwt[:, :], AF.Exp),
                      reads=["vwt"], writes=["vwt"], tag=f"vw2{sfx}")
            else:
                pb.op("PE", lambda e: e.matmul(dps[0][:, 0:32], cTRIL[:, :], adlT[:, 0:32],
                                               start=True, stop=True),
                      reads=["adlT"] + CST, writes=[PSK(0)], tag=f"cma{sfx}")
                pb.op("DVE", lambda e: e.tensor_copy(cumAT[:, :], dps[0][:, 0:32]),
                      reads=[PSK(0)], writes=["cumAT"], tag=f"cme{sfx}")
                pb.op("DVE", lambda e: e.tensor_sub(bcolt[:, :], adlT[:, 32:64], cumAT[:, :]),
                      reads=["adlT", "cumAT"], writes=["bcolt3"], tag=f"bct{sfx}")
                pb.op("PE", lambda e: e.transpose(dps[1][0:32, 0:128], cumAT[:, :], cIDF[:, :]),
                      reads=["cumAT"] + CST, writes=[PSK(1)], tag=f"cmf{sfx}")
                pb.op("DVE", lambda e: e.tensor_copy(cumaF[:, :], dps[1][0:32, 0:128]),
                      reads=[PSK(1)], writes=["cumaF"], tag=f"cmfe{sfx}")
            pb.op("PE", lambda e: e.matmul(dps[1][0:1, 0:32], onesf3[:, :], adlT[:, 0:32],
                                           start=True, stop=True),
                  reads=["adlT", "onesf3"], writes=[PSK(1)], tag=f"tot{sfx}")
            if not pass2:
                pb.op("ACT", lambda e: e.activation(exr[0:1, :], dps[1][0:1, 0:32], AF.Exp),
                      reads=[PSK(1)], writes=["exr"], tag=f"totx{sfx}")
                pb.op("PE", lambda e: e.matmul(dps[1][0:64, 0:32], ln.ones1f[0:1, 0:64],
                                               exr[0:1, :], start=True, stop=True),
                      reads=["exr", "ln_ones1f"], writes=[PSK(1)], tag=f"dab{sfx}")
                pb.op("DVE", lambda e: e.tensor_copy(dabc[:, :], dps[1][0:64, 0:32]),
                      reads=[PSK(1)], writes=["dabc"], tag=f"dabe{sfx}")
            else:
                pb.op("PE", lambda e: e.matmul(dps[1][0:64, 32:64], ln.ones1f[0:1, 0:64],
                                               logp[0:1, :], start=True, stop=True),
                      reads=["logp", "ln_ones1f"], writes=[PSK(1)], tag=f"lpb{sfx}")
                pb.op("ACT", lambda e: e.activation(pbc[:, :], dps[1][0:64, 32:64], AF.Exp),
                      reads=[PSK(1)], writes=["pbc"], tag=f"pbx{sfx}")
                pb.op("DVE", lambda e: e.tensor_add(logp[0:1, :], logp[0:1, :], dps[1][0:1, 0:32]),
                      reads=["logp", PSK(1)], writes=["logp"], tag=f"lpu{sfx}")

            if not pass2:
                for f in range(16):
                    pt = ptbs[f % 2][:, 0:128]
                    pb.op("PE", lambda e, f=f, c=c, pt=pt: e.transpose(
                        pt, xBC[:, f * T + c * Q:f * T + (c + 1) * Q], cIDB[:, :]),
                        reads=[("xbc", f)] + CST, writes=[PSK(6 + f % 2)], tag=f"xtr{sfx}_{f}")
                    pb.op("DVE", lambda e, f=f, pt=pt: e.tensor_copy(
                        XT[:, f * 128:(f + 1) * 128], pt),
                        reads=[PSK(6 + f % 2)], writes=[("XT", f)], tag=f"xte{sfx}_{f}")
                pt = ptbs[0][:, 0:128]
                pb.op("PE", lambda e, c=c, pt=pt: e.transpose(
                    pt, xBC[:, 16 * T + c * Q:16 * T + (c + 1) * Q], cIDB[:, :]),
                    reads=[("xbc", 16)] + CST, writes=[PSK(6)], tag=f"btr{sfx}")
                pb.op("DVE", lambda e, pt=pt: e.tensor_copy(BCT[:, :], pt),
                      reads=[PSK(6)], writes=["BCT"], tag=f"bte{sfx}")
                pb.op("SP", lambda e, c=c: e.dma_start(out=xct_scr[c], in_=XT[:, :]),
                      reads=[("XT", f) for f in range(16)], writes=[("xct", c)],
                      dma=True, chain="xctst", tag=f"xcts{sfx}")
                pb.op("SP", lambda e, c=c: e.dma_start(out=bct_scr[c], in_=BCT[:, :]),
                      reads=["BCT"], writes=[("bct", c)], dma=True, chain="bctst", tag=f"bcts{sfx}")
                pb.op("DVE", lambda e: e.tensor_copy(STst[:, :], ST[:, :]),
                      reads=["ST"], writes=["STst"], tag=f"stc{sfx}")
                pb.op("SP", lambda e, c=c: e.dma_start(out=stl_scr[c], in_=STst[:, :]),
                      reads=["STst"], writes=[("stl", c)], dma=True, chain="stlst", tag=f"stls{sfx}")
            else:
                pb.op("SP", lambda e, c=c: e.dma_start(out=XT[:, :], in_=xct_scr[c]),
                      reads=[("xct", c)], writes=[("XT", f_) for f_ in range(16)],
                      dma=True, chain="xctld", tag=f"xctl{sfx}")
                pb.op("SP", lambda e, c=c: e.dma_start(out=BCT[:, :], in_=bct_scr[c]),
                      reads=[("bct", c)], writes=["BCT"], dma=True, chain="bctld", tag=f"bctl{sfx}")
                pb.op("SP", lambda e, c=c: e.dma_start(out=STlo[:, :], in_=stl_scr[c]),
                      reads=[("stl", c)], writes=["STlo"], dma=True, chain="stlld", tag=f"stll{sfx}")
                pb.op("PE", lambda e: e.transpose(ptbs[0][0:64, 0:128], BCT[:, 64:128], cIDB[:, :]),
                      reads=["BCT"] + CST, writes=[PSK(6)], tag=f"cb0t{sfx}")
                pb.op("DVE", lambda e: e.tensor_copy(Cb0[:, :], ptbs[0][0:64, 0:128]),
                      reads=[PSK(6)], writes=["Cb0"], tag=f"cb0e{sfx}")
                pb.op("PE", lambda e, c=c: e.matmul(
                    dps[0][:, 0:128], xBC[0:64, 16 * T + c * Q:16 * T + (c + 1) * Q],
                    Cb0[:, :], start=True, stop=True),
                    reads=[("xbc", 16), "Cb0"], writes=[PSK(0)], tag=f"cb{sfx}")
                pb.op("DVE", lambda e: e.tensor_mul(CBc[:, :], dps[0][:, 0:128], cC01[:, :]),
                      reads=[PSK(0)] + CST, writes=["CBc"], tag=f"cbc{sfx}")

            for h in range(NH):
                hb = 64 * (h % 2)
                fch = h // 2
                if not pass2:
                    pb.op("DVE", lambda e, h=h: e.tensor_scalar_mul(
                        Bw[:, :], BCT[:, 0:64], vwt[:, h:h + 1]),
                        reads=["BCT", "vwt"], writes=["Bw"], tag=f"bw{sfx}_{h}")
                    pb.op("PE", lambda e, h=h: e.matmul(
                        dps[2][0:64, 0:64], Bw[:, :], XT[:, h * 64:(h + 1) * 64],
                        start=True, stop=True),
                        reads=["Bw"] + [("XT", f_) for f_ in range(16)], writes=[PSK(2)],
                        tag=f"dst{sfx}_{h}")
                    pb.op("DVE", lambda e, h=h: e.scalar_tensor_tensor(
                        ST[:, h * DS:(h + 1) * DS], ST[:, h * DS:(h + 1) * DS],
                        dabc[:, h:h + 1], dps[2][0:64, 0:64], OP.mult, OP.add),
                        reads=["ST", "dabc", PSK(2)], writes=["ST"], tag=f"stu{sfx}_{h}")
                else:
                    pb.op("PE", lambda e, h=h: e.matmul(
                        dps[1][0:1, 0:128], cIDF[0:32, h:h + 1], cumaF[:, :],
                        start=True, stop=True),
                        reads=["cumaF"] + CST, writes=[PSK(1)], tag=f"cre{sfx}_{h}")
                    pb.op("DVE", lambda e: e.tensor_copy(crowsb[:, :], dps[1][0:1, 0:128]),
                          reads=[PSK(1)], writes=["crowsb"], tag=f"crc{sfx}_{h}")
                    pb.op("PE", lambda e: e.matmul(dps[1][:, 128:256], ln.ones1f[0:1, :],
                                                   crowsb[0:1, :], start=True, stop=True),
                          reads=["crowsb", "ln_ones1f"], writes=[PSK(1)], tag=f"gbc{sfx}_{h}")
                    pb.op("PE", lambda e: e.matmul(dps[1][0:64, 256:384], ln.ones1f[0:1, 0:64],
                                                   crowsb[0:1, :], start=True, stop=True),
                          reads=["crowsb", "ln_ones1f"], writes=[PSK(1)], tag=f"g2b{sfx}_{h}")
                    pb.op("DVE", lambda e, h=h: e.tensor_scalar(
                        msk3[:, :], dps[1][:, 128:256], bcolt[:, h:h + 1], 30.0,
                        OP.add, OP.min),
                        reads=[PSK(1), "bcolt3"], writes=["msk3"], tag=f"mka{sfx}_{h}")
                    pb.op("ACT", lambda e: e.activation(msk3[:, :], msk3[:, :], AF.Exp),
                        reads=["msk3"], writes=["msk3"], tag=f"mk{sfx}_{h}")
                    pb.op("DVE", lambda e: e.tensor_mul(Wt[:, :], msk3[:, :], CBc[:, :]),
                          reads=["msk3", "CBc"], writes=["Wt"], tag=f"wt{sfx}_{h}")
                    pb.op("ACT", lambda e: e.activation(ecb[:, :], dps[1][0:64, 256:384], AF.Exp),
                          reads=[PSK(1)], writes=["ecb"], tag=f"ec{sfx}_{h}")
                    pb.op("DVE", lambda e: e.tensor_mul(Cw[:, :], Cb0[:, :], ecb[:, :]),
                        reads=["Cb0", "ecb"], writes=["Cw"], tag=f"cw{sfx}_{h}")
                    pb.op("DVE", lambda e, h=h: e.scalar_tensor_tensor(
                        STst[:, h * DS:(h + 1) * DS], STin[:, h * DS:(h + 1) * DS],
                        pbc[:, h:h + 1], STlo[:, h * DS:(h + 1) * DS], OP.mult, OP.add),
                        reads=["STin", "pbc", "STlo"], writes=["STst"], tag=f"sef{sfx}_{h}")

                    def ymm(e, h=h, hb=hb):
                        tp = (0, hb)
                        e.matmul(dps[2][hb:hb + 64, 0:128], STst[:, h * DS:(h + 1) * DS],
                                 Cw[:, :], start=True, stop=False, tile_position=tp)
                        return e.matmul(dps[2][hb:hb + 64, 0:128], XT[:, h * 64:(h + 1) * 64],
                                        Wt[:, :], start=False, stop=True, tile_position=tp)
                    pb.op("PE", ymm, reads=["STst", "Cw", "Wt"] + [("XT", f_) for f_ in range(16)],
                          writes=[PSK(2)], tag=f"ym{sfx}_{h}")
                    pb.op("DVE", lambda e, h=h, hb=hb, fch=fch, c=c: e.scalar_tensor_tensor(
                        ybuf[hb:hb + 64, fch * 128:(fch + 1) * 128],
                        xBC[hb:hb + 64, fch * T + c * Q:(fch) * T + (c + 1) * Q],
                        dcol[hb:hb + 64, h:h + 1], dps[2][hb:hb + 64, 0:128], OP.mult, OP.add),
                        reads=[("xbc", fch), "dcol", PSK(2)], writes=[("ybuf", h)],
                        tag=f"ye{sfx}_{h}")

            if pass2:
                pb.op("SP", lambda e, c=c: e.dma_start(
                    out=zbuf[:, :],
                    in_=z_scr.rearrange("k p (c j) -> c k p j", j=128)[c].rearrange("k p j -> p k j")),
                    reads=[("zs", k_) for k_ in range(16)], writes=["zbuf"], dma=True,
                    chain="zld", tag=f"zl{sfx}")
                pb.op("ACT", lambda e: e.activation(szb[:, :], zbuf[:, :], AF.Silu),
                      reads=["zbuf"], writes=["szb"], tag=f"sz{sfx}")
                pb.op("DVE", lambda e: e.tensor_tensor(yz[:, :], ybuf[:, :], szb[:, :], OP.mult),
                      reads=[("ybuf", h_) for h_ in range(NH)] + ["szb"], writes=["yz"],
                      tag=f"yz{sfx}")
                for f in range(16):
                    par = f % 2
                    pb.op("ACT", lambda e, f=f, par=par: e.activation(
                        sqz[:, par * 128:(par + 1) * 128], yz[:, f * 128:(f + 1) * 128],
                        AF.Square),
                        reads=["yz"], writes=[("szq", par)], tag=f"yq{sfx}_{f}")
                    pb.op("PE", lambda e, f=f, par=par: e.matmul(
                        dps[0][0:1, 0:128], onesb3[:, 0:1],
                        sqz[:, par * 128:(par + 1) * 128], start=(f == 0), stop=(f == 15)),
                        reads=[("szq", par), "onesb3"], writes=[PSK(0)], tag=f"yqm{sfx}_{f}")
                pb.op("DVE", lambda e: e.tensor_scalar_mul(rws3[0:1, 0:128], dps[0][0:1, 0:128],
                                                           1.0 / 2048.0),
                      reads=[PSK(0)], writes=["r3m"], tag=f"r3m{sfx}")
                epst3, epsk3 = cx.cc(1e-5, 1)
                pb.op("ACT", lambda e: e.activation(rws3[0:1, 128:256], rws3[0:1, 0:128],
                                                    AF.Sqrt, bias=epst3[0:1, 0:1]),
                      reads=["r3m", epsk3], writes=["r3s"], tag=f"r3s{sfx}")
                pb.op("DVE", lambda e: e.reciprocal(rws3[0:1, 0:128], rws3[0:1, 128:256]),
                      reads=["r3s"], writes=["r3m"], tag=f"r3r{sfx}")
                pb.op("PE", lambda e: e.matmul(dps[0][:, 0:128], ln.ones1f[0:1, :],
                                               rws3[0:1, 0:128], start=True, stop=True),
                      reads=["r3m", "ln_ones1f"], writes=[PSK(0)], tag=f"r3b{sfx}")
                pb.op("DVE", lambda e: e.tensor_copy(rbz[:, :], dps[0][:, 0:128]),
                      reads=[PSK(0)], writes=["rbz"], tag=f"r3be{sfx}")
                for f in range(16):
                    pb.op("DVE", lambda e, f=f: e.scalar_tensor_tensor(
                        ybuf[:, f * 128:(f + 1) * 128], yz[:, f * 128:(f + 1) * 128],
                        ccol(32 + f), rbz[:, :], OP.mult, OP.mult),
                        reads=["yz", "rbz"] + CST, writes=[("ybuf", 2 * f), ("ybuf", 2 * f + 1)],
                        tag=f"yn{sfx}_{f}")
                pb.op("SP", lambda e, c=c: e.dma_start(out=yn_scr[c], in_=ybuf[:, :]),
                      reads=[("ybuf", h_) for h_ in range(NH)], writes=[("yns", c)],
                      dma=True, chain="ynst", tag=f"yns{sfx}")

        for c in range(NCH):
            sschunk(c, pass2=False)
        pb.op("DVE", lambda e: e.tensor_copy(STst[:, :], ST[:, :]),
              reads=["ST"], writes=["STst"], tag="stfin")
        pb.op("POOL", lambda e: e.dma_start(out=cc_s_in, in_=STst[:, :]),
              reads=["STst"], writes=["ccsi"], dma=True, chain="ccs", tag="ccsst")
        pb.op("POOL", lambda e: e.collective_compute(
            "AllGather", OP.bypass, replica_groups=PAIRS,
            ins=[cc_s_in.opt()], outs=[cc_s_out.opt()]),
            reads=["ccsi"], writes=["ccso"], coll=True, chain="ccsg", tag="ccsg")
        pb.op("POOL", lambda e: e.dma_start(out=STin[:, :], in_=cc_s_out[0:64, :]),
              reads=["ccso"], writes=["STinr"], dma=True, chain="ccsl", tag="ccsl")
        pb.op("DVE", lambda e: e.tensor_scalar_mul(STin[:, :], STin[:, :], cSEL[0:64, :]),
              reads=["STinr"] + CST, writes=["STin"], tag="stinsel")
        for c in range(NCH):
            sschunk(c, pass2=True)
        pb.full_barrier("end2")
    esB.__exit__(None, None, None)

    es3 = ExitStack()
    with es3:
        def sb3(nm, sh, dt=F32):
            return es3.enter_context(nc.sbuf_tensor(nm, sh, dt, allow_name_mangling=True))
        dps67 = [es3.enter_context(nc.psum_tensor(f"dpsX{i}", [128, 512], F32, allow_name_mangling=True)) for i in range(2)]
        dps8 = dps + dps67
        xbf2 = sb3("xbf2", [128, DCH * T], BF16)
        s_tile = sb3("s_tile", [128, DCH * 512], F32)
        xmidb = sb3("xmidb", [128, DCH * 512], F32)
        wob = [sb3(f"wob{i}", [128, 16 * 128], BF16) for i in range(2)]
        sq3 = sb3("sq3", [128, 2 * 512], BF16)
        rws4 = sb3("rws4", [1, 2 * 512], F32)
        onesb4 = sb3("onesb4", [128, 1], BF16)
        pb.op("DVE", lambda e: e.memset(onesb4[:, :], 1.0), writes=["onesb4"], tag="o4b")
        epst, epsk = cx.cc(1e-5, 1)
        XB2 = [("xbf2", c) for c in range(DCH)]

        ynts = sb3("ynts", [128, 16 * 512], BF16)
        for ts in range(4):
            for c in range(DCH):
                pb.op("SP", lambda e, c=c, ts=ts: e.dma_start(
                    out=xmidb[:, c * 512:(c + 1) * 512],
                    in_=xres[c * 128:(c + 1) * 128, ts * 512:(ts + 1) * 512]),
                    writes=[("xmidb", c)], dma=True, chain="xold", tag=f"xo3_{ts}_{c}")
            for k in range(16):
                src = yn_scr.rearrange("c p (k j) -> k p c j", j=128)[k][:, ts * 4:(ts + 1) * 4, :]
                pb.op("SP", lambda e, k=k, src=src: e.dma_start(
                    out=ynts[:, k * 512:(k + 1) * 512].rearrange("p (c j) -> p c j", j=128),
                    in_=src),
                    reads=[("yns", cc2) for cc2 in range(ts * 4, (ts + 1) * 4)],
                    writes=[("ynts", k)], dma=True, chain="ynld", tag=f"ynl{ts}_{k}")
            for oc in range(16):
                wb = wob[oc % 2]
                pb.op("POOL", lambda e, oc=oc, wb=wb: e.dma_start(out=wb[:, :], in_=w_op[oc]),
                      writes=[("wob", oc % 2)], dma=True, chain="wopl", tag=f"wop{ts}_{oc}")
                bank, bk = dps8[oc % 2], PSK(oc % 2)

                def mmo(e, wb=wb, bank=bank):
                    last = None
                    for k in range(16):
                        last = e.matmul(bank[:, :], wb[:, k * 128:(k + 1) * 128],
                                        ynts[:, k * 512:(k + 1) * 512],
                                        start=(k == 0), stop=(k == 15))
                    return last
                pb.op("PE", mmo, reads=[("wob", oc % 2)] + [("ynts", k_) for k_ in range(16)],
                      writes=[bk], tag=f"opm{ts}_{oc}")
                pb.op("DVE", lambda e, bank=bank, oc=oc: e.tensor_add(
                    xmidb[:, oc * 512:(oc + 1) * 512], bank[:, :], xmidb[:, oc * 512:(oc + 1) * 512]),
                    reads=[bk, ("xmidb", oc)], writes=[("xmidb", oc)], tag=f"xm{ts}_{oc}")
            for c in range(DCH):
                pb.op("SP", lambda e, c=c, ts=ts: e.dma_start(
                    out=xmid_scr[c * 128:(c + 1) * 128, ts * 512:(ts + 1) * 512],
                    in_=xmidb[:, c * 512:(c + 1) * 512]),
                    reads=[("xmidb", c)], writes=[("xms", ts, c)], dma=True, chain="xmst",
                    tag=f"xms{ts}_{c}")
            for c in range(DCH):
                par = c % 2
                pb.op("ACT", lambda e, c=c, par=par: e.activation(
                    sq3[:, par * 512:(par + 1) * 512], xmidb[:, c * 512:(c + 1) * 512], AF.Square),
                    reads=[("xmidb", c)], writes=[("sq3", par)], tag=f"q3_{ts}_{c}")
                pb.op("PE", lambda e, c=c, par=par: e.matmul(
                    dps8[2][0:1, :], onesb4[:, 0:1], sq3[:, par * 512:(par + 1) * 512],
                    start=(c == 0), stop=(c == DCH - 1)),
                    reads=[("sq3", par), "onesb4"], writes=[PSK(2)], tag=f"qm3_{ts}_{c}")
            pb.op("DVE", lambda e: e.tensor_scalar_mul(rws4[0:1, 0:512], dps8[2][0:1, :], 1.0 / D),
                  reads=[PSK(2)], writes=["r4m"], tag=f"r4m{ts}")
            pb.op("ACT", lambda e: e.activation(rws4[0:1, 512:1024], rws4[0:1, 0:512], AF.Sqrt,
                                                bias=epst[0:1, 0:1]),
                  reads=["r4m", epsk], writes=["r4s"], tag=f"r4s{ts}")
            pb.op("DVE", lambda e: e.reciprocal(rws4[0:1, 0:512], rws4[0:1, 512:1024]),
                  reads=["r4s"], writes=["r4m"], tag=f"r4r{ts}")
            pb.op("PE", lambda e: e.matmul(dps8[3][:, :], ln.ones1f[0:1, :], rws4[0:1, 0:512],
                                           start=True, stop=True),
                  reads=["r4m", "ln_ones1f"], writes=[PSK(3)], tag=f"r4b{ts}")
            for c in range(DCH):
                pb.op("DVE", lambda e, c=c, ts=ts: e.scalar_tensor_tensor(
                    xbf2[:, c * T + ts * 512: c * T + (ts + 1) * 512],
                    xmidb[:, c * 512:(c + 1) * 512], ccol(16 + c), dps8[3][:, :],
                    OP.mult, OP.mult),
                    reads=[("xmidb", c), PSK(3)] + CST, writes=[("xbf2", c)], tag=f"x2{ts}_{c}")

        for oc in range(16):
            dst = mwd_scr.rearrange("i p (o j) -> o i p j", j=128)[oc]
            srcap = mlp_wd[oc].rearrange("p (i j) -> i p j", j=128)
            pb.op("POOL", lambda e, dst=dst, srcap=srcap: e.dma_start(out=dst, in_=srcap),
                  writes=[("mwdbf", oc)], dma=True, chain="mwdc", tag=f"mwdc{oc}")
        pb.op("SP", lambda e: e.nop(), reads=[("mwdbf", oc) for oc in range(16)], tag="mwb")

        def pre_ts(ts):
            for c in range(DCH):
                pb.op("SP", lambda e, c=c, ts=ts: e.dma_start(
                    out=s_tile[:, c * 512:(c + 1) * 512],
                    in_=xres[c * 128:(c + 1) * 128, ts * 512:(ts + 1) * 512]),
                    writes=[("s", c)], dma=True, chain="xold", tag=f"xo4_{ts}_{c}")
                pb.op("SP", lambda e, c=c, ts=ts: e.dma_start(
                    out=xmidb[:, c * 512:(c + 1) * 512],
                    in_=xmid_scr[c * 128:(c + 1) * 128, ts * 512:(ts + 1) * 512]),
                    reads=[("xms", ts, c)],
                    writes=[("xmidb", c)], dma=True, chain="xmld", tag=f"xml{ts}_{c}")
                pb.op("DVE", lambda e, c=c: e.tensor_add(
                    xmidb[:, c * 512:(c + 1) * 512], xmidb[:, c * 512:(c + 1) * 512],
                    s_tile[:, c * 512:(c + 1) * 512]),
                    reads=[("xmidb", c), ("s", c)], writes=[("xmidb", c)], tag=f"cmb{ts}_{c}")

        def evac(ts, half):
            for o in range(8):
                c = half * 8 + o
                pb.op("DVE", lambda e, o=o, c=c: e.tensor_add(
                    s_tile[:, c * 512:(c + 1) * 512], dps8[o][:, :], xmidb[:, c * 512:(c + 1) * 512]),
                    reads=[PSK(o), ("xmidb", c)], writes=[("s", c)], tag=f"ms{ts}_{c}")

        def post_ts(ts):
            ln.emit(ts, s_tile, [("s", c) for c in range(DCH)], xout, okey, None, [],
                    tag=f"ln3_{ts}")

        saved_es = cx.es
        cx.es = es3
        emit_ffn(cx, "m", xbf2, XB2, lambda g: mlp_gu[g], mwd_scr, h_scr, True, None,
                 pre_ts, evac, post_ts, dps8)
        cx.es = saved_es
    esS.__exit__(None, None, None)


def build_stage3(nc, es):
    pb = PB(nc, es)
    cx = Ctx(nc, es, pb)
    A = declare_stage3(nc)
    xout = nc.dram_tensor("s3_xout", [D, T], F32, kind="ExternalOutput").ap()
    dps = [cx.ps(f"dps{i}", [128, 512], F32) for i in range(6)]
    PSK = lambda i: ("psum", i)
    ln = LNKit(cx, [(dps[2], PSK(2)), (dps[3], PSK(3)), (dps[4], PSK(4)), (dps[5], PSK(5))])
    emit_stage3(cx, A, dps, ln, xout, "s3xo")
    pb.op("POOL", lambda e: e.nop(),
          reads=[("s3xo", ts, c) for ts in range(4) for c in range(DCH)], tag="fin")
    pb.emit()
    return nc


def _s3_host_base(ln1_w, in_proj_w, conv_w, conv_b, A_log, D_param, dt_bias, mnorm_w,
                  out_proj_w, ln2_w, mlp_in_w, mlp_out_w):
    ipT = np.zeros((2048, 34 * 128), np.float32)
    ipT[:, :4256] = in_proj_w.T
    idx = np.arange(128)
    ccols = np.zeros((128, 128), np.float32)
    ccols[:, 0:16] = vecc(ln1_w).T
    ccols[:, 16:32] = vecc(ln2_w).T
    ccols[:, 32:48] = vecc(mnorm_w).T
    cb = conv_b.reshape(17, 128)
    ccols[:, 48:65] = cb.T
    ccols[0:32, 65] = A_log
    ccols[0:32, 66] = dt_bias
    convw = np.ascontiguousarray(conv_w.reshape(2176, 4).reshape(17, 128, 4).transpose(0, 2, 1).reshape(68, 128))
    TRILh = (idx[:, None] <= idx[None, :]).astype(np.float32)
    TRIUh = (idx[:, None] > idx[None, :]).astype(np.float32)
    C01h = (idx[None, :] >= idx[:, None]).astype(np.float32)
    IDFh = np.eye(128, dtype=np.float32)
    IDBh = np.eye(128).astype(BF)
    return {"s3_win": tile_lhsT(ipT), "s3_wop": tile_lhsT(np.ascontiguousarray(out_proj_w.T)),
            "s3_mgu": tile_lhsT(np.ascontiguousarray(mlp_in_w.T)),
            "s3_mwd": tile_lhsT(np.ascontiguousarray(mlp_out_w.T)),
            "s3_ccols": ccols, "s3_convw": convw,
            "s3_arow": np.asarray(A_log).reshape(1, 32).astype(np.float32),
            "s3_drow": np.asarray(D_param).reshape(1, 32).astype(np.float32),
            "s3_TRIL": TRILh, "s3_TRIU": TRIUh, "s3_C01": C01h, "s3_IDF": IDFh, "s3_IDB": IDBh}


def run_stage3(x, ln1_w, in_proj_w, conv_w, conv_b, A_log, D_param, dt_bias, mnorm_w,
               out_proj_w, ln2_w, mlp_in_w, mlp_out_w):
    nc = bass.Bass("TRN2", target_bir_lowering=False)
    es = ExitStack()
    with es:
        build_stage3(nc, es)
    base = _s3_host_base(ln1_w, in_proj_w, conv_w, conv_b, A_log, D_param, dt_bias,
                         mnorm_w, out_proj_w, ln2_w, mlp_in_w, mlp_out_w)
    xs = _shard_x_fm(x)
    ims = []
    for c in range(NCORES):
        im = dict(base)
        im["s3_xres"] = xs[c]
        im["s3_SEL"] = np.full((128, 1), float(c % 2), np.float32)
        ims.append(im)
    res = run_bass_kernel_spmd(nc, ims, core_ids=list(range(NCORES)))
    return _gather_fm(res, "s3_xout", x)



def _shard_x_fm(x):
    out = []
    for c in range(NCORES):
        b, h = c // 2, c % 2
        out.append(np.ascontiguousarray(x[b, h * T:(h + 1) * T, :].T))
    return out


def _gather_fm(res, key, x_like):
    out = np.empty_like(x_like)
    for c in range(NCORES):
        b, h = c // 2, c % 2
        out[b, h * T:(h + 1) * T, :] = res.results[c][key].T
    return out


def run_stage4(x, w_gate_up, w_down, debug=False):
    nc = bass.Bass("TRN2", target_bir_lowering=False)
    es = ExitStack()
    with es:
        build_stage4(nc, es)
    gu = tile_lhsT(np.ascontiguousarray(w_gate_up.T))
    wd = tile_lhsT(np.ascontiguousarray(w_down.T))
    xs = _shard_x_fm(x)
    ims = [{"s4_xres": xs[c], "s4_gu": gu, "s4_wd": wd} for c in range(NCORES)]
    res = run_bass_kernel_spmd(nc, ims, core_ids=list(range(NCORES)))
    return _gather_fm(res, "s4_xout", x)


def _quant_host(w_gate_up, w_proj):
    nc = bass.Bass("TRN2", target_bir_lowering=False)
    es = ExitStack()
    with es:
        build_quant(nc, es)
    guT = np.ascontiguousarray(w_gate_up.T)
    wpT = np.ascontiguousarray(w_proj.T)
    E = np.zeros((128, 2), BF)
    E[:64, 0] = 1
    E[64:, 1] = 1
    ET = np.zeros((2, 128), np.float32)
    ET[0, :64] = 1
    ET[1, 64:] = 1
    ims = []
    for c in range(NCORES):
        ims.append({"q_gu": np.ascontiguousarray(guT[:, c * 2048:(c + 1) * 2048]),
                    "q_wp": np.ascontiguousarray(wpT[:, c * 256:(c + 1) * 256]),
                    "q_E": E, "q_ET": ET})
    res = run_bass_kernel_spmd(nc, ims, core_ids=list(range(NCORES)))
    wq_guT = np.concatenate([res.results[c]["q_gu_out"] for c in range(NCORES)], axis=1)
    wq_pT = np.concatenate([res.results[c]["q_wp_out"] for c in range(NCORES)], axis=1)
    return wq_guT, wq_pT


def run_stage2(x, w_gate_up, w_proj):
    wq_guT, wq_pT = _quant_host(w_gate_up, w_proj)
    nc = bass.Bass("TRN2", target_bir_lowering=False)
    es = ExitStack()
    with es:
        build_stage2(nc, es)
    gu = tile_lhsT(wq_guT)
    wd = tile_lhsT(wq_pT)
    xs = _shard_x_fm(x)
    ims = [{"s2_xres": xs[c], "s2_gu": gu, "s2_wd": wd} for c in range(NCORES)]
    res = run_bass_kernel_spmd(nc, ims, core_ids=list(range(NCORES)))
    return _gather_fm(res, "s2_xout", x)


def _quant_E():
    E = np.zeros((128, 2), BF)
    E[:64, 0] = 1
    E[64:, 1] = 1
    ET = np.zeros((2, 128), np.float32)
    ET[0, :64] = 1
    ET[1, 64:] = 1
    return E, ET


def build_fused(nc, es):
    pb = PB(nc, es)
    cx = Ctx(nc, es, pb)
    x_in = nc.dram_tensor("x_in", [D, T], F32, kind="ExternalInput").ap()
    x1 = nc.dram_tensor("xr1", [D, T], F32, kind="Internal").ap()
    x2 = nc.dram_tensor("xr2", [D, T], F32, kind="Internal").ap()
    x3 = nc.dram_tensor("xr3", [D, T], F32, kind="Internal").ap()
    x_out = nc.dram_tensor("x_out", [D, T], F32, kind="ExternalOutput").ap()
    s4_gu = nc.dram_tensor("s4_gu", [128, 128, 16 * 128], F32, kind="ExternalInput").ap()
    s4_wd = nc.dram_tensor("s4_wd", [16, 128, 64 * 128], F32, kind="ExternalInput").ap()
    s4_h = nc.dram_tensor("s4_h", [64, 128, T], BF16, kind="Internal").ap()
    s4_wdbf = nc.dram_tensor("s4_wdbf", [64, 128, 16 * 128], BF16, kind="Internal").ap()
    s2_h = nc.dram_tensor("s2_h", [64, 128, T], BF16, kind="Internal").ap()
    s2_wdbf = nc.dram_tensor("s2_wdbf", [64, 128, 16 * 128], BF16, kind="Internal").ap()

    dps = [cx.ps(f"dps{i}", [128, 512], F32) for i in range(6)]
    PSK = lambda i: ("psum", i)
    ln = LNKit(cx, [(dps[2], PSK(2)), (dps[3], PSK(3)), (dps[4], PSK(4)), (dps[5], PSK(5))])
    cx.cc(1e-5, 1)

    Aq = declare_quant(nc, internal_out=True)
    A1 = declare_stage1(nc, xres=x_in)
    A3 = declare_stage3(nc, xres=x2)

    esQ = ExitStack()

    def emit_q():
        saved = cx.es
        cx.es = esQ
        emit_quant(cx, Aq, dps, gather=True)
        cx.es = saved

    emit_stage1(cx, A1, dps, ln, x1, "s1xo",
                pre_ops=emit_q, post_proj=lambda: esQ.__exit__(None, None, None))
    pb.full_barrier("b12")

    gath_gu, gath_wp = Aq["gath_gu"], Aq["gath_wp"]

    def s2_gu_fn(oc):
        r, l = divmod(oc, 16)
        return gath_gu[r * 2048:(r + 1) * 2048, :].rearrange(
            "(k p) (l j) -> l p k j", p=128, j=128)[l]

    def s2_wd_fn(oc):
        r = oc // 2
        return gath_wp[r * 8192:(r + 1) * 8192,
                       (oc % 2) * 128:(oc % 2) * 128 + 128].rearrange("(i p) j -> i p j", p=128)

    emit_postln_ffn(cx, "s2", x1, x2, "s2xo", s2_gu_fn, s2_wd_fn, s2_wdbf, s2_h,
                    dps, ln, cast_w=False, rms_eps=F32EPS)
    pb.full_barrier("b23")
    emit_stage3(cx, A3, dps, ln, x3, "s3xo")
    pb.full_barrier("b34")
    emit_postln_ffn(cx, "s4", x3, x_out, "s4xo", lambda g: s4_gu[g],
                    lambda oc: s4_wd[oc].rearrange("p (i j) -> i p j", j=128),
                    s4_wdbf, s4_h, dps, ln, cast_w=True, rms_eps=None)
    pb.op("POOL", lambda e: e.nop(),
          reads=[("s4xo", ts, c) for ts in range(4) for c in range(DCH)], tag="fin")
    pb.emit()
    return nc


def run_fused(inputs):
    nc = bass.Bass("TRN2", target_bir_lowering=False)
    es = ExitStack()
    with es:
        build_fused(nc, es)
    x = np.asarray(inputs["L0_attn_x"], np.float32)
    base = {}
    base.update(_s1_host_base(inputs["L0_attn_wq"], inputs["L0_attn_bq"], inputs["L0_attn_wk"],
                              inputs["L0_attn_bk"], inputs["L0_attn_wv"], inputs["L0_attn_bv"],
                              inputs["L0_attn_wg"], inputs["L0_attn_bg"], inputs["L0_attn_wo"],
                              inputs["L0_attn_bo"], inputs["L0_attn_gn_w"],
                              inputs["L0_attn_gn_b"], inputs["L0_attn_gamma"]))
    base.update(_s3_host_base(inputs["L1_attn_ln1_w"], inputs["L1_attn_in_proj_w"],
                              inputs["L1_attn_conv_w"], inputs["L1_attn_conv_b"],
                              inputs["L1_attn_A_log"], inputs["L1_attn_D_param"],
                              inputs["L1_attn_dt_bias"], inputs["L1_attn_mnorm_w"],
                              inputs["L1_attn_out_proj_w"], inputs["L1_attn_ln2_w"],
                              inputs["L1_attn_mlp_in_w"], inputs["L1_attn_mlp_out_w"]))
    base["s4_gu"] = tile_lhsT(np.ascontiguousarray(np.asarray(inputs["L1_ffn_w_gate_up"]).T))
    base["s4_wd"] = tile_lhsT(np.ascontiguousarray(np.asarray(inputs["L1_ffn_w_down"]).T))
    E, ET = _quant_E()
    base["q_E"] = E
    base["q_ET"] = ET
    guT = np.ascontiguousarray(np.asarray(inputs["L0_ffn_w_gate_up"]).T)
    wpT = np.ascontiguousarray(np.asarray(inputs["L0_ffn_w_proj"]).T)
    xs = _shard_x_fm(x)
    ims = []
    for c in range(NCORES):
        im = dict(base)
        im["x_in"] = xs[c]
        im["q_gu"] = np.ascontiguousarray(guT[:, c * 2048:(c + 1) * 2048])
        im["q_wp"] = np.ascontiguousarray(wpT[:, c * 256:(c + 1) * 256])
        sel = np.full((128, 1), float(c % 2), np.float32)
        im["s1_SEL"] = sel
        im["s3_SEL"] = sel
        ims.append(im)
    res = run_bass_kernel_spmd(nc, ims, core_ids=list(range(NCORES)))
    return _gather_fm(res, "x_out", x)


def kernel(**inputs):
    return run_fused(inputs)


def kernel_multilaunch(**inputs):
    x = np.asarray(inputs["L0_attn_x"], np.float32)
    x = run_stage1(x, inputs["L0_attn_wq"], inputs["L0_attn_bq"], inputs["L0_attn_wk"],
                   inputs["L0_attn_bk"], inputs["L0_attn_wv"], inputs["L0_attn_bv"],
                   inputs["L0_attn_wg"], inputs["L0_attn_bg"], inputs["L0_attn_wo"],
                   inputs["L0_attn_bo"], inputs["L0_attn_gn_w"], inputs["L0_attn_gn_b"],
                   inputs["L0_attn_gamma"])
    x = run_stage2(x, inputs["L0_ffn_w_gate_up"], inputs["L0_ffn_w_proj"])
    x = run_stage3(x, inputs["L1_attn_ln1_w"], inputs["L1_attn_in_proj_w"],
                   inputs["L1_attn_conv_w"], inputs["L1_attn_conv_b"],
                   inputs["L1_attn_A_log"], inputs["L1_attn_D_param"],
                   inputs["L1_attn_dt_bias"], inputs["L1_attn_mnorm_w"],
                   inputs["L1_attn_out_proj_w"], inputs["L1_attn_ln2_w"],
                   inputs["L1_attn_mlp_in_w"], inputs["L1_attn_mlp_out_w"])
    x = run_stage4(x, inputs["L1_ffn_w_gate_up"], inputs["L1_ffn_w_down"])
    return x


# version 03 (final)
# speedup vs baseline: 1.3030x; 1.3030x over previous
import numpy as np
import ml_dtypes
from contextlib import ExitStack
import concourse.bass as bass
import concourse.mybir as mybir
from concourse.bass_utils import run_bass_kernel_spmd

F32 = mybir.dt.float32
BF16 = mybir.dt.bfloat16
I32 = mybir.dt.int32
AF = mybir.ActivationFunctionType
OP = mybir.AluOpType
AX = mybir.AxisListType
BF = ml_dtypes.bfloat16

NCORES = 8
T = 2048
D = 2048
DCH = D // 128
PAIRS = [[0, 1], [2, 3], [4, 5], [6, 7]]


def tile_lhsT(wT):
    K, O = wT.shape
    t = wT.reshape(K // 128, 128, O // 128, 128).transpose(2, 1, 0, 3)
    return np.ascontiguousarray(t.reshape(O // 128, 128, (K // 128) * 128))


def vecc(v):
    return np.ascontiguousarray(np.asarray(v, np.float32).reshape(-1, 128))



ENG2BLOCK = {"PE": "tensor", "ACT": "scalar", "DVE": "vector", "SP": "sync", "POOL": "gpsimd"}


class Step:
    __slots__ = ("eng", "fn", "waits", "sem", "inc", "tag")

    def __init__(self, eng, fn, waits, sem, inc, tag):
        self.eng, self.fn, self.waits, self.sem, self.inc, self.tag = eng, fn, waits, sem, inc, tag


class Chain:
    __slots__ = ("sem", "count", "total", "pending")

    def __init__(self, sem):
        self.sem = sem
        self.count = 0
        self.total = 0
        self.pending = {}


class PB:

    SEM_CAP = 30000

    def __init__(self, nc, es):
        self.nc = nc
        self.es = es
        self.steps = []
        self.cur_sem = {}
        self.cnt = {}
        self.chains = {}
        self.last_w = {}
        self.readers = {}
        self.readers_ch = {}
        self.hwm = {}
        self.n_sem = 0
        for e in ENG2BLOCK:
            self._new_sem(e)

    def _new_sem(self, eng):
        h = self.es.enter_context(self.nc.semaphore(f"s_{eng}_{self.n_sem}"))
        self.n_sem += 1
        self.cur_sem[eng] = h
        self.cnt[eng] = 0

    def chain(self, name):
        if name not in self.chains:
            h = self.es.enter_context(self.nc.semaphore(f"c_{name}_{self.n_sem}"))
            self.n_sem += 1
            self.chains[name] = Chain(h)
        return self.chains[name]

    def _resolve(self, tok, w):
        if tok is None:
            return
        if tok[0] == "c":
            _, s, v = tok
            if w.get(s, -1) < v:
                w[s] = v
        else:
            ch = tok[1]
            s, v = ch.sem, ch.total
            if w.get(s, -1) < v:
                w[s] = v

    def op(self, eng, fn, reads=(), writes=(), dma=False, coll=False, chain=None, tag=""):
        w = {}
        for r in reads:
            self._resolve(self.last_w.get(r), w)
        for wr in writes:
            self._resolve(self.last_w.get(wr), w)
            for s, v in self.readers.get(wr, {}).items():
                if w.get(s, -1) < v:
                    w[s] = v
            for rch in self.readers_ch.get(wr, ()):
                v = rch.total
                if w.get(rch.sem, -1) < v:
                    w[rch.sem] = v
        if dma or coll:
            assert chain is not None, f"dma step {tag} needs a chain"
            ch = self.chain(chain)
            for s, v in ch.pending.items():
                if w.get(s, -1) < v:
                    w[s] = v
            ch.pending = {}
            ch.count += 1
            inc = 1 if coll else 16
            ch.total += inc
            sem = ch.sem
            tok = ("d", ch)
            rd_tok = (ch.sem, ch.total)
            dma = True
        else:
            inc = 1
            if self.cnt[eng] + inc > self.SEM_CAP:
                self._new_sem(eng)
            sem = self.cur_sem[eng]
            self.cnt[eng] += inc
            tok = ("c", sem, self.cnt[eng])
            rd_tok = (sem, self.cnt[eng])
        waits = []
        for s, v in w.items():
            k = (eng, id(s))
            if self.hwm.get(k, 0) >= v:
                continue
            self.hwm[k] = v
            waits.append((s, v))
        for r in reads:
            if dma:
                self.readers_ch.setdefault(r, set()).add(ch)
            else:
                d = self.readers.setdefault(r, {})
                if d.get(rd_tok[0], -1) < rd_tok[1]:
                    d[rd_tok[0]] = rd_tok[1]
            lw = self.last_w.get(r)
            if lw is not None and lw[0] == "d":
                p = lw[1].pending
                if p.get(rd_tok[0], -1) < rd_tok[1]:
                    p[rd_tok[0]] = rd_tok[1]
        for wr in writes:
            self.last_w[wr] = tok
            self.readers[wr] = {}
            self.readers_ch[wr] = set()
        self.steps.append(Step(eng, fn, waits, sem, inc, tag))
        return tok

    def full_barrier(self, tag="bar"):
        waits = []
        for e2, s in self.cur_sem.items():
            if self.cnt[e2] > 0:
                waits.append((s, self.cnt[e2]))
        for ch in self.chains.values():
            if ch.total > 0:
                waits.append((ch.sem, ch.total))
        for eng in ENG2BLOCK:
            mywaits = []
            for s, v in waits:
                k = (eng, id(s))
                if self.hwm.get(k, 0) >= v:
                    continue
                self.hwm[k] = v
                mywaits.append((s, v))
            if self.cnt[eng] + 1 > self.SEM_CAP:
                self._new_sem(eng)
            sem = self.cur_sem[eng]
            self.cnt[eng] += 1
            self.steps.append(Step(eng, lambda e: e.nop(), mywaits, sem, 1, f"{tag}_{eng}"))

    def emit(self):
        with self.nc.Block() as block:
            for eng, bname in ENG2BLOCK.items():
                def mk(engname):
                    def body(e):
                        for st in self.steps:
                            if st.eng != engname:
                                continue
                            for s, v in st.waits:
                                e.wait_ge(s, v)
                            ins = st.fn(e)
                            assert ins is not None, f"step {st.tag} returned None"
                            ins.then_inc(st.sem, st.inc)
                    return body
                getattr(block, bname)(mk(eng))


class Ctx:
    def __init__(self, nc, es, pb):
        self.nc = nc
        self.es = es
        self.pb = pb
        self._cc = {}

    def sb(self, nm, sh, dt=F32):
        return self.es.enter_context(self.nc.sbuf_tensor(nm, sh, dt, allow_name_mangling=True))

    def ps(self, nm, sh, dt=F32):
        return self.es.enter_context(self.nc.psum_tensor(nm, sh, dt, allow_name_mangling=True))

    def cc(self, val, parts=128):
        key = ("cc", float(val), parts)
        if key not in self._cc:
            t = self.sb(f"cc_{len(self._cc)}", [parts, 1], F32)
            self.pb.op("DVE", lambda e, t=t, v=float(val): e.memset(t[:, :], v),
                       writes=[key], tag="cc")
            self._cc[key] = t
        return self._cc[key], key



class LNKit:

    def __init__(self, cx, banks4, tw=512):
        self.cx = cx
        self.tw = tw
        self.b = banks4
        self.stg = cx.sb("ln_stg", [128, 2 * tw], BF16)
        self.sqg = cx.sb("ln_sqg", [128, 2 * tw], BF16)
        self.rows = cx.sb("ln_rows", [1, 4 * tw], F32)
        self.onesb = cx.sb("ln_onesb", [128, 1], BF16)
        self.ones1f = cx.sb("ln_ones1f", [1, 128], F32)
        cx.pb.op("DVE", lambda e: e.memset(self.onesb[:, :], 1.0), writes=["ln_onesb"], tag="ln_ones")
        cx.pb.op("DVE", lambda e: e.memset(self.ones1f[:, :], 1.0), writes=["ln_ones1f"], tag="ln_ones1")

    def emit(self, ts, s_tile, s_keys, xres_new, xres_keyf, xbf, xbf_keys, eps=1e-5, tag="ln"):
        cx, pb, tw = self.cx, self.cx.pb, self.tw
        (bs, ks), (bq, kq), (bm, km), (br, kr) = self.b
        for c in range(DCH):
            par = c % 2
            sl = slice(par * tw, (par + 1) * tw)
            pb.op("ACT", lambda e, c=c, sl=sl: e.activation(
                self.stg[:, sl], s_tile[:, c * tw:(c + 1) * tw], AF.Copy),
                reads=[s_keys[c]], writes=[("ln_stg", par)], tag=f"{tag}_cp{c}")
            pb.op("PE", lambda e, c=c, sl=sl: e.matmul(
                bs[0:1, :], self.onesb[:, 0:1], self.stg[:, sl],
                start=(c == 0), stop=(c == DCH - 1)),
                reads=[("ln_stg", par), "ln_onesb"], writes=[ks], tag=f"{tag}_ms{c}")
            pb.op("ACT", lambda e, c=c, sl=sl: e.activation(
                self.sqg[:, sl], s_tile[:, c * tw:(c + 1) * tw], AF.Square),
                reads=[s_keys[c]], writes=[("ln_sqg", par)], tag=f"{tag}_sq{c}")
            pb.op("PE", lambda e, c=c, sl=sl: e.matmul(
                bq[0:1, :], self.onesb[:, 0:1], self.sqg[:, sl],
                start=(c == 0), stop=(c == DCH - 1)),
                reads=[("ln_sqg", par), "ln_onesb"], writes=[kq], tag=f"{tag}_mq{c}")
        r = self.rows
        pb.op("DVE", lambda e: e.tensor_scalar_mul(r[0:1, 0:tw], bs[0:1, :], 1.0 / D),
              reads=[ks], writes=["ln_r0"], tag=f"{tag}_mean")
        pb.op("DVE", lambda e: e.tensor_scalar_mul(r[0:1, tw:2 * tw], bq[0:1, :], 1.0 / D),
              reads=[kq], writes=["ln_r1"], tag=f"{tag}_msq")
        pb.op("DVE", lambda e: e.tensor_mul(r[0:1, 2 * tw:3 * tw], r[0:1, 0:tw], r[0:1, 0:tw]),
              reads=["ln_r0"], writes=["ln_r2"], tag=f"{tag}_m2")
        pb.op("DVE", lambda e: e.tensor_sub(r[0:1, tw:2 * tw], r[0:1, tw:2 * tw], r[0:1, 2 * tw:3 * tw]),
              reads=["ln_r1", "ln_r2"], writes=["ln_r1"], tag=f"{tag}_var")
        epst, epsk = cx.cc(eps, 1)
        pb.op("ACT", lambda e: e.activation(r[0:1, 2 * tw:3 * tw], r[0:1, tw:2 * tw], AF.Sqrt,
                                            bias=epst[0:1, 0:1]),
              reads=["ln_r1", epsk], writes=["ln_r2"], tag=f"{tag}_sd")
        pb.op("DVE", lambda e: e.reciprocal(r[0:1, 3 * tw:4 * tw], r[0:1, 2 * tw:3 * tw]),
              reads=["ln_r2"], writes=["ln_r3"], tag=f"{tag}_rstd")
        pb.op("PE", lambda e: e.matmul(bm[:, :], self.ones1f[0:1, :], r[0:1, 0:tw],
                                       start=True, stop=True),
              reads=["ln_r0", "ln_ones1f"], writes=[km], tag=f"{tag}_bcm")
        pb.op("PE", lambda e: e.matmul(br[:, :], self.ones1f[0:1, :], r[0:1, 3 * tw:4 * tw],
                                       start=True, stop=True),
              reads=["ln_r3", "ln_ones1f"], writes=[kr], tag=f"{tag}_bcr")
        s3 = s_tile[:, :].rearrange("p (c w) -> p c w", w=tw)
        mb = bm[:, :].unsqueeze(1).broadcast_to([128, DCH, tw])
        rb = br[:, :].unsqueeze(1).broadcast_to([128, DCH, tw])
        pb.op("DVE", lambda e: e.tensor_tensor(s3, s3, mb, OP.subtract),
              reads=list(s_keys) + [km], writes=list(s_keys), tag=f"{tag}_sub")
        pb.op("DVE", lambda e: e.tensor_tensor(s3, s3, rb, OP.mult),
              reads=list(s_keys) + [kr], writes=list(s_keys), tag=f"{tag}_mul")
        for c in range(DCH):
            pb.op("SP", lambda e, c=c: e.dma_start(
                out=xres_new[c * 128:(c + 1) * 128, ts * tw:(ts + 1) * tw],
                in_=s_tile[:, c * tw:(c + 1) * tw]),
                reads=list(s_keys), writes=[(xres_keyf, ts, c)], dma=True, chain="lnst", tag=f"{tag}_st{c}")
        if xbf is not None:
            xbf3 = xbf[:, :].rearrange("p (c t) -> p c t", t=T)[:, :, ts * tw:(ts + 1) * tw]
            pb.op("DVE", lambda e: e.tensor_copy(xbf3, s3),
                  reads=list(s_keys), writes=list(xbf_keys), tag=f"{tag}_bf")



F32EPS = 1.1920929e-07


def emit_ffn(cx, nm, xbf, XBF, gu_fn, wd_scr, hscr, cast_w, rms_eps, pre_ts, evac, post_ts,
             dps, lnrows=None):
    pb = cx.pb
    wbufA = [cx.sb(f"{nm}_wA{i}", [128, 16 * 128], BF16) for i in range(4)]
    hstg = cx.sb(f"{nm}_hstg", [128, T], BF16)
    sg = cx.sb(f"{nm}_sg", [128, 512], F32)
    if rms_eps is not None:
        hsq = cx.sb(f"{nm}_hsq", [128, 512], BF16)
        rstdbc = cx.sb(f"{nm}_rstdbc", [128, 4 * 512], F32)
        rows = cx.sb(f"{nm}_rows", [1, 2 * 512], F32)
        onesb = cx.sb(f"{nm}_onesb", [128, 1], BF16)
        ones1f = cx.sb(f"{nm}_ones1f", [1, 128], F32)
        pb.op("DVE", lambda e: e.memset(onesb[:, :], 1.0), writes=[(nm, "onesb")], tag=f"{nm}ones")
        pb.op("DVE", lambda e: e.memset(ones1f[:, :], 1.0), writes=[(nm, "ones1f")], tag=f"{nm}ones1")

    ldq = "POOL" if cast_w else "SP"
    for g in range(64):
        par = g % 2
        wbg, wbu = wbufA[2 * par], wbufA[2 * par + 1]
        pb.op(ldq, lambda e, g=g, wbg=wbg: e.dma_start(out=wbg[:, :], in_=gu_fn(g)),
              writes=[(nm, "wA", 2 * par)], dma=True, chain=f"{nm}_wA{2 * par}", tag=f"{nm}wg{g}")
        pb.op(ldq, lambda e, g=g, wbu=wbu: e.dma_start(out=wbu[:, :], in_=gu_fn(g + 64)),
              writes=[(nm, "wA", 2 * par + 1)], dma=True, chain=f"{nm}_wA{2 * par + 1}",
              tag=f"{nm}wu{g}")
        for ts in range(4):
            pg, pgk = dps[ts % 2], ("psum", ts % 2)
            pu, puk = dps[2 + ts % 2], ("psum", 2 + ts % 2)

            def mmA(e, wb, pp, ts=ts):
                last = None
                for k in range(16):
                    last = e.matmul(pp[:, :], wb[:, k * 128:(k + 1) * 128],
                                    xbf[:, k * T + ts * 512: k * T + (ts + 1) * 512],
                                    start=(k == 0), stop=(k == 15))
                return last
            pb.op("PE", lambda e, pg=pg, ts=ts, wbg=wbg: mmA(e, wbg, pg, ts),
                  reads=[(nm, "wA", 2 * par)] + XBF, writes=[pgk], tag=f"{nm}gmm{g}_{ts}")
            pb.op("PE", lambda e, pu=pu, ts=ts, wbu=wbu: mmA(e, wbu, pu, ts),
                  reads=[(nm, "wA", 2 * par + 1)] + XBF, writes=[puk], tag=f"{nm}umm{g}_{ts}")
            pb.op("ACT", lambda e, pg=pg: e.activation(sg[:, :], pg[:, :], AF.Silu),
                  reads=[pgk], writes=[(nm, "sg")], tag=f"{nm}silu{g}_{ts}")
            pb.op("DVE", lambda e, pu=pu, ts=ts: e.tensor_mul(
                hstg[:, ts * 512:(ts + 1) * 512], sg[:, :], pu[:, :]),
                reads=[(nm, "sg"), puk], writes=[(nm, "hstg", ts)], tag=f"{nm}hmul{g}_{ts}")
            if rms_eps is not None:
                pb.op("ACT", lambda e, ts=ts: e.activation(
                    hsq[:, :], hstg[:, ts * 512:(ts + 1) * 512], AF.Square),
                    reads=[(nm, "hstg", ts)], writes=[(nm, "hsq")], tag=f"{nm}hsq{g}_{ts}")
                pb.op("PE", lambda e, ts=ts, g=g: e.matmul(
                    dps[4 + ts][0:1, :], onesb[:, 0:1], hsq[:, :],
                    start=(g == 0), stop=(g == 63)),
                    reads=[(nm, "hsq"), (nm, "onesb")], writes=[("psum", 4 + ts)],
                    tag=f"{nm}hstat{g}_{ts}")
        pb.op("SP", lambda e, g=g: e.dma_start(out=hscr[g], in_=hstg[:, :]),
              reads=[(nm, "hstg", t_) for t_ in range(4)],
              writes=[(nm, "hscr", g)], dma=True, chain=f"{nm}_hst", tag=f"{nm}hst{g}")

    if rms_eps is not None:
        epst, epsk = cx.cc(rms_eps, 1)
        for ts in range(4):
            pb.op("DVE", lambda e, ts=ts: e.tensor_scalar_mul(
                rows[0:1, 0:512], dps[4 + ts][0:1, :], 1.0 / 8192.0),
                reads=[("psum", 4 + ts)], writes=[(nm, "rows0")], tag=f"{nm}rmean{ts}")
            pb.op("ACT", lambda e: e.activation(rows[0:1, 512:1024], rows[0:1, 0:512], AF.Sqrt,
                                                bias=epst[0:1, 0:1]),
                  reads=[(nm, "rows0"), epsk], writes=[(nm, "rows1")], tag=f"{nm}rsd{ts}")
            pb.op("DVE", lambda e: e.reciprocal(rows[0:1, 0:512], rows[0:1, 512:1024]),
                  reads=[(nm, "rows1")], writes=[(nm, "rows0")], tag=f"{nm}rrecip{ts}")
            pb.op("PE", lambda e, ts=ts: e.matmul(
                dps[4][:, :], ones1f[0:1, :], rows[0:1, 0:512],
                start=True, stop=True),
                reads=[(nm, "rows0"), (nm, "ones1f")], writes=[("psum", 4)], tag=f"{nm}rbc{ts}")
            pb.op("DVE", lambda e, ts=ts: e.tensor_copy(
                rstdbc[:, ts * 512:(ts + 1) * 512], dps[4][:, :]),
                reads=[("psum", 4)], writes=[(nm, "rstdbc")], tag=f"{nm}rbce{ts}")

    GB = 4
    hbuf = [cx.sb(f"{nm}_hbuf{b}", [128, GB * 512], BF16) for b in range(2)]
    wdb = [cx.sb(f"{nm}_wdb{b}", [128, GB * 8 * 128], BF16) for b in range(2)]
    for ts in range(4):
        pre_ts(ts)
        for half in range(2):
            for ig in range(64 // GB):
                par = ig % 2
                hb, wb = hbuf[par], wdb[par]
                hsrc = hscr[ig * GB:(ig + 1) * GB, :, ts * 512:(ts + 1) * 512].rearrange(
                    "i p t -> p i t")
                pb.op("SP", lambda e, hb=hb, hsrc=hsrc: e.dma_start(
                    out=hb[:, :].rearrange("p (i t) -> p i t", t=512), in_=hsrc),
                    reads=[(nm, "hscr", i_) for i_ in range(ig * GB, (ig + 1) * GB)],
                    writes=[(nm, "hbuf", par)], dma=True,
                    chain=f"{nm}_hld{par}", tag=f"{nm}hld{ts}_{half}_{ig}")
                rhs, rhs_key = hb, (nm, "hbuf", par)
                if rms_eps is not None:
                    rb4 = rstdbc[:, ts * 512:(ts + 1) * 512].unsqueeze(1).broadcast_to(
                        [128, GB, 512])
                    pb.op("DVE", lambda e, hb=hb, rb4=rb4: e.tensor_tensor(
                        hb[:, :].rearrange("p (i t) -> p i t", t=512),
                        hb[:, :].rearrange("p (i t) -> p i t", t=512), rb4, OP.mult),
                        reads=[(nm, "hbuf", par), (nm, "rstdbc")], writes=[(nm, "hbuf", par)],
                        tag=f"{nm}hn{ts}_{half}_{ig}")
                wsrc = wd_scr[ig * GB:(ig + 1) * GB, :,
                              half * 8 * 128:(half + 1) * 8 * 128].rearrange("i p o -> p i o")
                pb.op("POOL", lambda e, wb=wb, wsrc=wsrc: e.dma_start(
                    out=wb[:, :].rearrange("p (i o) -> p i o", o=8 * 128), in_=wsrc),
                    writes=[(nm, "wdb", par)], dma=True, chain=f"{nm}_wdl{par}",
                    tag=f"{nm}wdl{ts}_{half}_{ig}")

                def mmdown(e, ig=ig, rhs=rhs, wb=wb):
                    last = None
                    for il in range(GB):
                        i = ig * GB + il
                        for o in range(8):
                            last = e.matmul(
                                dps[o][:, :],
                                wb[:, (il * 8 + o) * 128:(il * 8 + o + 1) * 128],
                                rhs[:, il * 512:(il + 1) * 512],
                                start=(i == 0), stop=(i == 63))
                    return last
                pb.op("PE", mmdown, reads=[rhs_key, (nm, "wdb", par)],
                      writes=[("psum", o) for o in range(8)], tag=f"{nm}dmm{ts}_{half}_{ig}")
            evac(ts, half)
        post_ts(ts)


def emit_postln_ffn(cx, pref, xres, xout, okey, gu_fn, wd_src_fn, wdbf, hscr,
                    dps6, ln, cast_w, rms_eps):
    pb, nc = cx.pb, cx.nc
    esF = ExitStack()
    with esF:
        saved_es = cx.es
        cx.es = esF
        dpsX = [esF.enter_context(nc.psum_tensor(f"{pref}_px{i}", [128, 512], F32, allow_name_mangling=True))
                for i in range(2)]
        dps = dps6 + dpsX
        xbf = cx.sb(f"{pref}_xbf", [128, DCH * T], BF16)
        s_tile = cx.sb(f"{pref}_s_tile", [128, DCH * 512], F32)
        xold = cx.sb(f"{pref}_xold", [128, DCH * 512], F32)

        XBF = [(pref, "xbf", c) for c in range(DCH)]
        for c in range(DCH):
            pb.op("POOL", lambda e, c=c: e.dma_start(
                out=xbf[:, c * T:(c + 1) * T], in_=xres[c * 128:(c + 1) * 128, :]),
                writes=[(pref, "xbf", c)], dma=True, chain=f"{pref}xld", tag=f"{pref}xld{c}")

        for oc in range(16):
            dst = wdbf.rearrange("i p (o j) -> o i p j", j=128)[oc]
            srcap = wd_src_fn(oc)
            pb.op("POOL", lambda e, dst=dst, srcap=srcap: e.dma_start(out=dst, in_=srcap),
                  writes=[(pref, "wdbf", oc)], dma=True, chain=f"{pref}wdc", tag=f"{pref}wdc{oc}")
        pb.op("SP", lambda e: e.nop(), reads=[(pref, "wdbf", oc) for oc in range(16)],
              tag=f"{pref}wdbf_barrier")

        def pre_ts(ts):
            for c in range(DCH):
                pb.op("SP", lambda e, c=c, ts=ts: e.dma_start(
                    out=xold[:, c * 512:(c + 1) * 512],
                    in_=xres[c * 128:(c + 1) * 128, ts * 512:(ts + 1) * 512]),
                    writes=[(pref, "xold", c)], dma=True, chain=f"{pref}xold",
                    tag=f"{pref}xold{ts}_{c}")

        def evac(ts, half):
            for o in range(8):
                c = half * 8 + o
                pb.op("DVE", lambda e, o=o, c=c: e.tensor_add(
                    s_tile[:, c * 512:(c + 1) * 512], dps[o][:, :], xold[:, c * 512:(c + 1) * 512]),
                    reads=[("psum", o), (pref, "xold", c)], writes=[(pref, "s", c)],
                    tag=f"{pref}sadd{ts}_{c}")

        def post_ts(ts):
            ln.emit(ts, s_tile, [(pref, "s", c) for c in range(DCH)], xout, okey, xbf, XBF,
                    tag=f"{pref}lnB_{ts}")

        emit_ffn(cx, pref, xbf, XBF, gu_fn, wdbf, hscr, cast_w, rms_eps, pre_ts, evac,
                 post_ts, dps)
        cx.es = saved_es


def _ffn_postln_standalone(nc, es, pref, cast_w, rms_eps):
    pb = PB(nc, es)
    cx = Ctx(nc, es, pb)
    wdt = F32 if cast_w else BF16
    xres = nc.dram_tensor(f"{pref}_xres", [D, T], F32, kind="ExternalInput").ap()
    gu_t = nc.dram_tensor(f"{pref}_gu", [128, 128, 16 * 128], wdt, kind="ExternalInput").ap()
    wd_t = nc.dram_tensor(f"{pref}_wd", [16, 128, 64 * 128], wdt, kind="ExternalInput").ap()
    xout = nc.dram_tensor(f"{pref}_xout", [D, T], F32, kind="ExternalOutput").ap()
    hscr = nc.dram_tensor(f"{pref}_h", [64, 128, T], BF16, kind="Internal").ap()
    wdbf = nc.dram_tensor(f"{pref}_wdbf", [64, 128, 16 * 128], BF16, kind="Internal").ap()
    dps6 = [cx.ps(f"dps{i}", [128, 512], F32) for i in range(6)]
    PSK = lambda i: ("psum", i)
    ln = LNKit(cx, [(dps6[2], PSK(2)), (dps6[3], PSK(3)), (dps6[4], PSK(4)), (dps6[5], PSK(5))])
    emit_postln_ffn(cx, pref, xres, xout, f"{pref}xo", lambda g: gu_t[g],
                    lambda oc: wd_t[oc].rearrange("p (i j) -> i p j", j=128),
                    wdbf, hscr, dps6, ln, cast_w, rms_eps)
    pb.op("POOL", lambda e: e.nop(),
          reads=[(f"{pref}xo", ts, c) for ts in range(4) for c in range(DCH)], tag="fin")
    pb.emit()
    return nc


def build_stage4(nc, es, debug=False):
    return _ffn_postln_standalone(nc, es, "s4", cast_w=True, rms_eps=None)


def build_stage2(nc, es):
    return _ffn_postln_standalone(nc, es, "s2", cast_w=False, rms_eps=F32EPS)



def declare_quant(nc, internal_out):
    A = {}
    A["q_gu"] = nc.dram_tensor("q_gu", [2048, 2048], F32, kind="ExternalInput").ap()
    A["q_wp"] = nc.dram_tensor("q_wp", [8192, 256], F32, kind="ExternalInput").ap()
    A["q_E"] = nc.dram_tensor("q_E", [128, 2], BF16, kind="ExternalInput").ap()
    A["q_ET"] = nc.dram_tensor("q_ET", [2, 128], F32, kind="ExternalInput").ap()
    ok = "Internal" if internal_out else "ExternalOutput"
    A["o_gu"] = nc.dram_tensor("q_gu_out", [2048, 2048], BF16, kind=ok).ap()
    A["o_wp"] = nc.dram_tensor("q_wp_out", [8192, 256], BF16, kind=ok).ap()
    if internal_out:
        A["gath_gu"] = nc.dram_tensor("q_gath_gu", [8 * 2048, 2048], BF16, kind="Internal").ap()
        A["gath_wp"] = nc.dram_tensor("q_gath_wp", [8 * 8192, 256], BF16, kind="Internal").ap()
    return A


def emit_quant(cx, A, dps, gather, deferred=None):
    pb, nc = cx.pb, cx.nc
    q_gu, q_wp, q_E, q_ET = A["q_gu"], A["q_wp"], A["q_E"], A["q_ET"]
    o_gu, o_wp = A["o_gu"], A["o_wp"]
    PSK = lambda i: ("psum", i)
    p_sum = dps[2][0:2, :]
    p_rsb = dps[3]
    p_scb = dps[4]

    wt = cx.sb("wt", [128, 512], F32)
    wbf = cx.sb("wbf", [128, 512], BF16)
    wabs = cx.sb("wabs", [128, 512], BF16)
    Eb = cx.sb("Eb", [128, 2], BF16)
    ETf = cx.sb("ETf", [2, 128], F32)
    srow = cx.sb("srow", [2, 512], F32)
    sbfrow = cx.sb("sbfrow", [2, 512], BF16)
    sfrow = cx.sb("sfrow", [2, 512], F32)
    rrow = cx.sb("rrow", [2, 512], F32)
    ratio = cx.sb("ratio", [128, 512], BF16)
    qi = cx.sb("qi", [128, 512], I32)
    qf = cx.sb("qf", [128, 512], F32)
    wq = cx.sb("wq", [128, 512], BF16)

    pb.op("SP", lambda e: e.dma_start(out=Eb[:, :], in_=q_E), writes=["Eb"], dma=True,
          chain="qconst", tag="ldE")
    pb.op("SP", lambda e: e.dma_start(out=ETf[:, :], in_=q_ET), writes=["ETf"], dma=True,
          chain="qconst", tag="ldET")

    def tile(src, dst, r0, No, tg):
        pb.op("SP", lambda e: e.dma_start(out=wt[:, 0:No], in_=src),
              writes=["wt"], dma=True, chain="qw", tag=f"qld{tg}")
        pb.op("DVE", lambda e: e.tensor_copy(wbf[:, 0:No], wt[:, 0:No]),
              reads=["wt"], writes=["wbf"], tag=f"qbf{tg}")
        pb.op("ACT", lambda e: e.activation(wabs[:, 0:No], wbf[:, 0:No], AF.Abs),
              reads=["wbf"], writes=["wabs"], tag=f"qabs{tg}")
        pb.op("PE", lambda e: e.matmul(p_sum[:, 0:No], Eb[:, :], wabs[:, 0:No],
                                       start=True, stop=True),
              reads=["wabs", "Eb"], writes=[("psum", 2)], tag=f"qsum{tg}")
        pb.op("DVE", lambda e: e.tensor_scalar(srow[:, 0:No], p_sum[:, 0:No],
                                               1.0 / 64.0, 1e-8, OP.mult, OP.max),
              reads=[("psum", 2)], writes=["srow"], tag=f"qscale{tg}")
        pb.op("DVE", lambda e: e.tensor_copy(sbfrow[:, 0:No], srow[:, 0:No]),
              reads=["srow"], writes=["sbfrow"], tag=f"qsbf{tg}")
        pb.op("DVE", lambda e: e.tensor_copy(sfrow[:, 0:No], sbfrow[:, 0:No]),
              reads=["sbfrow"], writes=["sfrow"], tag=f"qsf{tg}")
        pb.op("DVE", lambda e: e.reciprocal(rrow[:, 0:No], sfrow[:, 0:No]),
              reads=["sfrow"], writes=["rrow"], tag=f"qrr{tg}")
        pb.op("PE", lambda e: e.matmul(p_rsb[:, 0:No], ETf[:, :], rrow[:, 0:No],
                                       start=True, stop=True),
              reads=["rrow", "ETf"], writes=[("psum", 3)], tag=f"qrsb{tg}")
        pb.op("PE", lambda e: e.matmul(p_scb[:, 0:No], ETf[:, :], sfrow[:, 0:No],
                                       start=True, stop=True),
              reads=["sfrow", "ETf"], writes=[("psum", 4)], tag=f"qscb{tg}")
        pb.op("DVE", lambda e: e.tensor_mul(ratio[:, 0:No], wbf[:, 0:No], p_rsb[:, 0:No]),
              reads=["wbf", ("psum", 3)], writes=["ratio"], tag=f"qratio{tg}")
        pb.op("DVE", lambda e: e.tensor_copy(qi[:, 0:No], ratio[:, 0:No]),
              reads=["ratio"], writes=["qi"], tag=f"qi{tg}")
        pb.op("DVE", lambda e: e.tensor_copy(qf[:, 0:No], qi[:, 0:No]),
              reads=["qi"], writes=["qf"], tag=f"qf{tg}")
        pb.op("DVE", lambda e: e.tensor_scalar(qf[:, 0:No], qf[:, 0:No], -1.0, 1.0,
                                               OP.max, OP.min),
              reads=["qf"], writes=["qf"], tag=f"qclip{tg}")
        pb.op("DVE", lambda e: e.tensor_mul(wq[:, 0:No], qf[:, 0:No], p_scb[:, 0:No]),
              reads=["qf", ("psum", 4)], writes=["wq"], tag=f"qwq{tg}")
        pb.op("SP", lambda e: e.dma_start(out=dst, in_=wq[:, 0:No]),
              reads=["wq"], writes=[("qo", tg)], dma=True, chain="qo", tag=f"qst{tg}")
        return ("qo", tg)

    outs = []
    jobs = []
    for dc in range(16):
        for j in range(4):
            jobs.append((q_gu[dc * 128:(dc + 1) * 128, j * 512:(j + 1) * 512],
                         o_gu[dc * 128:(dc + 1) * 128, j * 512:(j + 1) * 512],
                         dc, 512, f"gu{dc}_{j}"))
    for hc in range(64):
        jobs.append((q_wp[hc * 128:(hc + 1) * 128, :],
                     o_wp[hc * 128:(hc + 1) * 128, :],
                     hc, 256, f"wp{hc}"))
    if deferred is not None:
        def mk(job):
            return lambda: outs.append(tile(*job))
        deferred.extend(mk(j) for j in jobs)
        def finish():
            while deferred:
                deferred.pop(0)()
            _quant_gather(cx, A, outs, gather)
        return finish
    for job in jobs:
        outs.append(tile(*job))
    _quant_gather(cx, A, outs, gather)
    return outs


def _quant_gather(cx, A, outs, gather):
    pb = cx.pb
    if gather:
        pb.op("POOL", lambda e: e.collective_compute(
            "AllGather", OP.bypass, replica_groups=[list(range(8))],
            ins=[A["o_gu"].opt()], outs=[A["gath_gu"].opt()]),
            reads=outs, writes=["qgath_gu"], coll=True, chain="qag1", tag="qag1")
        pb.op("POOL", lambda e: e.collective_compute(
            "AllGather", OP.bypass, replica_groups=[list(range(8))],
            ins=[A["o_wp"].opt()], outs=[A["gath_wp"].opt()]),
            reads=outs, writes=["qgath_wp"], coll=True, chain="qag2", tag="qag2")
    return outs


def build_quant(nc, es):
    pb = PB(nc, es)
    cx = Ctx(nc, es, pb)
    A = declare_quant(nc, internal_out=False)
    dps = [cx.ps(f"dps{i}", [128, 512], F32) for i in range(6)]
    outs = emit_quant(cx, A, dps, gather=False)
    pb.op("POOL", lambda e: e.nop(), reads=outs, tag="fin")
    pb.emit()
    return nc



def declare_stage1(nc, xres=None):
    A = {}
    A["xres"] = xres if xres is not None else nc.dram_tensor("s1_xres", [D, T], F32, kind="ExternalInput").ap()
    for nm, sh, dt in [("wq", [16, 128, 2048], F32), ("wk", [16, 128, 2048], F32),
                       ("wv", [16, 128, 2048], F32), ("wg", [16, 128, 2048], F32),
                       ("wo", [16, 128, 2048], F32), ("bcols", [7 * 16, 128], F32),
                       ("gam", [1, 16], F32), ("DL", [128, 128], F32),
                       ("T1", [128, 128], F32), ("C01", [128, 128], F32),
                       ("REV", [128, 1], F32), ("J128", [128, 16], F32),
                       ("IDB", [128, 128], BF16), ("SEL", [128, 1], F32)]:
        A[nm] = nc.dram_tensor(f"s1_{nm}", sh, dt, kind="ExternalInput").ap()
    for nm, sh, dt in [("qs", [16, 128, T], BF16), ("ks", [16, 128, T], BF16),
                       ("gs", [16, 128, T], BF16), ("vts", [16, 128, T], BF16),
                       ("sls", [16, 128, 15 * 128], BF16), ("ccsi", [128, 16 * 128], BF16),
                       ("ccso", [256, 16 * 128], BF16), ("ccti", [1, 32], F32),
                       ("ccto", [1, 32], F32)]:
        A[nm] = nc.dram_tensor(f"s1_{nm}", sh, dt, kind="Internal").ap()
    return A


def emit_stage1(cx, A, dps, ln, xout, okey, pre_ops=None, post_proj=None,
                interleave_state=None):
    pb, nc = cx.pb, cx.nc
    xres = A["xres"]
    w_q, w_k, w_v, w_g, w_o = A["wq"], A["wk"], A["wv"], A["wg"], A["wo"]
    bcolsd, gamr = A["bcols"], A["gam"]
    DL, T1, C01, REV, J128, IDB, SEL = A["DL"], A["T1"], A["C01"], A["REV"], A["J128"], A["IDB"], A["SEL"]
    q_scr, k_scr, g_scr, vt_scr, sl_scr = A["qs"], A["ks"], A["gs"], A["vts"], A["sls"]
    cc_s_in, cc_s_out, cc_t_in, cc_t_out = A["ccsi"], A["ccso"], A["ccti"], A["ccto"]
    PSK = lambda i: ("psum", i)

    esS = ExitStack()
    esS.__enter__()
    ptbs = [esS.enter_context(nc.psum_tensor(f"s1ptb{i}", [128, 512], BF16, allow_name_mangling=True)) for i in range(2)]
    bcols = esS.enter_context(nc.sbuf_tensor("bcols", [128, 7 * 16], F32, allow_name_mangling=True))
    if pre_ops is not None:
        pre_ops()
    interleave = interleave_state.get("deferred") if interleave_state else None
    for v in range(7 * 16):
        pb.op("SP", lambda e, v=v: e.dma_start(out=bcols[:, v:v + 1], in_=bcolsd[v, :]),
              writes=[("bcols", v)], dma=True, chain="cols", tag=f"col{v}")
    BCOLS = [("bcols", v) for v in range(7 * 16)]

    def bcol(vec, oc):
        return bcols[:, vec * 16 + oc: vec * 16 + oc + 1]

    cx.cc(1e-5, 1)

    esP = ExitStack()
    with esP:
        def sbP(nm, sh, dt=F32):
            return esP.enter_context(nc.sbuf_tensor(nm, sh, dt, allow_name_mangling=True))
        xbf = sbP("xbf", [128, DCH * T], BF16)
        wpb = [sbP(f"wpb{i}", [128, 16 * 128], BF16) for i in range(2)]
        stgo = sbP("stgo", [128, T], BF16)
        vtstg = sbP("vtstg", [128, T], BF16)
        identP = sbP("identP", [128, 128], BF16)

        pb.op("SP", lambda e: e.dma_start(out=identP[:, :], in_=IDB),
              writes=["identP"], dma=True, chain="cst", tag="idb")
        XBF = [("xbf", c) for c in range(DCH)]
        for c in range(DCH):
            pb.op("POOL", lambda e, c=c: e.dma_start(
                out=xbf[:, c * T:(c + 1) * T], in_=xres[c * 128:(c + 1) * 128, :]),
                writes=[("xbf", c)], dma=True, chain="xld", tag=f"xld{c}")

        def proj(nm, w_ap, bvec, func, scr, do_vt):
            for oc in range(16):
                if interleave:
                    for _ in range(2):
                        if interleave:
                            interleave.pop(0)()
                wb = wpb[oc % 2]
                pb.op("POOL", lambda e, oc=oc, wb=wb: e.dma_start(out=wb[:, :], in_=w_ap[oc]),
                      writes=[("wpb", oc % 2)], dma=True, chain=f"{nm}w", tag=f"{nm}w{oc}")
                for ts in range(4):
                    bank, bk = dps[ts % 2], PSK(ts % 2)

                    def mm(e, wb=wb, bank=bank, ts=ts):
                        last = None
                        for k in range(16):
                            last = e.matmul(bank[:, :], wb[:, k * 128:(k + 1) * 128],
                                            xbf[:, k * T + ts * 512: k * T + (ts + 1) * 512],
                                            start=(k == 0), stop=(k == 15))
                        return last
                    pb.op("PE", mm, reads=[("wpb", oc % 2)] + XBF, writes=[bk], tag=f"{nm}mm{oc}_{ts}")
                    pb.op("ACT", lambda e, bank=bank, ts=ts, oc=oc: e.activation(
                        stgo[:, ts * 512:(ts + 1) * 512], bank[:, :], func,
                        bias=bcol(bvec, oc)),
                        reads=[bk] + BCOLS, writes=[("stgo", ts)], tag=f"{nm}ev{oc}_{ts}")
                if not do_vt:
                    pb.op("SP", lambda e, oc=oc, scr=scr: e.dma_start(out=scr[oc], in_=stgo[:, :]),
                          reads=[("stgo", t_) for t_ in range(4)],
                          writes=[(f"{nm}s", oc)], dma=True, chain=f"{nm}st", tag=f"{nm}st{oc}")
                else:
                    for tc in range(16):
                        pt = ptbs[tc % 2][:, 0:128]
                        pb.op("PE", lambda e, tc=tc, pt=pt: e.transpose(
                            pt, stgo[:, tc * 128:(tc + 1) * 128], identP[:, :]),
                            reads=[("stgo", tc // 4), "identP"], writes=[("ptb", tc % 2)],
                            tag=f"{nm}tr{oc}_{tc}")
                        pb.op("DVE", lambda e, tc=tc, pt=pt: e.tensor_copy(
                            vtstg[:, tc * 128:(tc + 1) * 128], pt),
                            reads=[("ptb", tc % 2)], writes=[("vtstg", tc)], tag=f"{nm}te{oc}_{tc}")
                    pb.op("SP", lambda e, oc=oc, scr=scr: e.dma_start(out=scr[oc], in_=vtstg[:, :]),
                          reads=[("vtstg", t_) for t_ in range(16)],
                          writes=[(f"{nm}s", oc)], dma=True, chain=f"{nm}st", tag=f"{nm}st{oc}")

        proj("q", w_q, 0, AF.Identity, q_scr, False)
        proj("k", w_k, 1, AF.Identity, k_scr, False)
        proj("g", w_g, 2, AF.Silu, g_scr, False)
        proj("v", w_v, 3, AF.Identity, vt_scr, True)
    if post_proj is not None:
        post_proj()
    pb.full_barrier("endP")

    esA = ExitStack()
    with esA:
        def sbA(nm, sh, dt=F32):
            return esA.enter_context(nc.sbuf_tensor(nm, sh, dt, allow_name_mangling=True))
        ret = sbA("ret", [128, 16 * T], BF16)
        khb = sbA("khb", [128, T], BF16)
        qhb = sbA("qhb", [128, T], BF16)
        vthb = sbA("vthb", [128, T], BF16)
        slocb = sbA("slocb", [128, 15 * 128], BF16)
        ghb = sbA("ghb", [128, T], BF16)
        sinb = sbA("sinb", [128, 16 * 128], BF16)
        Dmask = sbA("Dmask", [128, 16 * 128], BF16)
        Gq = sbA("Gq", [128, 16 * 128], BF16)
        gpow = sbA("gpow", [128, 16 * 16], F32)
        wdecs = sbA("wdecs", [128, 16], F32)
        lnbc = sbA("lnbc", [128, 16], F32)
        lnrow = sbA("lnrow", [1, 16], F32)
        gamrs = sbA("gamrs", [1, 16], F32)
        S = sbA("S", [128, 16 * 128], F32)
        sfin = sbA("sfin", [128, 16 * 128], BF16)
        sstg = sbA("sstg", [128, 15 * 128], BF16)
        ktd = sbA("ktd", [128, 128], BF16)
        msk = sbA("msk", [128, 128], BF16)
        qg = sbA("qg", [128, 128], BF16)
        qg2 = sbA("qg2", [128, 128], BF16)
        statc = sbA("statc", [128, 32], F32)
        dead = sbA("dead", [128, T], BF16)
        onesf = sbA("onesf", [128, 1], F32)
        tmpc = sbA("tmpc", [128, 128], F32)
        tmpm = sbA("tmpm", [128, 128], F32)
        rws = sbA("rws", [1, 64], F32)
        acol = sbA("acol", [128, 16], F32)
        bcolt = sbA("bcolt", [128, 16], F32)
        identA = sbA("identA", [128, 128], BF16)
        cD = sbA("cD", [128, 128], F32)
        cT1 = sbA("cT1", [128, 128], F32)
        cC01 = sbA("cC01", [128, 128], F32)
        cREV = sbA("cREV", [128, 1], F32)
        cJ = sbA("cJ", [128, 16], F32)
        cSEL = sbA("cSEL", [128, 1], F32)
        s_tile = sbA("s_tile", [128, DCH * 512], F32)
        xold = sbA("xold", [128, DCH * 512], F32)
        wob = [sbA(f"wob{i}", [128, 16 * 128], BF16) for i in range(2)]

        for t_, ap_ in [(identA, IDB), (cD, DL), (cT1, T1), (cC01, C01), (cREV, REV),
                        (cJ, J128), (cSEL, SEL), (gamrs, gamr)]:
            pb.op("SP", lambda e, t_=t_, ap_=ap_: e.dma_start(out=t_[:, :], in_=ap_),
                  writes=[("cst2", id(t_))], dma=True, chain="cst2", tag="cst2")
        CST = [("cst2", id(t_)) for t_ in (identA, cD, cT1, cC01, cREV, cJ, cSEL, gamrs)]
        pb.op("DVE", lambda e: e.memset(onesf[:, :], 1.0), writes=["onesf"], tag="onesf")
        pb.op("DVE", lambda e: e.memset(S[:, :], 0.0), writes=["S"], tag="S0")

        pb.op("ACT", lambda e: e.activation(lnrow[0:1, :], gamrs[0:1, :], AF.Ln),
              reads=CST, writes=["lnrow"], tag="lngam")
        pb.op("PE", lambda e: e.matmul(dps[4][:, 0:16], ln.ones1f[0:1, :], lnrow[0:1, :],
                                       start=True, stop=True),
              reads=["lnrow", "ln_ones1f"], writes=[PSK(4)], tag="lnbcm")
        pb.op("DVE", lambda e: e.tensor_copy(lnbc[:, :], dps[4][:, 0:16]),
              reads=[PSK(4)], writes=["lnbc"], tag="lnbce")
        for h in range(16):
            lc = lnbc[:, h:h + 1]
            pb.op("DVE", lambda e, h=h, lc=lc: e.tensor_scalar_mul(tmpm[:, :], cD[:, :], lc),
                  reads=["lnbc"] + CST, writes=["tmpm"], tag=f"dm1_{h}")
            pb.op("ACT", lambda e: e.activation(tmpc[:, :], tmpm[:, :], AF.Exp),
                  reads=["tmpm"], writes=["tmpc"], tag=f"dm2_{h}")
            pb.op("DVE", lambda e, h=h: e.tensor_mul(Dmask[:, h * 128:(h + 1) * 128],
                                                     tmpc[:, :], cC01[:, :]),
                  reads=["tmpc"] + CST, writes=[("Dmask", h)], tag=f"dm3_{h}")
            pb.op("DVE", lambda e, h=h, lc=lc: e.tensor_scalar_mul(tmpm[:, :], cT1[:, :], lc),
                  reads=["lnbc"] + CST, writes=["tmpm"], tag=f"gq1_{h}")
            pb.op("ACT", lambda e, h=h: e.activation(Gq[:, h * 128:(h + 1) * 128],
                                                     tmpm[:, :], AF.Exp),
                  reads=["tmpm"], writes=[("Gq", h)], tag=f"gq2_{h}")
            pb.op("DVE", lambda e, h=h, lc=lc: e.tensor_scalar_mul(tmpm[:, 0:16], cJ[:, :], lc),
                  reads=["lnbc"] + CST, writes=["tmpm"], tag=f"gp1_{h}")
            pb.op("ACT", lambda e, h=h: e.activation(gpow[:, h * 16:(h + 1) * 16],
                                                     tmpm[:, 0:16], AF.Exp),
                  reads=["tmpm"], writes=[("gpow", h)], tag=f"gp2_{h}")
            pb.op("DVE", lambda e, h=h, lc=lc: e.tensor_scalar_mul(tmpm[:, 0:1], cREV[:, :], lc),
                  reads=["lnbc"] + CST, writes=["tmpm"], tag=f"wd1_{h}")
            pb.op("ACT", lambda e, h=h: e.activation(wdecs[:, h:h + 1], tmpm[:, 0:1], AF.Exp),
                  reads=["tmpm"], writes=[("wdecs", h)], tag=f"wd2_{h}")

        for h in range(16):
            pb.op("SP", lambda e, h=h: e.dma_start(out=khb[:, :], in_=k_scr[h]),
                  reads=[("ks", h)], writes=["khb"], dma=True, chain="khl", tag=f"kh1_{h}")
            pb.op("SP", lambda e, h=h: e.dma_start(out=vthb[:, :], in_=vt_scr[h]),
                  reads=[("vs", h)], writes=["vthb"], dma=True, chain="vhl", tag=f"vh1_{h}")
            Sh = S[:, h * 128:(h + 1) * 128]
            for j in range(16):
                if j >= 1:
                    pb.op("DVE", lambda e, h=h, j=j, Sh=Sh: e.tensor_copy(
                        sstg[:, (j - 1) * 128:j * 128], Sh),
                        reads=[("S", h)], writes=[("sstg", j - 1)], tag=f"sst{h}_{j}")
                pt = ptbs[0][:, 0:128]
                pb.op("PE", lambda e, j=j, pt=pt: e.transpose(
                    pt, khb[:, j * 128:(j + 1) * 128], identA[:, :]),
                    reads=["khb", "identA"] + CST, writes=[("ptb", 0)], tag=f"ktr{h}_{j}")
                pb.op("DVE", lambda e, h=h, pt=pt: e.tensor_scalar_mul(
                    ktd[:, :], pt, wdecs[:, h:h + 1]),
                    reads=[("ptb", 0), ("wdecs", h)], writes=["ktd"], tag=f"ktd{h}_{j}")
                pb.op("PE", lambda e, j=j: e.matmul(dps[2][:, 0:128], ktd[:, :],
                                                    vthb[:, j * 128:(j + 1) * 128],
                                                    start=True, stop=True),
                      reads=["ktd", "vthb"], writes=[PSK(2)], tag=f"ds{h}_{j}")
                pb.op("DVE", lambda e, h=h, Sh=Sh: e.scalar_tensor_tensor(
                    Sh, Sh, gpow[:, h * 16 + 1:h * 16 + 2], dps[2][:, 0:128],
                    OP.mult, OP.add),
                    reads=[("S", h), ("gpow", h), PSK(2)], writes=[("S", h)], tag=f"su{h}_{j}")
            pb.op("SP", lambda e, h=h: e.dma_start(out=sl_scr[h], in_=sstg[:, :]),
                  reads=[("sstg", j_) for j_ in range(15)],
                  writes=[("sls", h)], dma=True, chain="slst", tag=f"slst{h}")
            pb.op("DVE", lambda e, h=h, Sh=Sh: e.tensor_copy(sfin[:, h * 128:(h + 1) * 128], Sh),
                  reads=[("S", h)], writes=[("sfin", h)], tag=f"sfin{h}")

        pb.op("POOL", lambda e: e.dma_start(out=cc_s_in, in_=sfin[:, :]),
              reads=[("sfin", h) for h in range(16)], writes=["ccsi"], dma=True,
              chain="ccst", tag="ccst")
        pb.op("POOL", lambda e: e.collective_compute(
            "AllGather", OP.bypass, replica_groups=PAIRS,
            ins=[cc_s_in.opt()], outs=[cc_s_out.opt()]),
            reads=["ccsi"], writes=["ccso"], coll=True, chain="ccag", tag="ccag")
        pb.op("POOL", lambda e: e.dma_start(out=sinb[:, :], in_=cc_s_out[0:128, :]),
              reads=["ccso"], writes=["sinraw"], dma=True, chain="ccld", tag="ccld")
        pb.op("DVE", lambda e: e.tensor_scalar_mul(sinb[:, :], sinb[:, :], cSEL[:, :]),
              reads=["sinraw"] + CST, writes=["sinb"], tag="sinsel")

        for h in range(16):
            pb.op("SP", lambda e, h=h: e.dma_start(out=khb[:, :], in_=k_scr[h]),
                  reads=[("ks", h)], writes=["khb"], dma=True, chain="khl", tag=f"kh2_{h}")
            pb.op("SP", lambda e, h=h: e.dma_start(out=qhb[:, :], in_=q_scr[h]),
                  reads=[("qs", h)], writes=["qhb"], dma=True, chain="qhl", tag=f"qh2_{h}")
            pb.op("SP", lambda e, h=h: e.dma_start(out=vthb[:, :], in_=vt_scr[h]),
                  reads=[("vs", h)], writes=["vthb"], dma=True, chain="vhl", tag=f"vh2_{h}")
            pb.op("SP", lambda e, h=h: e.dma_start(out=slocb[:, :], in_=sl_scr[h]),
                  reads=[("sls", h)], writes=["slocb"], dma=True, chain="sll", tag=f"sll{h}")
            for j in range(16):
                pb.op("PE", lambda e, j=j: e.matmul(
                    dps[2][:, 0:128], khb[:, j * 128:(j + 1) * 128],
                    qhb[:, j * 128:(j + 1) * 128], start=True, stop=True),
                    reads=["khb", "qhb"], writes=[PSK(2)], tag=f"sc{h}_{j}")
                pb.op("DVE", lambda e, h=h: e.tensor_mul(
                    msk[:, :], dps[2][:, 0:128], Dmask[:, h * 128:(h + 1) * 128]),
                    reads=[PSK(2), ("Dmask", h)], writes=["msk"], tag=f"mk{h}_{j}")
                pb.op("DVE", lambda e, h=h, j=j: e.tensor_mul(
                    qg[:, :], qhb[:, j * 128:(j + 1) * 128], Gq[:, h * 128:(h + 1) * 128]),
                    reads=["qhb", ("Gq", h)], writes=["qg"], tag=f"qg{h}_{j}")
                pb.op("DVE", lambda e, h=h, j=j: e.tensor_scalar_mul(
                    qg2[:, :], qg[:, :], gpow[:, h * 16 + j:h * 16 + j + 1]),
                    reads=["qg", ("gpow", h)], writes=["qg2"], tag=f"qg2{h}_{j}")

                def ymm(e, h=h, j=j):
                    e.matmul(dps[3][:, 0:128], vthb[:, j * 128:(j + 1) * 128], msk[:, :],
                             start=True, stop=False)
                    if j >= 1:
                        e.matmul(dps[3][:, 0:128], slocb[:, (j - 1) * 128:j * 128], qg[:, :],
                                 start=False, stop=False)
                    return e.matmul(dps[3][:, 0:128], sinb[:, h * 128:(h + 1) * 128], qg2[:, :],
                                    start=False, stop=True)
                pb.op("PE", ymm, reads=["vthb", "msk", "slocb", "qg", "qg2", "sinb"],
                      writes=[PSK(3)], tag=f"y{h}_{j}")
                pb.op("DVE", lambda e, h=h, j=j: e.tensor_copy(
                    ret[:, h * T + j * 128: h * T + (j + 1) * 128], dps[3][:, 0:128]),
                    reads=[PSK(3)], writes=[("ret", h)], tag=f"ye{h}_{j}")

        for h in range(16):
            pb.op("ACT", lambda e, h=h: e.activation(
                dead[:, :], ret[:, h * T:(h + 1) * T], AF.Copy,
                accum_out=statc[:, h:h + 1]),
                reads=[("ret", h)], writes=["dead", ("statc", h)], tag=f"sts{h}")
            pb.op("ACT", lambda e, h=h: e.activation(
                dead[:, :], ret[:, h * T:(h + 1) * T], AF.Square,
                accum_out=statc[:, 16 + h:17 + h]),
                reads=[("ret", h)], writes=["dead", ("statc", 16 + h)], tag=f"stq{h}")
        pb.op("PE", lambda e: e.matmul(dps[5][0:1, 0:32], onesf[:, :], statc[:, :],
                                       start=True, stop=True),
              reads=["onesf"] + [("statc", i) for i in range(32)], writes=[PSK(5)], tag="stmm")
        pb.op("DVE", lambda e: e.tensor_copy(rws[0:1, 0:32], dps[5][0:1, 0:32]),
              reads=[PSK(5)], writes=["rws"], tag="stev")
        pb.op("POOL", lambda e: e.dma_start(out=cc_t_in, in_=rws[0:1, 0:32]),
              reads=["rws"], writes=["ccti"], dma=True, chain="ccst2", tag="ccst2")
        pb.op("POOL", lambda e: e.collective_compute(
            "AllReduce", OP.add, replica_groups=PAIRS,
            ins=[cc_t_in.opt()], outs=[cc_t_out.opt()]),
            reads=["ccti"], writes=["ccto"], coll=True, chain="ccar", tag="ccar")
        pb.op("POOL", lambda e: e.dma_start(out=rws[0:1, 32:64], in_=cc_t_out),
              reads=["ccto"], writes=["rws2"], dma=True, chain="ccld2", tag="ccld2")
        CNT = 1.0 / (4096.0 * 128.0)
        pb.op("DVE", lambda e: e.tensor_scalar_mul(rws[0:1, 0:16], rws[0:1, 32:48], CNT),
              reads=["rws2"], writes=["rwm"], tag="gn_mean")
        pb.op("DVE", lambda e: e.tensor_scalar_mul(rws[0:1, 16:32], rws[0:1, 48:64], CNT),
              reads=["rws2"], writes=["rwq"], tag="gn_msq")
        pb.op("DVE", lambda e: e.tensor_mul(rws[0:1, 32:48], rws[0:1, 0:16], rws[0:1, 0:16]),
              reads=["rwm"], writes=["rwm2"], tag="gn_m2")
        pb.op("DVE", lambda e: e.tensor_sub(rws[0:1, 16:32], rws[0:1, 16:32], rws[0:1, 32:48]),
              reads=["rwq", "rwm2"], writes=["rwv"], tag="gn_var")
        epst, epsk = cx.cc(1e-5, 1)
        pb.op("ACT", lambda e: e.activation(rws[0:1, 32:48], rws[0:1, 16:32], AF.Sqrt,
                                            bias=epst[0:1, 0:1]),
              reads=["rwv", epsk], writes=["rwsd"], tag="gn_sd")
        pb.op("DVE", lambda e: e.reciprocal(rws[0:1, 16:32], rws[0:1, 32:48]),
              reads=["rwsd"], writes=["rwr"], tag="gn_rstd")
        pb.op("PE", lambda e: e.matmul(dps[4][:, 0:32], ln.ones1f[0:1, :], rws[0:1, 0:32],
                                       start=True, stop=True),
              reads=["rwm", "rwr", "ln_ones1f"], writes=[PSK(4)], tag="gn_bc")
        pb.op("DVE", lambda e: e.tensor_mul(acol[:, :], bcols[:, 5 * 16:6 * 16], dps[4][:, 16:32]),
              reads=[PSK(4)] + BCOLS, writes=["acol"], tag="gn_a")
        pb.op("DVE", lambda e: e.tensor_mul(bcolt[:, :], acol[:, :], dps[4][:, 0:16]),
              reads=["acol", PSK(4)], writes=["bcolt"], tag="gn_b1")
        pb.op("DVE", lambda e: e.tensor_sub(bcolt[:, :], bcols[:, 6 * 16:7 * 16], bcolt[:, :]),
              reads=["bcolt"] + BCOLS, writes=["bcolt"], tag="gn_b2")
        for h in range(16):
            pb.op("ACT", lambda e, h=h: e.activation(
                ret[:, h * T:(h + 1) * T], ret[:, h * T:(h + 1) * T], AF.Identity,
                scale=acol[:, h:h + 1], bias=bcolt[:, h:h + 1]),
                reads=[("ret", h), "acol", "bcolt"], writes=[("ret", h)], tag=f"gnap{h}")
            pb.op("SP", lambda e, h=h: e.dma_start(out=ghb[:, :], in_=g_scr[h]),
                  reads=[("gs", h)], writes=["ghb"], dma=True, chain="ghl", tag=f"ghl{h}")
            pb.op("DVE", lambda e, h=h: e.tensor_mul(
                ret[:, h * T:(h + 1) * T], ret[:, h * T:(h + 1) * T], ghb[:, :]),
                reads=[("ret", h), "ghb"], writes=[("ret", h)], tag=f"gmul{h}")

        RETALL = [("ret", h) for h in range(16)]
        for ts in range(4):
            for c in range(DCH):
                pb.op("SP", lambda e, c=c, ts=ts: e.dma_start(
                    out=xold[:, c * 512:(c + 1) * 512],
                    in_=xres[c * 128:(c + 1) * 128, ts * 512:(ts + 1) * 512]),
                    writes=[("xold", c)], dma=True, chain="xold", tag=f"xold{ts}_{c}")
            for oc in range(16):
                wb = wob[oc % 2]
                pb.op("POOL", lambda e, oc=oc, wb=wb, ts=ts: e.dma_start(out=wb[:, :], in_=w_o[oc]),
                      writes=[("wob", oc % 2)], dma=True, chain="wol", tag=f"wol{ts}_{oc}")
                bank, bk = dps[oc % 2], PSK(oc % 2)

                def mmo(e, wb=wb, bank=bank, ts=ts):
                    last = None
                    for k in range(16):
                        last = e.matmul(bank[:, :], wb[:, k * 128:(k + 1) * 128],
                                        ret[:, k * T + ts * 512: k * T + (ts + 1) * 512],
                                        start=(k == 0), stop=(k == 15))
                    return last
                pb.op("PE", mmo, reads=[("wob", oc % 2)] + RETALL, writes=[bk], tag=f"omm{ts}_{oc}")
                pb.op("DVE", lambda e, bank=bank, oc=oc: e.scalar_tensor_tensor(
                    s_tile[:, oc * 512:(oc + 1) * 512], bank[:, :], bcol(4, oc),
                    xold[:, oc * 512:(oc + 1) * 512], OP.add, OP.add),
                    reads=[bk, ("xold", oc)] + BCOLS, writes=[("s", oc)], tag=f"sadd{ts}_{oc}")
            ln.emit(ts, s_tile, [("s", c) for c in range(DCH)], xout, okey, None, [],
                    tag=f"ln1_{ts}")
    esS.__exit__(None, None, None)


def build_stage1(nc, es):
    pb = PB(nc, es)
    cx = Ctx(nc, es, pb)
    A = declare_stage1(nc)
    xout = nc.dram_tensor("s1_xout", [D, T], F32, kind="ExternalOutput").ap()
    dps = [cx.ps(f"dps{i}", [128, 512], F32) for i in range(6)]
    PSK = lambda i: ("psum", i)
    ln = LNKit(cx, [(dps[2], PSK(2)), (dps[3], PSK(3)), (dps[4], PSK(4)), (dps[5], PSK(5))])
    emit_stage1(cx, A, dps, ln, xout, "s1xo")
    pb.op("POOL", lambda e: e.nop(),
          reads=[("s1xo", ts, c) for ts in range(4) for c in range(DCH)], tag="fin")
    pb.emit()
    return nc


def _s1_host_base(wq, bq, wk, bk, wv, bv, wg, bg, wo, bo, gn_w, gn_b, gamma):
    idx = np.arange(128, dtype=np.float32)
    DLh = idx[None, :] - idx[:, None]
    T1h = np.broadcast_to(idx[None, :] + 1.0, (128, 128)).copy()
    C01h = (idx[None, :] >= idx[:, None]).astype(np.float32)
    REVh = (127.0 - idx)[:, None].copy()
    J128h = np.broadcast_to(128.0 * np.arange(16, dtype=np.float32)[None, :], (128, 16)).copy()
    IDBh = np.eye(128).astype(BF)
    bcolsh = np.concatenate([vecc(v) for v in (bq, bk, bg, bv, bo, gn_w, gn_b)], axis=0)
    return {"s1_wq": tile_lhsT(wq), "s1_wk": tile_lhsT(wk), "s1_wv": tile_lhsT(wv),
            "s1_wg": tile_lhsT(wg), "s1_wo": tile_lhsT(wo),
            "s1_bcols": bcolsh, "s1_gam": np.asarray(gamma).reshape(1, 16).astype(np.float32),
            "s1_DL": DLh.astype(np.float32), "s1_T1": T1h, "s1_C01": C01h,
            "s1_REV": REVh, "s1_J128": J128h, "s1_IDB": IDBh}


def run_stage1(x, wq, bq, wk, bk, wv, bv, wg, bg, wo, bo, gn_w, gn_b, gamma):
    nc = bass.Bass("TRN2", target_bir_lowering=False)
    es = ExitStack()
    with es:
        build_stage1(nc, es)
    base = _s1_host_base(wq, bq, wk, bk, wv, bv, wg, bg, wo, bo, gn_w, gn_b, gamma)
    xs = _shard_x_fm(x)
    ims = []
    for c in range(NCORES):
        im = dict(base)
        im["s1_xres"] = xs[c]
        im["s1_SEL"] = np.full((128, 1), float(c % 2), np.float32)
        ims.append(im)
    res = run_bass_kernel_spmd(nc, ims, core_ids=list(range(NCORES)))
    return _gather_fm(res, "s1_xout", x)



def declare_stage3(nc, xres=None):
    A = {}
    NH, DS, NCH = 32, 64, 16
    A["xres"] = xres if xres is not None else nc.dram_tensor("s3_xres", [D, T], F32, kind="ExternalInput").ap()
    for nm, sh, dt in [("win", [34, 128, 2048], F32), ("wop", [16, 128, 2048], F32),
                       ("mgu", [128, 128, 2048], F32), ("mwd", [16, 128, 64 * 128], F32),
                       ("ccols", [128, 128], F32), ("convw", [68, 128], F32),
                       ("arow", [1, 32], F32), ("drow", [1, 32], F32),
                       ("TRIL", [128, 128], F32), ("TRIU", [128, 128], F32),
                       ("C01", [128, 128], F32), ("IDF", [128, 128], F32),
                       ("IDB", [128, 128], BF16), ("SEL", [128, 1], F32)]:
        A[nm] = nc.dram_tensor(f"s3_{nm}", sh, dt, kind="ExternalInput").ap()
    for nm, sh, dt in [("zs", [16, 128, T], BF16), ("xct", [NCH, 128, 2048], BF16),
                       ("bct", [NCH, 128, 128], BF16), ("stl", [NCH, 64, NH * DS], BF16),
                       ("yn", [NCH, 128, 2048], BF16), ("xmid", [D, T], F32),
                       ("h", [64, 128, T], BF16), ("mwdbf", [64, 128, 16 * 128], BF16),
                       ("ccci", [128, 51], BF16), ("ccco", [256, 51], BF16),
                       ("ccsi", [64, NH * DS], BF16), ("ccso", [128, NH * DS], BF16)]:
        A[nm] = nc.dram_tensor(f"s3_{nm}", sh, dt, kind="Internal").ap()
    return A


def emit_stage3(cx, A, dps, ln, xout, okey):
    pb, nc = cx.pb, cx.nc
    NH, DH, DS, Q = 32, 64, 64, 128
    NCH = 16
    xres = A["xres"]
    w_in, w_op, mlp_gu, mlp_wd = A["win"], A["wop"], A["mgu"], A["mwd"]
    ccols, convw, arow, drow = A["ccols"], A["convw"], A["arow"], A["drow"]
    TRIL, TRIU, C01, IDF, IDB, SEL = A["TRIL"], A["TRIU"], A["C01"], A["IDF"], A["IDB"], A["SEL"]
    z_scr, xct_scr, bct_scr, stl_scr, yn_scr = A["zs"], A["xct"], A["bct"], A["stl"], A["yn"]
    xmid_scr, h_scr, mwd_scr = A["xmid"], A["h"], A["mwdbf"]
    cc_c_in, cc_c_out, cc_s_in, cc_s_out = A["ccci"], A["ccco"], A["ccsi"], A["ccso"]
    PSK = lambda i: ("psum", i)
    cx.cc(1e-5, 1)

    esS = ExitStack()
    esS.__enter__()

    def sbS(nm, sh, dt=F32):
        return esS.enter_context(nc.sbuf_tensor(nm, sh, dt, allow_name_mangling=True))
    cco = sbS("cco", [128, 128], F32)
    cvw = sbS("cvw", [128, 68], F32)
    cTRIL = sbS("cTRIL", [128, 128], F32)
    cTRIU = sbS("cTRIU", [128, 128], F32)
    cC01 = sbS("cC01", [128, 128], F32)
    cIDF = sbS("cIDF", [128, 128], F32)
    cIDB = sbS("cIDB", [128, 128], BF16)
    cSEL = sbS("cSEL", [128, 1], F32)
    car = sbS("car", [1, 32], F32)
    cdr = sbS("cdr", [1, 32], F32)
    acol = sbS("acol", [32, 1], F32)

    pb.op("SP", lambda e: e.dma_start(out=cco[:, :], in_=ccols), writes=["cco"], dma=True,
          chain="cst", tag="ccols")
    for t_, ap_, k_ in [(cvw, convw.rearrange("a b -> b a"), "cvw0")]:
        pass
    for v in range(68):
        pb.op("SP", lambda e, v=v: e.dma_start(out=cvw[:, v:v + 1], in_=convw[v, :]),
              writes=[("cvw", v)], dma=True, chain="cst", tag=f"cvw{v}")
    CVW = [("cvw", v) for v in range(68)]
    for t_, ap_ in [(cTRIL, TRIL), (cTRIU, TRIU), (cC01, C01), (cIDF, IDF), (cIDB, IDB),
                    (cSEL, SEL), (car, arow), (cdr, drow)]:
        pb.op("SP", lambda e, t_=t_, ap_=ap_: e.dma_start(out=t_[:, :], in_=ap_),
              writes=[("cst", id(t_))], dma=True, chain="cst", tag="cst")
    CST = ["cco"] + CVW + [("cst", id(t_)) for t_ in (cTRIL, cTRIU, cC01, cIDF, cIDB, cSEL, car, cdr)]

    def ccol(i):
        return cco[:, i:i + 1]

    pb.op("ACT", lambda e: e.activation(acol[:, :], cco[0:32, 65:66], AF.Exp),
          reads=CST, writes=["acol"], tag="aexp")
    pb.op("DVE", lambda e: e.tensor_scalar_mul(acol[:, :], acol[:, :], -1.0),
          reads=["acol"], writes=["acol"], tag="aneg")

    XBCK = [("xbc", i) for i in range(17)]

    esB = ExitStack()
    esB.__enter__()
    xBC = esB.enter_context(nc.sbuf_tensor("xBC", [128, 17 * T], BF16, allow_name_mangling=True))
    stk = esB.enter_context(nc.sbuf_tensor("stk", [64, T], F32, allow_name_mangling=True))

    es1 = ExitStack()
    with es1:
        def sb1(nm, sh, dt=F32):
            return es1.enter_context(nc.sbuf_tensor(nm, sh, dt, allow_name_mangling=True))
        xnsl = sb1("xnsl", [128, DCH * 512], BF16)
        xsl = sb1("xsl", [128, DCH * 512], F32)
        sq1 = sb1("sq1", [128, 2 * 512], BF16)
        rws1 = sb1("rws1", [1, 2 * 512], F32)
        wpb = [sb1(f"wpb{i}", [128, 16 * 128], BF16) for i in range(2)]
        zstg = sb1("zstg", [128, 512], BF16)
        dstg = sb1("dstg", [32, T], F32)
        onesb1 = sb1("onesb1", [128, 1], BF16)
        pb.op("DVE", lambda e: e.memset(onesb1[:, :], 1.0), writes=["onesb1"], tag="ones1b")
        epst, epsk = cx.cc(1e-5, 1)

        XN = [("xnsl", c) for c in range(DCH)]
        for ts in range(4):
            for c in range(DCH):
                pb.op("SP", lambda e, c=c, ts=ts: e.dma_start(
                    out=xsl[:, c * 512:(c + 1) * 512],
                    in_=xres[c * 128:(c + 1) * 128, ts * 512:(ts + 1) * 512]),
                    writes=[("xsl", c)], dma=True, chain="xsl", tag=f"x1_{ts}_{c}")
            for c in range(DCH):
                par = c % 2
                pb.op("ACT", lambda e, c=c, par=par: e.activation(
                    sq1[:, par * 512:(par + 1) * 512], xsl[:, c * 512:(c + 1) * 512], AF.Square),
                    reads=[("xsl", c)], writes=[("sq1", par)], tag=f"q1_{ts}_{c}")
                pb.op("PE", lambda e, c=c, par=par: e.matmul(
                    dps[2][0:1, :], onesb1[:, 0:1], sq1[:, par * 512:(par + 1) * 512],
                    start=(c == 0), stop=(c == DCH - 1)),
                    reads=[("sq1", par), "onesb1"], writes=[PSK(2)], tag=f"qm1_{ts}_{c}")
            pb.op("DVE", lambda e: e.tensor_scalar_mul(rws1[0:1, 0:512], dps[2][0:1, :], 1.0 / D),
                  reads=[PSK(2)], writes=["r1m"], tag=f"r1m{ts}")
            pb.op("ACT", lambda e: e.activation(rws1[0:1, 512:1024], rws1[0:1, 0:512], AF.Sqrt,
                                                bias=epst[0:1, 0:1]),
                  reads=["r1m", epsk], writes=["r1s"], tag=f"r1s{ts}")
            pb.op("DVE", lambda e: e.reciprocal(rws1[0:1, 0:512], rws1[0:1, 512:1024]),
                  reads=["r1s"], writes=["r1m"], tag=f"r1r{ts}")
            pb.op("PE", lambda e: e.matmul(dps[3][:, :], ln.ones1f[0:1, :], rws1[0:1, 0:512],
                                           start=True, stop=True),
                  reads=["r1m", "ln_ones1f"], writes=[PSK(3)], tag=f"r1b{ts}")
            for c in range(DCH):
                pb.op("DVE", lambda e, c=c, ts=ts: e.scalar_tensor_tensor(
                    xnsl[:, c * 512:(c + 1) * 512],
                    xsl[:, c * 512:(c + 1) * 512], ccol(c), dps[3][:, :], OP.mult, OP.mult),
                    reads=[("xsl", c), PSK(3)] + CST, writes=[("xnsl", c)], tag=f"xn{ts}_{c}")

            for oc in range(34):
                wb = wpb[oc % 2]
                pb.op("POOL", lambda e, oc=oc, wb=wb, ts=ts: e.dma_start(out=wb[:, :], in_=w_in[oc]),
                      writes=[("wpb", oc % 2)], dma=True, chain="winl", tag=f"winl{ts}_{oc}")
                bank, bk = dps[oc % 2], PSK(oc % 2)

                def mm(e, wb=wb, bank=bank):
                    last = None
                    for k in range(16):
                        last = e.matmul(bank[:, :], wb[:, k * 128:(k + 1) * 128],
                                        xnsl[:, k * 512:(k + 1) * 512],
                                        start=(k == 0), stop=(k == 15))
                    return last
                pb.op("PE", mm, reads=[("wpb", oc % 2)] + XN, writes=[bk], tag=f"ipm{oc}_{ts}")
                if oc < 16:
                    pb.op("ACT", lambda e, bank=bank: e.activation(
                        zstg[:, :], bank[:, :], AF.Copy),
                        reads=[bk], writes=["zstg"], tag=f"zev{oc}_{ts}")
                    pb.op("SP", lambda e, oc=oc, ts=ts: e.dma_start(
                        out=z_scr[oc, :, ts * 512:(ts + 1) * 512], in_=zstg[:, :]),
                        reads=["zstg"], writes=[("zs", oc)], dma=True, chain="zst",
                        tag=f"zst{ts}_{oc}")
                elif oc < 33:
                    pb.op("ACT", lambda e, bank=bank, ts=ts, oc=oc: e.activation(
                        xBC[:, (oc - 16) * T + ts * 512:(oc - 16) * T + (ts + 1) * 512],
                        bank[:, :], AF.Copy),
                        reads=[bk], writes=[("xbc", oc - 16)], tag=f"bev{oc}_{ts}")
                else:
                    pb.op("DVE", lambda e, bank=bank, ts=ts: e.tensor_copy(
                        dstg[:, ts * 512:(ts + 1) * 512], bank[0:32, :]),
                        reads=[bk], writes=[("dstg", ts)], tag=f"dev{ts}")

        dtb = dstg
        pb.op("DVE", lambda e: e.tensor_scalar_add(dstg[:, :], dstg[:, :], cco[0:32, 66:67]),
              reads=[("dstg", t_) for t_ in range(4)] + CST, writes=["dtb"], tag="dtb")
        ab1 = sb1("ab1", [32, T], F32)
        ab2 = sb1("ab2", [32, T], F32)
        pb.op("DVE", lambda e: e.scalar_tensor_tensor(ab1[:, :], dstg[:, :], -1.0, dstg[:, :],
                                                      OP.mult, OP.max),
              reads=["dtb"], writes=["ab1"], tag="dtabs")
        pb.op("ACT", lambda e: e.activation(ab2[:, :], ab1[:, :], AF.Exp, scale=-1.0),
              reads=["ab1"], writes=["ab2"], tag="dtexp")
        pb.op("ACT", lambda e: e.activation(ab1[:, :], ab2[:, :], AF.Ln, bias=1.0),
              reads=["ab2"], writes=["ab1"], tag="dtln")
        pb.op("DVE", lambda e: e.tensor_scalar_max(ab2[:, :], dstg[:, :], 0.0),
              reads=["dtb"], writes=["ab2"], tag="dtrelu")
        pb.op("DVE", lambda e: e.tensor_add(ab1[:, :], ab1[:, :], ab2[:, :]),
              reads=["ab1", "ab2"], writes=["dt"], tag="dtsum")
        pb.op("DVE", lambda e: e.tensor_scalar_mul(stk[0:32, :], ab1[:, :], acol[:, :]),
              reads=["dt", "acol"], writes=["stk0"], tag="adt")
        pb.op("ACT", lambda e: e.activation(stk[32:64, :], ab1[:, :], AF.Ln),
              reads=["dt"], writes=["stk1"], tag="lndt")

        slv = sb1("slv", [128, 51], BF16)
        extb = sb1("extb", [128, 51], BF16)
        for cc_ in range(17):
            pb.op("DVE", lambda e, cc_=cc_: e.tensor_copy(
                slv[:, cc_ * 3:(cc_ + 1) * 3], xBC[:, cc_ * T + T - 3: cc_ * T + T]),
                reads=[("xbc", cc_)], writes=[("slv", cc_)], tag=f"slv{cc_}")
        pb.op("POOL", lambda e: e.dma_start(out=cc_c_in, in_=slv[:, :]),
              reads=[("slv", i) for i in range(17)], writes=["ccci"], dma=True,
              chain="ccc", tag="cccst")
        pb.op("POOL", lambda e: e.collective_compute(
            "AllGather", OP.bypass, replica_groups=PAIRS,
            ins=[cc_c_in.opt()], outs=[cc_c_out.opt()]),
            reads=["ccci"], writes=["ccco"], coll=True, chain="cccg", tag="cccg")
        pb.op("POOL", lambda e: e.dma_start(out=extb[:, :], in_=cc_c_out[0:128, :]),
              reads=["ccco"], writes=["extraw"], dma=True, chain="cccl", tag="cccl")
        pb.op("DVE", lambda e: e.tensor_scalar_mul(extb[:, :], extb[:, :], cSEL[:, :]),
              reads=["extraw"] + CST, writes=["extb"], tag="extsel")

        ext = sb1("ext", [128, 3 + T], BF16)
        cacc = sb1("cacc", [128, T], F32)
        for cc_ in range(17):
            pb.op("DVE", lambda e, cc_=cc_: e.tensor_copy(ext[:, 0:3], extb[:, cc_ * 3:(cc_ + 1) * 3]),
                  reads=["extb"], writes=["ext3"], tag=f"cx3_{cc_}")
            pb.op("DVE", lambda e, cc_=cc_: e.tensor_copy(ext[:, 3:3 + T], xBC[:, cc_ * T:(cc_ + 1) * T]),
                  reads=[("xbc", cc_)], writes=["extT"], tag=f"cxT_{cc_}")
            pb.op("DVE", lambda e, cc_=cc_: e.tensor_scalar_mul(
                cacc[:, :], ext[:, 0:T], cvw[:, cc_ * 4:cc_ * 4 + 1]),
                reads=["ext3", "extT"] + CST, writes=["cacc"], tag=f"cv0_{cc_}")
            for kk in range(1, 4):
                pb.op("DVE", lambda e, cc_=cc_, kk=kk: e.scalar_tensor_tensor(
                    cacc[:, :], ext[:, kk:kk + T], cvw[:, cc_ * 4 + kk:cc_ * 4 + kk + 1],
                    cacc[:, :], OP.mult, OP.add),
                    reads=["ext3", "extT", "cacc"] + CST, writes=["cacc"], tag=f"cv{kk}_{cc_}")
            pb.op("ACT", lambda e, cc_=cc_: e.activation(
                xBC[:, cc_ * T:(cc_ + 1) * T], cacc[:, :], AF.Silu, bias=ccol(48 + cc_)),
                reads=["cacc"] + CST, writes=[("xbc", cc_)], tag=f"cvs_{cc_}")
        pb.full_barrier("end1")

    es2 = ExitStack()
    with es2:
        def sb2(nm, sh, dt=F32):
            return es2.enter_context(nc.sbuf_tensor(nm, sh, dt, allow_name_mangling=True))
        ptbs = [es2.enter_context(nc.psum_tensor(f"ptb{i}", [128, 512], BF16, allow_name_mangling=True)) for i in range(2)]
        XT = sb2("XT", [128, 2048], BF16)
        BCT = sb2("BCT", [128, 128], BF16)
        ST = sb2("ST", [64, NH * DS], F32)
        STst = sb2("STst", [64, NH * DS], BF16)
        STin = sb2("STin", [64, NH * DS], BF16)
        STlo = sb2("STlo", [64, NH * DS], BF16)
        ybuf = sb2("ybuf", [128, 2048], BF16)
        yz = sb2("yz", [128, 2048], F32)
        zbuf = sb2("zbuf", [128, 2048], BF16)
        szb = sb2("szb", [128, 2048], F32)
        sqz = sb2("sqz", [128, 2 * 128], BF16)
        Cb0 = sb2("Cb0", [64, 128], BF16)
        adlT = sb2("adlT", [128, 64], F32)
        cumAT = sb2("cumAT", [128, 32], F32)
        vwt = sb2("vwt", [128, 32], F32)
        bcolt = sb2("bcolt3", [128, 32], F32)
        cumaF = sb2("cumaF", [32, 128], F32)
        crowsb = sb2("crowsb", [1, 128], F32)
        msk3 = sb2("msk3", [128, 128], F32)
        ecb = sb2("ecb", [64, 128], F32)
        Wt = sb2("Wt", [128, 128], BF16)
        Cw = sb2("Cw", [64, 128], BF16)
        Bw = sb2("Bw", [128, 64], BF16)
        CBc = sb2("CBc", [128, 128], F32)
        dabc = sb2("dabc", [64, 32], F32)
        pbc = sb2("pbc", [64, 32], F32)
        dcol = sb2("dcol", [128, 32], F32)
        exr = sb2("exr", [1, 32], F32)
        logp = sb2("logp", [1, 32], F32)
        rws3 = sb2("rws3", [1, 2 * 128], F32)
        rbz = sb2("rbz", [128, 128], F32)
        onesb3 = sb2("onesb3", [128, 1], BF16)
        onesf3 = sb2("onesf3", [128, 1], F32)
        pb.op("DVE", lambda e: e.memset(onesb3[:, :], 1.0), writes=["onesb3"], tag="o3b")
        pb.op("DVE", lambda e: e.memset(onesf3[:, :], 1.0), writes=["onesf3"], tag="o3f")
        pb.op("DVE", lambda e: e.memset(logp[:, :], 0.0), writes=["logp"], tag="lp0")
        pb.op("DVE", lambda e: e.memset(ST[:, :], 0.0), writes=["ST"], tag="st0")

        pb.op("PE", lambda e: e.matmul(dps[4][:, 0:32], ln.ones1f[0:1, :], cdr[0:1, :],
                                       start=True, stop=True),
              reads=CST + ["ln_ones1f"], writes=[PSK(4)], tag="dbc")
        pb.op("DVE", lambda e: e.tensor_copy(dcol[:, :], dps[4][:, 0:32]),
              reads=[PSK(4)], writes=["dcol"], tag="dbce")

        def sschunk(c, pass2):
            sfx = f"p{2 if pass2 else 1}_{c}"
            pb.op("PE", lambda e, c=c: e.transpose(
                dps[0][:, 0:64], stk[:, c * Q:(c + 1) * Q], cIDF[0:64, 0:64]),
                reads=["stk0", "stk1"] + CST, writes=[PSK(0)], tag=f"adl{sfx}")
            pb.op("DVE", lambda e: e.tensor_copy(adlT[:, :], dps[0][:, 0:64]),
                  reads=[PSK(0)], writes=["adlT"], tag=f"adle{sfx}")
            if not pass2:
                pb.op("PE", lambda e: e.matmul(dps[0][:, 0:32], cTRIU[:, :], adlT[:, 0:32],
                                               start=True, stop=True),
                      reads=["adlT"] + CST, writes=[PSK(0)], tag=f"rcu{sfx}")
                pb.op("DVE", lambda e: e.tensor_add(vwt[:, :], dps[0][:, 0:32], adlT[:, 32:64]),
                      reads=[PSK(0), "adlT"], writes=["vwt"], tag=f"vw1{sfx}")
                pb.op("ACT", lambda e: e.activation(vwt[:, :], vwt[:, :], AF.Exp),
                      reads=["vwt"], writes=["vwt"], tag=f"vw2{sfx}")
            else:
                pb.op("PE", lambda e: e.matmul(dps[0][:, 0:32], cTRIL[:, :], adlT[:, 0:32],
                                               start=True, stop=True),
                      reads=["adlT"] + CST, writes=[PSK(0)], tag=f"cma{sfx}")
                pb.op("DVE", lambda e: e.tensor_copy(cumAT[:, :], dps[0][:, 0:32]),
                      reads=[PSK(0)], writes=["cumAT"], tag=f"cme{sfx}")
                pb.op("DVE", lambda e: e.tensor_sub(bcolt[:, :], adlT[:, 32:64], cumAT[:, :]),
                      reads=["adlT", "cumAT"], writes=["bcolt3"], tag=f"bct{sfx}")
                pb.op("PE", lambda e: e.transpose(dps[1][0:32, 0:128], cumAT[:, :], cIDF[:, :]),
                      reads=["cumAT"] + CST, writes=[PSK(1)], tag=f"cmf{sfx}")
                pb.op("DVE", lambda e: e.tensor_copy(cumaF[:, :], dps[1][0:32, 0:128]),
                      reads=[PSK(1)], writes=["cumaF"], tag=f"cmfe{sfx}")
            pb.op("PE", lambda e: e.matmul(dps[1][0:1, 0:32], onesf3[:, :], adlT[:, 0:32],
                                           start=True, stop=True),
                  reads=["adlT", "onesf3"], writes=[PSK(1)], tag=f"tot{sfx}")
            if not pass2:
                pb.op("ACT", lambda e: e.activation(exr[0:1, :], dps[1][0:1, 0:32], AF.Exp),
                      reads=[PSK(1)], writes=["exr"], tag=f"totx{sfx}")
                pb.op("PE", lambda e: e.matmul(dps[1][0:64, 0:32], ln.ones1f[0:1, 0:64],
                                               exr[0:1, :], start=True, stop=True),
                      reads=["exr", "ln_ones1f"], writes=[PSK(1)], tag=f"dab{sfx}")
                pb.op("DVE", lambda e: e.tensor_copy(dabc[:, :], dps[1][0:64, 0:32]),
                      reads=[PSK(1)], writes=["dabc"], tag=f"dabe{sfx}")
            else:
                pb.op("PE", lambda e: e.matmul(dps[1][0:64, 32:64], ln.ones1f[0:1, 0:64],
                                               logp[0:1, :], start=True, stop=True),
                      reads=["logp", "ln_ones1f"], writes=[PSK(1)], tag=f"lpb{sfx}")
                pb.op("ACT", lambda e: e.activation(pbc[:, :], dps[1][0:64, 32:64], AF.Exp),
                      reads=[PSK(1)], writes=["pbc"], tag=f"pbx{sfx}")
                pb.op("DVE", lambda e: e.tensor_add(logp[0:1, :], logp[0:1, :], dps[1][0:1, 0:32]),
                      reads=["logp", PSK(1)], writes=["logp"], tag=f"lpu{sfx}")

            if not pass2:
                for f in range(16):
                    pt = ptbs[f % 2][:, 0:128]
                    pb.op("PE", lambda e, f=f, c=c, pt=pt: e.transpose(
                        pt, xBC[:, f * T + c * Q:f * T + (c + 1) * Q], cIDB[:, :]),
                        reads=[("xbc", f)] + CST, writes=[PSK(6 + f % 2)], tag=f"xtr{sfx}_{f}")
                    pb.op("DVE", lambda e, f=f, pt=pt: e.tensor_copy(
                        XT[:, f * 128:(f + 1) * 128], pt),
                        reads=[PSK(6 + f % 2)], writes=[("XT", f)], tag=f"xte{sfx}_{f}")
                pt = ptbs[0][:, 0:128]
                pb.op("PE", lambda e, c=c, pt=pt: e.transpose(
                    pt, xBC[:, 16 * T + c * Q:16 * T + (c + 1) * Q], cIDB[:, :]),
                    reads=[("xbc", 16)] + CST, writes=[PSK(6)], tag=f"btr{sfx}")
                pb.op("DVE", lambda e, pt=pt: e.tensor_copy(BCT[:, :], pt),
                      reads=[PSK(6)], writes=["BCT"], tag=f"bte{sfx}")
                pb.op("SP", lambda e, c=c: e.dma_start(out=xct_scr[c], in_=XT[:, :]),
                      reads=[("XT", f) for f in range(16)], writes=[("xct", c)],
                      dma=True, chain="xctst", tag=f"xcts{sfx}")
                pb.op("SP", lambda e, c=c: e.dma_start(out=bct_scr[c], in_=BCT[:, :]),
                      reads=["BCT"], writes=[("bct", c)], dma=True, chain="bctst", tag=f"bcts{sfx}")
                pb.op("DVE", lambda e: e.tensor_copy(STst[:, :], ST[:, :]),
                      reads=["ST"], writes=["STst"], tag=f"stc{sfx}")
                pb.op("SP", lambda e, c=c: e.dma_start(out=stl_scr[c], in_=STst[:, :]),
                      reads=["STst"], writes=[("stl", c)], dma=True, chain="stlst", tag=f"stls{sfx}")
            else:
                pb.op("SP", lambda e, c=c: e.dma_start(out=XT[:, :], in_=xct_scr[c]),
                      reads=[("xct", c)], writes=[("XT", f_) for f_ in range(16)],
                      dma=True, chain="xctld", tag=f"xctl{sfx}")
                pb.op("SP", lambda e, c=c: e.dma_start(out=BCT[:, :], in_=bct_scr[c]),
                      reads=[("bct", c)], writes=["BCT"], dma=True, chain="bctld", tag=f"bctl{sfx}")
                pb.op("SP", lambda e, c=c: e.dma_start(out=STlo[:, :], in_=stl_scr[c]),
                      reads=[("stl", c)], writes=["STlo"], dma=True, chain="stlld", tag=f"stll{sfx}")
                pb.op("PE", lambda e: e.transpose(ptbs[0][0:64, 0:128], BCT[:, 64:128], cIDB[:, :]),
                      reads=["BCT"] + CST, writes=[PSK(6)], tag=f"cb0t{sfx}")
                pb.op("DVE", lambda e: e.tensor_copy(Cb0[:, :], ptbs[0][0:64, 0:128]),
                      reads=[PSK(6)], writes=["Cb0"], tag=f"cb0e{sfx}")
                pb.op("PE", lambda e, c=c: e.matmul(
                    dps[0][:, 0:128], xBC[0:64, 16 * T + c * Q:16 * T + (c + 1) * Q],
                    Cb0[:, :], start=True, stop=True),
                    reads=[("xbc", 16), "Cb0"], writes=[PSK(0)], tag=f"cb{sfx}")
                pb.op("DVE", lambda e: e.tensor_mul(CBc[:, :], dps[0][:, 0:128], cC01[:, :]),
                      reads=[PSK(0)] + CST, writes=["CBc"], tag=f"cbc{sfx}")

            for h in range(NH):
                hb = 64 * (h % 2)
                fch = h // 2
                if not pass2:
                    pb.op("DVE", lambda e, h=h: e.tensor_scalar_mul(
                        Bw[:, :], BCT[:, 0:64], vwt[:, h:h + 1]),
                        reads=["BCT", "vwt"], writes=["Bw"], tag=f"bw{sfx}_{h}")
                    pb.op("PE", lambda e, h=h: e.matmul(
                        dps[2][0:64, 0:64], Bw[:, :], XT[:, h * 64:(h + 1) * 64],
                        start=True, stop=True),
                        reads=["Bw"] + [("XT", f_) for f_ in range(16)], writes=[PSK(2)],
                        tag=f"dst{sfx}_{h}")
                    pb.op("DVE", lambda e, h=h: e.scalar_tensor_tensor(
                        ST[:, h * DS:(h + 1) * DS], ST[:, h * DS:(h + 1) * DS],
                        dabc[:, h:h + 1], dps[2][0:64, 0:64], OP.mult, OP.add),
                        reads=["ST", "dabc", PSK(2)], writes=["ST"], tag=f"stu{sfx}_{h}")
                else:
                    pb.op("PE", lambda e, h=h: e.matmul(
                        dps[1][0:1, 0:128], cIDF[0:32, h:h + 1], cumaF[:, :],
                        start=True, stop=True),
                        reads=["cumaF"] + CST, writes=[PSK(1)], tag=f"cre{sfx}_{h}")
                    pb.op("DVE", lambda e: e.tensor_copy(crowsb[:, :], dps[1][0:1, 0:128]),
                          reads=[PSK(1)], writes=["crowsb"], tag=f"crc{sfx}_{h}")
                    pb.op("PE", lambda e: e.matmul(dps[1][:, 128:256], ln.ones1f[0:1, :],
                                                   crowsb[0:1, :], start=True, stop=True),
                          reads=["crowsb", "ln_ones1f"], writes=[PSK(1)], tag=f"gbc{sfx}_{h}")
                    pb.op("PE", lambda e: e.matmul(dps[1][0:64, 256:384], ln.ones1f[0:1, 0:64],
                                                   crowsb[0:1, :], start=True, stop=True),
                          reads=["crowsb", "ln_ones1f"], writes=[PSK(1)], tag=f"g2b{sfx}_{h}")
                    pb.op("DVE", lambda e, h=h: e.tensor_scalar(
                        msk3[:, :], dps[1][:, 128:256], bcolt[:, h:h + 1], 30.0,
                        OP.add, OP.min),
                        reads=[PSK(1), "bcolt3"], writes=["msk3"], tag=f"mka{sfx}_{h}")
                    pb.op("ACT", lambda e: e.activation(msk3[:, :], msk3[:, :], AF.Exp),
                        reads=["msk3"], writes=["msk3"], tag=f"mk{sfx}_{h}")
                    pb.op("DVE", lambda e: e.tensor_mul(Wt[:, :], msk3[:, :], CBc[:, :]),
                          reads=["msk3", "CBc"], writes=["Wt"], tag=f"wt{sfx}_{h}")
                    pb.op("ACT", lambda e: e.activation(ecb[:, :], dps[1][0:64, 256:384], AF.Exp),
                          reads=[PSK(1)], writes=["ecb"], tag=f"ec{sfx}_{h}")
                    pb.op("DVE", lambda e: e.tensor_mul(Cw[:, :], Cb0[:, :], ecb[:, :]),
                        reads=["Cb0", "ecb"], writes=["Cw"], tag=f"cw{sfx}_{h}")
                    pb.op("DVE", lambda e, h=h: e.scalar_tensor_tensor(
                        STst[:, h * DS:(h + 1) * DS], STin[:, h * DS:(h + 1) * DS],
                        pbc[:, h:h + 1], STlo[:, h * DS:(h + 1) * DS], OP.mult, OP.add),
                        reads=["STin", "pbc", "STlo"], writes=["STst"], tag=f"sef{sfx}_{h}")

                    def ymm(e, h=h, hb=hb):
                        tp = (0, hb)
                        e.matmul(dps[2][hb:hb + 64, 0:128], STst[:, h * DS:(h + 1) * DS],
                                 Cw[:, :], start=True, stop=False, tile_position=tp)
                        return e.matmul(dps[2][hb:hb + 64, 0:128], XT[:, h * 64:(h + 1) * 64],
                                        Wt[:, :], start=False, stop=True, tile_position=tp)
                    pb.op("PE", ymm, reads=["STst", "Cw", "Wt"] + [("XT", f_) for f_ in range(16)],
                          writes=[PSK(2)], tag=f"ym{sfx}_{h}")
                    pb.op("DVE", lambda e, h=h, hb=hb, fch=fch, c=c: e.scalar_tensor_tensor(
                        ybuf[hb:hb + 64, fch * 128:(fch + 1) * 128],
                        xBC[hb:hb + 64, fch * T + c * Q:(fch) * T + (c + 1) * Q],
                        dcol[hb:hb + 64, h:h + 1], dps[2][hb:hb + 64, 0:128], OP.mult, OP.add),
                        reads=[("xbc", fch), "dcol", PSK(2)], writes=[("ybuf", h)],
                        tag=f"ye{sfx}_{h}")

            if pass2:
                pb.op("SP", lambda e, c=c: e.dma_start(
                    out=zbuf[:, :],
                    in_=z_scr.rearrange("k p (c j) -> c k p j", j=128)[c].rearrange("k p j -> p k j")),
                    reads=[("zs", k_) for k_ in range(16)], writes=["zbuf"], dma=True,
                    chain="zld", tag=f"zl{sfx}")
                pb.op("ACT", lambda e: e.activation(szb[:, :], zbuf[:, :], AF.Silu),
                      reads=["zbuf"], writes=["szb"], tag=f"sz{sfx}")
                pb.op("DVE", lambda e: e.tensor_tensor(yz[:, :], ybuf[:, :], szb[:, :], OP.mult),
                      reads=[("ybuf", h_) for h_ in range(NH)] + ["szb"], writes=["yz"],
                      tag=f"yz{sfx}")
                for f in range(16):
                    par = f % 2
                    pb.op("ACT", lambda e, f=f, par=par: e.activation(
                        sqz[:, par * 128:(par + 1) * 128], yz[:, f * 128:(f + 1) * 128],
                        AF.Square),
                        reads=["yz"], writes=[("szq", par)], tag=f"yq{sfx}_{f}")
                    pb.op("PE", lambda e, f=f, par=par: e.matmul(
                        dps[0][0:1, 0:128], onesb3[:, 0:1],
                        sqz[:, par * 128:(par + 1) * 128], start=(f == 0), stop=(f == 15)),
                        reads=[("szq", par), "onesb3"], writes=[PSK(0)], tag=f"yqm{sfx}_{f}")
                pb.op("DVE", lambda e: e.tensor_scalar_mul(rws3[0:1, 0:128], dps[0][0:1, 0:128],
                                                           1.0 / 2048.0),
                      reads=[PSK(0)], writes=["r3m"], tag=f"r3m{sfx}")
                epst3, epsk3 = cx.cc(1e-5, 1)
                pb.op("ACT", lambda e: e.activation(rws3[0:1, 128:256], rws3[0:1, 0:128],
                                                    AF.Sqrt, bias=epst3[0:1, 0:1]),
                      reads=["r3m", epsk3], writes=["r3s"], tag=f"r3s{sfx}")
                pb.op("DVE", lambda e: e.reciprocal(rws3[0:1, 0:128], rws3[0:1, 128:256]),
                      reads=["r3s"], writes=["r3m"], tag=f"r3r{sfx}")
                pb.op("PE", lambda e: e.matmul(dps[0][:, 0:128], ln.ones1f[0:1, :],
                                               rws3[0:1, 0:128], start=True, stop=True),
                      reads=["r3m", "ln_ones1f"], writes=[PSK(0)], tag=f"r3b{sfx}")
                pb.op("DVE", lambda e: e.tensor_copy(rbz[:, :], dps[0][:, 0:128]),
                      reads=[PSK(0)], writes=["rbz"], tag=f"r3be{sfx}")
                for f in range(16):
                    pb.op("DVE", lambda e, f=f: e.scalar_tensor_tensor(
                        ybuf[:, f * 128:(f + 1) * 128], yz[:, f * 128:(f + 1) * 128],
                        ccol(32 + f), rbz[:, :], OP.mult, OP.mult),
                        reads=["yz", "rbz"] + CST, writes=[("ybuf", 2 * f), ("ybuf", 2 * f + 1)],
                        tag=f"yn{sfx}_{f}")
                pb.op("SP", lambda e, c=c: e.dma_start(out=yn_scr[c], in_=ybuf[:, :]),
                      reads=[("ybuf", h_) for h_ in range(NH)], writes=[("yns", c)],
                      dma=True, chain="ynst", tag=f"yns{sfx}")

        for c in range(NCH):
            sschunk(c, pass2=False)
        pb.op("DVE", lambda e: e.tensor_copy(STst[:, :], ST[:, :]),
              reads=["ST"], writes=["STst"], tag="stfin")
        pb.op("POOL", lambda e: e.dma_start(out=cc_s_in, in_=STst[:, :]),
              reads=["STst"], writes=["ccsi"], dma=True, chain="ccs", tag="ccsst")
        pb.op("POOL", lambda e: e.collective_compute(
            "AllGather", OP.bypass, replica_groups=PAIRS,
            ins=[cc_s_in.opt()], outs=[cc_s_out.opt()]),
            reads=["ccsi"], writes=["ccso"], coll=True, chain="ccsg", tag="ccsg")
        pb.op("POOL", lambda e: e.dma_start(out=STin[:, :], in_=cc_s_out[0:64, :]),
              reads=["ccso"], writes=["STinr"], dma=True, chain="ccsl", tag="ccsl")
        pb.op("DVE", lambda e: e.tensor_scalar_mul(STin[:, :], STin[:, :], cSEL[0:64, :]),
              reads=["STinr"] + CST, writes=["STin"], tag="stinsel")
        for c in range(NCH):
            sschunk(c, pass2=True)
        pb.full_barrier("end2")
    esB.__exit__(None, None, None)

    es3 = ExitStack()
    with es3:
        def sb3(nm, sh, dt=F32):
            return es3.enter_context(nc.sbuf_tensor(nm, sh, dt, allow_name_mangling=True))
        dps67 = [es3.enter_context(nc.psum_tensor(f"dpsX{i}", [128, 512], F32, allow_name_mangling=True)) for i in range(2)]
        dps8 = dps + dps67
        xbf2 = sb3("xbf2", [128, DCH * T], BF16)
        s_tile = sb3("s_tile", [128, DCH * 512], F32)
        xmidb = sb3("xmidb", [128, DCH * 512], F32)
        es3a = ExitStack()
        es3a.__enter__()
        def sb3a(nm, sh, dt=F32):
            return es3a.enter_context(nc.sbuf_tensor(nm, sh, dt, allow_name_mangling=True))
        wob = [sb3a(f"wob{i}", [128, 16 * 128], BF16) for i in range(2)]
        sq3 = sb3a("sq3", [128, 2 * 512], BF16)
        rws4 = sb3a("rws4", [1, 2 * 512], F32)
        onesb4 = sb3a("onesb4", [128, 1], BF16)
        pb.op("DVE", lambda e: e.memset(onesb4[:, :], 1.0), writes=["onesb4"], tag="o4b")
        epst, epsk = cx.cc(1e-5, 1)
        XB2 = [("xbf2", c) for c in range(DCH)]

        ynts = sb3a("ynts", [128, 16 * 512], BF16)
        for ts in range(4):
            for c in range(DCH):
                pb.op("SP", lambda e, c=c, ts=ts: e.dma_start(
                    out=xmidb[:, c * 512:(c + 1) * 512],
                    in_=xres[c * 128:(c + 1) * 128, ts * 512:(ts + 1) * 512]),
                    writes=[("xmidb", c)], dma=True, chain="xold", tag=f"xo3_{ts}_{c}")
            for k in range(16):
                src = yn_scr.rearrange("c p (k j) -> k p c j", j=128)[k][:, ts * 4:(ts + 1) * 4, :]
                pb.op("SP", lambda e, k=k, src=src: e.dma_start(
                    out=ynts[:, k * 512:(k + 1) * 512].rearrange("p (c j) -> p c j", j=128),
                    in_=src),
                    reads=[("yns", cc2) for cc2 in range(ts * 4, (ts + 1) * 4)],
                    writes=[("ynts", k)], dma=True, chain="ynld", tag=f"ynl{ts}_{k}")
            for oc in range(16):
                wb = wob[oc % 2]
                pb.op("POOL", lambda e, oc=oc, wb=wb: e.dma_start(out=wb[:, :], in_=w_op[oc]),
                      writes=[("wob", oc % 2)], dma=True, chain="wopl", tag=f"wop{ts}_{oc}")
                bank, bk = dps8[oc % 2], PSK(oc % 2)

                def mmo(e, wb=wb, bank=bank):
                    last = None
                    for k in range(16):
                        last = e.matmul(bank[:, :], wb[:, k * 128:(k + 1) * 128],
                                        ynts[:, k * 512:(k + 1) * 512],
                                        start=(k == 0), stop=(k == 15))
                    return last
                pb.op("PE", mmo, reads=[("wob", oc % 2)] + [("ynts", k_) for k_ in range(16)],
                      writes=[bk], tag=f"opm{ts}_{oc}")
                pb.op("DVE", lambda e, bank=bank, oc=oc: e.tensor_add(
                    xmidb[:, oc * 512:(oc + 1) * 512], bank[:, :], xmidb[:, oc * 512:(oc + 1) * 512]),
                    reads=[bk, ("xmidb", oc)], writes=[("xmidb", oc)], tag=f"xm{ts}_{oc}")
            for c in range(DCH):
                pb.op("SP", lambda e, c=c, ts=ts: e.dma_start(
                    out=xmid_scr[c * 128:(c + 1) * 128, ts * 512:(ts + 1) * 512],
                    in_=xmidb[:, c * 512:(c + 1) * 512]),
                    reads=[("xmidb", c)], writes=[("xms", ts, c)], dma=True, chain="xmst",
                    tag=f"xms{ts}_{c}")
            for c in range(DCH):
                par = c % 2
                pb.op("ACT", lambda e, c=c, par=par: e.activation(
                    sq3[:, par * 512:(par + 1) * 512], xmidb[:, c * 512:(c + 1) * 512], AF.Square),
                    reads=[("xmidb", c)], writes=[("sq3", par)], tag=f"q3_{ts}_{c}")
                pb.op("PE", lambda e, c=c, par=par: e.matmul(
                    dps8[2][0:1, :], onesb4[:, 0:1], sq3[:, par * 512:(par + 1) * 512],
                    start=(c == 0), stop=(c == DCH - 1)),
                    reads=[("sq3", par), "onesb4"], writes=[PSK(2)], tag=f"qm3_{ts}_{c}")
            pb.op("DVE", lambda e: e.tensor_scalar_mul(rws4[0:1, 0:512], dps8[2][0:1, :], 1.0 / D),
                  reads=[PSK(2)], writes=["r4m"], tag=f"r4m{ts}")
            pb.op("ACT", lambda e: e.activation(rws4[0:1, 512:1024], rws4[0:1, 0:512], AF.Sqrt,
                                                bias=epst[0:1, 0:1]),
                  reads=["r4m", epsk], writes=["r4s"], tag=f"r4s{ts}")
            pb.op("DVE", lambda e: e.reciprocal(rws4[0:1, 0:512], rws4[0:1, 512:1024]),
                  reads=["r4s"], writes=["r4m"], tag=f"r4r{ts}")
            pb.op("PE", lambda e: e.matmul(dps8[3][:, :], ln.ones1f[0:1, :], rws4[0:1, 0:512],
                                           start=True, stop=True),
                  reads=["r4m", "ln_ones1f"], writes=[PSK(3)], tag=f"r4b{ts}")
            for c in range(DCH):
                pb.op("DVE", lambda e, c=c, ts=ts: e.scalar_tensor_tensor(
                    xbf2[:, c * T + ts * 512: c * T + (ts + 1) * 512],
                    xmidb[:, c * 512:(c + 1) * 512], ccol(16 + c), dps8[3][:, :],
                    OP.mult, OP.mult),
                    reads=[("xmidb", c), PSK(3)] + CST, writes=[("xbf2", c)], tag=f"x2{ts}_{c}")

        pb.full_barrier("endOP")
        es3a.__exit__(None, None, None)
        for oc in range(16):
            dst = mwd_scr.rearrange("i p (o j) -> o i p j", j=128)[oc]
            srcap = mlp_wd[oc].rearrange("p (i j) -> i p j", j=128)
            pb.op("POOL", lambda e, dst=dst, srcap=srcap: e.dma_start(out=dst, in_=srcap),
                  writes=[("mwdbf", oc)], dma=True, chain="mwdc", tag=f"mwdc{oc}")
        pb.op("SP", lambda e: e.nop(), reads=[("mwdbf", oc) for oc in range(16)], tag="mwb")

        def pre_ts(ts):
            for c in range(DCH):
                pb.op("SP", lambda e, c=c, ts=ts: e.dma_start(
                    out=s_tile[:, c * 512:(c + 1) * 512],
                    in_=xres[c * 128:(c + 1) * 128, ts * 512:(ts + 1) * 512]),
                    writes=[("s", c)], dma=True, chain="xold", tag=f"xo4_{ts}_{c}")
                pb.op("SP", lambda e, c=c, ts=ts: e.dma_start(
                    out=xmidb[:, c * 512:(c + 1) * 512],
                    in_=xmid_scr[c * 128:(c + 1) * 128, ts * 512:(ts + 1) * 512]),
                    reads=[("xms", ts, c)],
                    writes=[("xmidb", c)], dma=True, chain="xmld", tag=f"xml{ts}_{c}")
                pb.op("DVE", lambda e, c=c: e.tensor_add(
                    xmidb[:, c * 512:(c + 1) * 512], xmidb[:, c * 512:(c + 1) * 512],
                    s_tile[:, c * 512:(c + 1) * 512]),
                    reads=[("xmidb", c), ("s", c)], writes=[("xmidb", c)], tag=f"cmb{ts}_{c}")

        def evac(ts, half):
            for o in range(8):
                c = half * 8 + o
                pb.op("DVE", lambda e, o=o, c=c: e.tensor_add(
                    s_tile[:, c * 512:(c + 1) * 512], dps8[o][:, :], xmidb[:, c * 512:(c + 1) * 512]),
                    reads=[PSK(o), ("xmidb", c)], writes=[("s", c)], tag=f"ms{ts}_{c}")

        def post_ts(ts):
            ln.emit(ts, s_tile, [("s", c) for c in range(DCH)], xout, okey, None, [],
                    tag=f"ln3_{ts}")

        saved_es = cx.es
        cx.es = es3
        emit_ffn(cx, "m", xbf2, XB2, lambda g: mlp_gu[g], mwd_scr, h_scr, True, None,
                 pre_ts, evac, post_ts, dps8)
        cx.es = saved_es
    esS.__exit__(None, None, None)


def build_stage3(nc, es):
    pb = PB(nc, es)
    cx = Ctx(nc, es, pb)
    A = declare_stage3(nc)
    xout = nc.dram_tensor("s3_xout", [D, T], F32, kind="ExternalOutput").ap()
    dps = [cx.ps(f"dps{i}", [128, 512], F32) for i in range(6)]
    PSK = lambda i: ("psum", i)
    ln = LNKit(cx, [(dps[2], PSK(2)), (dps[3], PSK(3)), (dps[4], PSK(4)), (dps[5], PSK(5))])
    emit_stage3(cx, A, dps, ln, xout, "s3xo")
    pb.op("POOL", lambda e: e.nop(),
          reads=[("s3xo", ts, c) for ts in range(4) for c in range(DCH)], tag="fin")
    pb.emit()
    return nc


def _s3_host_base(ln1_w, in_proj_w, conv_w, conv_b, A_log, D_param, dt_bias, mnorm_w,
                  out_proj_w, ln2_w, mlp_in_w, mlp_out_w):
    ipT = np.zeros((2048, 34 * 128), np.float32)
    ipT[:, :4256] = in_proj_w.T
    idx = np.arange(128)
    ccols = np.zeros((128, 128), np.float32)
    ccols[:, 0:16] = vecc(ln1_w).T
    ccols[:, 16:32] = vecc(ln2_w).T
    ccols[:, 32:48] = vecc(mnorm_w).T
    cb = conv_b.reshape(17, 128)
    ccols[:, 48:65] = cb.T
    ccols[0:32, 65] = A_log
    ccols[0:32, 66] = dt_bias
    convw = np.ascontiguousarray(conv_w.reshape(2176, 4).reshape(17, 128, 4).transpose(0, 2, 1).reshape(68, 128))
    TRILh = (idx[:, None] <= idx[None, :]).astype(np.float32)
    TRIUh = (idx[:, None] > idx[None, :]).astype(np.float32)
    C01h = (idx[None, :] >= idx[:, None]).astype(np.float32)
    IDFh = np.eye(128, dtype=np.float32)
    IDBh = np.eye(128).astype(BF)
    return {"s3_win": tile_lhsT(ipT), "s3_wop": tile_lhsT(np.ascontiguousarray(out_proj_w.T)),
            "s3_mgu": tile_lhsT(np.ascontiguousarray(mlp_in_w.T)),
            "s3_mwd": tile_lhsT(np.ascontiguousarray(mlp_out_w.T)),
            "s3_ccols": ccols, "s3_convw": convw,
            "s3_arow": np.asarray(A_log).reshape(1, 32).astype(np.float32),
            "s3_drow": np.asarray(D_param).reshape(1, 32).astype(np.float32),
            "s3_TRIL": TRILh, "s3_TRIU": TRIUh, "s3_C01": C01h, "s3_IDF": IDFh, "s3_IDB": IDBh}


def run_stage3(x, ln1_w, in_proj_w, conv_w, conv_b, A_log, D_param, dt_bias, mnorm_w,
               out_proj_w, ln2_w, mlp_in_w, mlp_out_w):
    nc = bass.Bass("TRN2", target_bir_lowering=False)
    es = ExitStack()
    with es:
        build_stage3(nc, es)
    base = _s3_host_base(ln1_w, in_proj_w, conv_w, conv_b, A_log, D_param, dt_bias,
                         mnorm_w, out_proj_w, ln2_w, mlp_in_w, mlp_out_w)
    xs = _shard_x_fm(x)
    ims = []
    for c in range(NCORES):
        im = dict(base)
        im["s3_xres"] = xs[c]
        im["s3_SEL"] = np.full((128, 1), float(c % 2), np.float32)
        ims.append(im)
    res = run_bass_kernel_spmd(nc, ims, core_ids=list(range(NCORES)))
    return _gather_fm(res, "s3_xout", x)



def _shard_x_fm(x):
    out = []
    for c in range(NCORES):
        b, h = c // 2, c % 2
        out.append(np.ascontiguousarray(x[b, h * T:(h + 1) * T, :].T))
    return out


def _gather_fm(res, key, x_like):
    out = np.empty_like(x_like)
    for c in range(NCORES):
        b, h = c // 2, c % 2
        out[b, h * T:(h + 1) * T, :] = res.results[c][key].T
    return out


def run_stage4(x, w_gate_up, w_down, debug=False):
    nc = bass.Bass("TRN2", target_bir_lowering=False)
    es = ExitStack()
    with es:
        build_stage4(nc, es)
    gu = tile_lhsT(np.ascontiguousarray(w_gate_up.T))
    wd = tile_lhsT(np.ascontiguousarray(w_down.T))
    xs = _shard_x_fm(x)
    ims = [{"s4_xres": xs[c], "s4_gu": gu, "s4_wd": wd} for c in range(NCORES)]
    res = run_bass_kernel_spmd(nc, ims, core_ids=list(range(NCORES)))
    return _gather_fm(res, "s4_xout", x)


def _quant_host(w_gate_up, w_proj):
    nc = bass.Bass("TRN2", target_bir_lowering=False)
    es = ExitStack()
    with es:
        build_quant(nc, es)
    guT = np.ascontiguousarray(w_gate_up.T)
    wpT = np.ascontiguousarray(w_proj.T)
    E = np.zeros((128, 2), BF)
    E[:64, 0] = 1
    E[64:, 1] = 1
    ET = np.zeros((2, 128), np.float32)
    ET[0, :64] = 1
    ET[1, 64:] = 1
    ims = []
    for c in range(NCORES):
        ims.append({"q_gu": np.ascontiguousarray(guT[:, c * 2048:(c + 1) * 2048]),
                    "q_wp": np.ascontiguousarray(wpT[:, c * 256:(c + 1) * 256]),
                    "q_E": E, "q_ET": ET})
    res = run_bass_kernel_spmd(nc, ims, core_ids=list(range(NCORES)))
    wq_guT = np.concatenate([res.results[c]["q_gu_out"] for c in range(NCORES)], axis=1)
    wq_pT = np.concatenate([res.results[c]["q_wp_out"] for c in range(NCORES)], axis=1)
    return wq_guT, wq_pT


def run_stage2(x, w_gate_up, w_proj):
    wq_guT, wq_pT = _quant_host(w_gate_up, w_proj)
    nc = bass.Bass("TRN2", target_bir_lowering=False)
    es = ExitStack()
    with es:
        build_stage2(nc, es)
    gu = tile_lhsT(wq_guT)
    wd = tile_lhsT(wq_pT)
    xs = _shard_x_fm(x)
    ims = [{"s2_xres": xs[c], "s2_gu": gu, "s2_wd": wd} for c in range(NCORES)]
    res = run_bass_kernel_spmd(nc, ims, core_ids=list(range(NCORES)))
    return _gather_fm(res, "s2_xout", x)


def _quant_E():
    E = np.zeros((128, 2), BF)
    E[:64, 0] = 1
    E[64:, 1] = 1
    ET = np.zeros((2, 128), np.float32)
    ET[0, :64] = 1
    ET[1, 64:] = 1
    return E, ET


def build_fused(nc, es):
    pb = PB(nc, es)
    cx = Ctx(nc, es, pb)
    x_in = nc.dram_tensor("x_in", [D, T], F32, kind="ExternalInput").ap()
    x1 = nc.dram_tensor("xr1", [D, T], F32, kind="Internal").ap()
    x2 = nc.dram_tensor("xr2", [D, T], F32, kind="Internal").ap()
    x3 = nc.dram_tensor("xr3", [D, T], F32, kind="Internal").ap()
    x_out = nc.dram_tensor("x_out", [D, T], F32, kind="ExternalOutput").ap()
    s4_gu = nc.dram_tensor("s4_gu", [128, 128, 16 * 128], F32, kind="ExternalInput").ap()
    s4_wd = nc.dram_tensor("s4_wd", [16, 128, 64 * 128], F32, kind="ExternalInput").ap()
    s4_h = nc.dram_tensor("s4_h", [64, 128, T], BF16, kind="Internal").ap()
    s4_wdbf = nc.dram_tensor("s4_wdbf", [64, 128, 16 * 128], BF16, kind="Internal").ap()
    s2_h = nc.dram_tensor("s2_h", [64, 128, T], BF16, kind="Internal").ap()
    s2_wdbf = nc.dram_tensor("s2_wdbf", [64, 128, 16 * 128], BF16, kind="Internal").ap()

    dps = [cx.ps(f"dps{i}", [128, 512], F32) for i in range(6)]
    PSK = lambda i: ("psum", i)
    ln = LNKit(cx, [(dps[2], PSK(2)), (dps[3], PSK(3)), (dps[4], PSK(4)), (dps[5], PSK(5))])
    cx.cc(1e-5, 1)

    Aq = declare_quant(nc, internal_out=True)
    A1 = declare_stage1(nc, xres=x_in)
    A3 = declare_stage3(nc, xres=x2)

    esQ = ExitStack()
    qstate = {}

    def emit_q():
        saved = cx.es
        cx.es = esQ
        qstate["deferred"] = []
        qstate["finish"] = emit_quant(cx, Aq, dps, gather=True, deferred=qstate["deferred"])
        cx.es = saved

    def post_proj():
        qstate["finish"]()
        esQ.__exit__(None, None, None)

    emit_stage1(cx, A1, dps, ln, x1, "s1xo",
                pre_ops=emit_q, post_proj=post_proj,
                interleave_state=qstate)
    pb.full_barrier("b12")

    gath_gu, gath_wp = Aq["gath_gu"], Aq["gath_wp"]

    def s2_gu_fn(oc):
        r, l = divmod(oc, 16)
        return gath_gu[r * 2048:(r + 1) * 2048, :].rearrange(
            "(k p) (l j) -> l p k j", p=128, j=128)[l]

    def s2_wd_fn(oc):
        r = oc // 2
        return gath_wp[r * 8192:(r + 1) * 8192,
                       (oc % 2) * 128:(oc % 2) * 128 + 128].rearrange("(i p) j -> i p j", p=128)

    emit_postln_ffn(cx, "s2", x1, x2, "s2xo", s2_gu_fn, s2_wd_fn, s2_wdbf, s2_h,
                    dps, ln, cast_w=False, rms_eps=F32EPS)
    pb.full_barrier("b23")
    emit_stage3(cx, A3, dps, ln, x3, "s3xo")
    pb.full_barrier("b34")
    emit_postln_ffn(cx, "s4", x3, x_out, "s4xo", lambda g: s4_gu[g],
                    lambda oc: s4_wd[oc].rearrange("p (i j) -> i p j", j=128),
                    s4_wdbf, s4_h, dps, ln, cast_w=True, rms_eps=None)
    pb.op("POOL", lambda e: e.nop(),
          reads=[("s4xo", ts, c) for ts in range(4) for c in range(DCH)], tag="fin")
    pb.emit()
    return nc


def run_fused(inputs):
    nc = bass.Bass("TRN2", target_bir_lowering=False)
    es = ExitStack()
    with es:
        build_fused(nc, es)
    x = np.asarray(inputs["L0_attn_x"], np.float32)
    base = {}
    base.update(_s1_host_base(inputs["L0_attn_wq"], inputs["L0_attn_bq"], inputs["L0_attn_wk"],
                              inputs["L0_attn_bk"], inputs["L0_attn_wv"], inputs["L0_attn_bv"],
                              inputs["L0_attn_wg"], inputs["L0_attn_bg"], inputs["L0_attn_wo"],
                              inputs["L0_attn_bo"], inputs["L0_attn_gn_w"],
                              inputs["L0_attn_gn_b"], inputs["L0_attn_gamma"]))
    base.update(_s3_host_base(inputs["L1_attn_ln1_w"], inputs["L1_attn_in_proj_w"],
                              inputs["L1_attn_conv_w"], inputs["L1_attn_conv_b"],
                              inputs["L1_attn_A_log"], inputs["L1_attn_D_param"],
                              inputs["L1_attn_dt_bias"], inputs["L1_attn_mnorm_w"],
                              inputs["L1_attn_out_proj_w"], inputs["L1_attn_ln2_w"],
                              inputs["L1_attn_mlp_in_w"], inputs["L1_attn_mlp_out_w"]))
    base["s4_gu"] = tile_lhsT(np.ascontiguousarray(np.asarray(inputs["L1_ffn_w_gate_up"]).T))
    base["s4_wd"] = tile_lhsT(np.ascontiguousarray(np.asarray(inputs["L1_ffn_w_down"]).T))
    E, ET = _quant_E()
    base["q_E"] = E
    base["q_ET"] = ET
    guT = np.ascontiguousarray(np.asarray(inputs["L0_ffn_w_gate_up"]).T)
    wpT = np.ascontiguousarray(np.asarray(inputs["L0_ffn_w_proj"]).T)
    xs = _shard_x_fm(x)
    ims = []
    for c in range(NCORES):
        im = dict(base)
        im["x_in"] = xs[c]
        im["q_gu"] = np.ascontiguousarray(guT[:, c * 2048:(c + 1) * 2048])
        im["q_wp"] = np.ascontiguousarray(wpT[:, c * 256:(c + 1) * 256])
        sel = np.full((128, 1), float(c % 2), np.float32)
        im["s1_SEL"] = sel
        im["s3_SEL"] = sel
        ims.append(im)
    res = run_bass_kernel_spmd(nc, ims, core_ids=list(range(NCORES)))
    return _gather_fm(res, "x_out", x)


def kernel(**inputs):
    return run_fused(inputs)


def kernel_multilaunch(**inputs):
    x = np.asarray(inputs["L0_attn_x"], np.float32)
    x = run_stage1(x, inputs["L0_attn_wq"], inputs["L0_attn_bq"], inputs["L0_attn_wk"],
                   inputs["L0_attn_bk"], inputs["L0_attn_wv"], inputs["L0_attn_bv"],
                   inputs["L0_attn_wg"], inputs["L0_attn_bg"], inputs["L0_attn_wo"],
                   inputs["L0_attn_bo"], inputs["L0_attn_gn_w"], inputs["L0_attn_gn_b"],
                   inputs["L0_attn_gamma"])
    x = run_stage2(x, inputs["L0_ffn_w_gate_up"], inputs["L0_ffn_w_proj"])
    x = run_stage3(x, inputs["L1_attn_ln1_w"], inputs["L1_attn_in_proj_w"],
                   inputs["L1_attn_conv_w"], inputs["L1_attn_conv_b"],
                   inputs["L1_attn_A_log"], inputs["L1_attn_D_param"],
                   inputs["L1_attn_dt_bias"], inputs["L1_attn_mnorm_w"],
                   inputs["L1_attn_out_proj_w"], inputs["L1_attn_ln2_w"],
                   inputs["L1_attn_mlp_in_w"], inputs["L1_attn_mlp_out_w"])
    x = run_stage4(x, inputs["L1_ffn_w_gate_up"], inputs["L1_ffn_w_down"])
    return x
